# version 02
# baseline (speedup 1.0000x reference)
import numpy as np
import ml_dtypes
from contextlib import ExitStack

import concourse.bass as bass
import concourse.mybir as mybir
from concourse.bass_utils import run_bass_kernel_spmd

F32 = mybir.dt.float32
BF16 = mybir.dt.bfloat16
I32 = mybir.dt.int32
AF = mybir.ActivationFunctionType
ALU = mybir.AluOpType
AX = mybir.AxisListType.X

ENGINES = ("sync", "vector", "scalar", "tensor", "gpsimd")
NEGBIG = -1.0e30
EPS = 1e-5
CSC = float(128.0 ** -0.5)
LN_CSC_MARGIN = float(np.log(1.02 * CSC))


class Buf:
    __slots__ = ("ready", "users", "name")

    def __init__(self, name=""):
        self.ready = None
        self.users = []
        self.name = name


SEMKEYS = list(ENGINES) + [e + "D" for e in ENGINES]


class Prog:
    def __init__(self, nc, ctx):
        self.nc = nc
        self.q = {e: [] for e in ENGINES}
        self.sem = {k: ctx.enter_context(nc.semaphore("S" + k)) for k in SEMKEYS}
        self.cnt = {k: 0 for k in SEMKEYS}
        self.clock = {e: {k: 0 for k in SEMKEYS} for e in ENGINES}
        self.regs = {}
        self.bufs = {}

    def B(self, name):
        if name not in self.bufs:
            self.bufs[name] = Buf(name)
        return self.bufs[name]

    def emit(self, eng, fn, reads=(), writes=(), waits=(), inc=1, semkey=None):
        key = semkey or eng
        need = {}

        def add(t):
            if t is None:
                return
            f, v = t
            if v > need.get(f, 0):
                need[f] = v

        for b in reads:
            add(b.ready)
        for b in writes:
            add(b.ready)
            for t in b.users:
                add(t)
        for t in waits:
            add(t)

        cl = self.clock[eng]
        real = tuple((f, v) for f, v in need.items() if v > cl[f])
        for f, v in real:
            cl[f] = v

        self.cnt[key] += inc
        ticket = (key, self.cnt[key])
        sems = self.sem
        my_sem = sems[key]

        def g(engine, _fn=fn, _real=real, _inc=inc, _sem=my_sem):
            for f, v in _real:
                engine.wait_ge(sems[f], v)
            ins = _fn(engine)
            if _inc:
                ins.then_inc(_sem, _inc)

        self.q[eng].append(g)
        if inc:
            for b in reads:
                b.users.append(ticket)
            for b in writes:
                b.ready = ticket
                b.users = []
        return ticket

    def dma(self, eng, out_ap, in_ap, reads=(), writes=(), waits=(), slow=False):
        def fn(engine, _o=out_ap, _i=in_ap, _s=slow):
            return engine.dma_start(out=_o, in_=_i, allow_slow_non_contiguous=_s)
        return self.emit(eng, fn, reads=reads, writes=writes, waits=waits,
                         inc=16, semkey=eng + "D")

    def tick(self, eng="tensor"):
        return [(eng, self.cnt[eng])]

    def fence(self):
        snap = [(k, self.cnt[k]) for k in SEMKEYS if self.cnt[k] > 0]
        for e in ENGINES:
            self.emit(e, lambda engine: engine.nop(), waits=snap, inc=1)

    def replay(self, block):
        for e in ENGINES:
            qe = self.q[e]
            if not qe:
                continue

            def body(engine, _qe=qe):
                for f in _qe:
                    f(engine)

            getattr(block, e)(body)


def build_program(npairs, T, dbg=()):
    TOK = T // 2
    NT = TOK // 512
    NQSB = T // 1024
    NKC = T // 128
    NH = TOK // 1024
    NCORE = 2 * npairs
    NTOK_ALL = npairs * T

    nc = bass.Bass("TRN2", target_bir_lowering=False)
    ctx = ExitStack()
    io = {}

    def din(name, shape, dt=F32):
        io[name] = nc.dram_tensor(name, list(shape), dt, kind="ExternalInput").ap()
        return io[name]

    def dout(name, shape, dt=F32):
        io[name] = nc.dram_tensor(name, list(shape), dt, kind="ExternalOutput").ap()
        return io[name]

    def dint(name, shape, dt=F32):
        io[name] = nc.dram_tensor(name, list(shape), dt, kind="Internal").ap()
        return io[name]

    x_in = din("x", (4096, TOK))
    y_out = dout("y", (4096, TOK))
    awq = din("awq", (4096, 4096))
    awk = din("awk", (4096, 1024))
    awv = din("awv", (4096, 1024))
    awo = din("awo", (4096, 4096))
    fwqt = din("fwqt", (4096, 64))
    fwrt = din("fwrt", (64, 48))
    akt = din("akt", (64, 1024))
    avt = din("avt", (4096, 1024))
    wvdt = din("wvdt", (4096, 256))
    wvut = din("wvut", (256, 4096))
    wu1t = din("wu1t", (4096, 1024))
    wu2t = din("wu2t", (1024, 1))
    bo2 = din("bo2", (128, 32))
    bu12 = din("bu12", (128, 8))
    scal = din("scal", (1, 4))
    logrel = din("logrel", (1, 1024))
    tabc = din("tabc", (128, TOK))
    tabs = din("tabs", (128, TOK))
    identf = din("identf", (128, 128))
    identb = din("identb", (128, 128), BF16)
    tri = din("tri", (128, 128), BF16)
    oneq = din("oneq", (128, 1024), BF16)
    onek = din("onek", (128, 256), BF16)
    pari = din("pari", (1, 1), I32)

    x1d = dint("x1d", (4096, TOK))
    x2d = dint("x2d", (4096, TOK))
    xbfd = dint("xbfd", (4096, TOK), BF16)
    auxd = dint("auxd", (4096, TOK), BF16)
    rowsd = dint("rowsd", (32, TOK))
    wqb = dint("wqb", (4096, 4096), BF16)
    wkb = dint("wkb", (4096, 1024), BF16)
    wvb = dint("wvb", (4096, 1024), BF16)
    wob = dint("wob", (4096, 4096), BF16)
    fwqtb = dint("fwqtb", (4096, 64), BF16)
    fwrtb = dint("fwrtb", (64, 48), BF16)
    aktb = dint("aktb", (64, 1024), BF16)
    avtb = dint("avtb", (4096, 1024), BF16)
    wvdtb = dint("wvdtb", (4096, 256), BF16)
    wvutb = dint("wvutb", (256, 4096), BF16)
    wu1tb = dint("wu1tb", (4096, 1024), BF16)
    wu2tb = dint("wu2tb", (1024, 1), BF16)
    apd = dint("apd", (1024, 256), BF16)
    colsd = dint("colsd", (4, 4096))
    qc_d = dint("qc_d", (4096, TOK), BF16)
    qg_d = dint("qg_d", (2 * 4096, TOK), BF16)
    kc_d = dint("kc_d", (1024, TOK), BF16)
    kg_d = dint("kg_d", (2 * 1024, TOK), BF16)
    vc_d = dint("vc_d", (TOK, 1024), BF16)
    vg_d = dint("vg_d", (2 * TOK, 1024), BF16)
    oc_d = dint("oc_d", (2048, T), BF16)
    og_d = dint("og_d", (4096, T), BF16)
    qm_d = dint("qm_d", (2048, T), BF16)
    km_d = dint("km_d", (512, T), BF16)
    vm_d = dint("vm_d", (T, 512), BF16)
    om_d = dint("om_d", (4096, TOK), BF16)
    st_c = dint("st_c", (1, 64))
    st_g = dint("st_g", (1, 64))
    lv_c = dint("lv_c", (1, 8))
    lv_g = dint("lv_g", (1, 8))
    lv_c2 = dint("lv_c2", (1, 8))
    lv_g2 = dint("lv_g2", (1, 8))

    for name in dbg:
        src = io[name]
        dout("dbg_" + name, list(src.tensor.shape), src.tensor.dtype)

    groups_pair = [[2 * i, 2 * i + 1] for i in range(npairs)]
    groups_all = [list(range(NCORE))]

    with ctx:
        p = Prog(nc, ctx)
        B = p.B
        e = ctx.enter_context

        CONST = e(nc.sbuf_tensor("CONST", [128, 256], F32))
        IDF = CONST[:, 0:128]
        ONESF = CONST[0:1, 128:256]
        CONSTB = e(nc.sbuf_tensor("CONSTB", [128, 384], BF16))
        IDB = CONSTB[:, 0:128]
        TRIB = CONSTB[:, 128:256]
        ONESB = CONSTB[:, 256:257]
        BO2 = e(nc.sbuf_tensor("BO2", [128, 48], F32))
        SCAL = e(nc.sbuf_tensor("SCAL", [1, 16], F32))
        LR = e(nc.sbuf_tensor("LRR", [1, 1024], F32))
        PTI = e(nc.sbuf_tensor("PTI", [1, 8], I32))
        BIAS = e(nc.sbuf_tensor("BIAS", [128, 8], F32))
        ROWA = e(nc.sbuf_tensor("ROWA", [1, 4096], F32))
        ROWB = e(nc.sbuf_tensor("ROWB", [1, 4096], F32))
        ROWD = e(nc.sbuf_tensor("ROWD", [1, 128], F32))
        RKBF = e(nc.sbuf_tensor("RKBF", [48, 1024], BF16))
        FWQ = e(nc.sbuf_tensor("FWQ", [128, 2048], BF16))
        FWR = e(nc.sbuf_tensor("FWR", [64, 48], BF16))
        AKT = e(nc.sbuf_tensor("AKT", [64, 1024], BF16))
        APR = e(nc.sbuf_tensor("APR", [128, 2048], BF16))
        WVU = e(nc.sbuf_tensor("WVU", [128, 8192], BF16))
        W2S = e(nc.sbuf_tensor("W2S", [128, 8], BF16))
        CSF = e(nc.sbuf_tensor("CSF", [1, 1152], F32))
        DCA = e(nc.sbuf_tensor("DCA", [128, 512], F32))
        DCB = e(nc.sbuf_tensor("DCB", [128, 512], BF16))
        DCA2 = e(nc.sbuf_tensor("DCA2", [128, 512], F32))
        DCB2 = e(nc.sbuf_tensor("DCB2", [128, 512], BF16))

        p.dma("sync", CONST[:, 0:128], identf, writes=[B("IDF")])
        p.dma("sync", CONSTB[:, 0:128], identb, writes=[B("IDB")])
        p.dma("sync", CONSTB[:, 128:256], tri, writes=[B("TRI")])
        p.dma("sync", BO2[:, 0:32], bo2, writes=[B("BO2")])
        p.dma("sync", BO2[:, 32:40], bu12, writes=[B("BU1")])
        p.dma("sync", SCAL[0:1, 0:4], scal, writes=[B("SCAL")])
        p.dma("sync", LR[0:1, :], logrel, writes=[B("LRR")])
        p.dma("sync", PTI[0:1, 0:1], pari, writes=[B("PTI")])
        p.emit("vector", lambda v: v.memset(CONST[0:1, 128:256], 1.0),
               writes=[B("ONESF")])
        p.emit("vector", lambda v: v.memset(CONSTB[:, 256:257], 1.0),
               writes=[B("ONESB")])
        CONSTV = e(nc.sbuf_tensor("CONSTV", [128, 4], F32))
        for ci, cv in ((0, EPS), (1, 1.0), (2, LN_CSC_MARGIN)):
            p.emit("vector", lambda v, _i=ci, _v=cv: v.memset(
                CONSTV[:, _i:_i + 1], _v), writes=[B("constv")])
            nc.const_aps.register(F32, cv, CONSTV[:, ci:ci + 1])

        def mkregs(engine):
            rp = engine.alloc_register("rp")
            engine.reg_load(rp, PTI[0:1, 0:1])
            r1 = engine.alloc_register("rqr")
            engine.reg_mul(r1, rp, 2048 * TOK)
            r2 = engine.alloc_register("rkr")
            engine.reg_mul(r2, rp, 512 * TOK)
            r3 = engine.alloc_register("rvr")
            engine.reg_mul(r3, rp, 512)
            r4 = engine.alloc_register("ror")
            engine.reg_mul(r4, rp, TOK)
            rt = engine.alloc_register("rtt")
            p.regs.update(rq=r1, rk=r2, rv=r3, ro=r4, rt=rt)
            return engine.nop()

        p.emit("sync", mkregs, reads=[B("PTI")])

        with ExitStack() as s0:
            e0 = s0.enter_context
            P0S = e0(nc.psum_tensor("P0S", [128, 512], F32))
            CW = e0(nc.sbuf_tensor("CW", [128, 1024], F32))
            CW2 = e0(nc.sbuf_tensor("CW2", [128, 1024], F32))
            CWB = e0(nc.sbuf_tensor("CWB", [128, 1024], BF16))
            CWB2 = e0(nc.sbuf_tensor("CWB2", [128, 1024], BF16))
            WA0 = e0(nc.sbuf_tensor("WA0", [128, 4096], BF16))
            WD0 = e0(nc.sbuf_tensor("WD0", [128, 8192], BF16))
            EV0 = e0(nc.sbuf_tensor("EV0", [128, 256], BF16))

            def cast_weight(src, dst, rows, cols, colsum=None):
                nrc = (rows + 127) // 128
                for c0 in range(0, cols, 512):
                    cc = min(512, cols - c0)
                    for ri in range(nrc):
                        r0 = ri * 128
                        rr = min(128, rows - r0)
                        stage, stageb, sb = (
                            (CW, CWB, "cw0") if ri % 2 == 0 else (CW2, CWB2, "cw1"))
                        p.dma("sync", stage[0:rr, 0:cc],
                              src[r0:r0 + rr, c0:c0 + cc], writes=[B(sb)])
                        p.emit("vector",
                               lambda v, _s=stage, _b=stageb, _rr=rr, _cc=cc:
                               v.tensor_copy(_b[0:_rr, 0:_cc], _s[0:_rr, 0:_cc]),
                               reads=[B(sb)], writes=[B(sb + "b")])
                        p.dma("sync", dst[r0:r0 + rr, c0:c0 + cc],
                              stageb[0:rr, 0:cc], reads=[B(sb + "b")],
                              writes=[B(dst.tensor.name)])
                        if colsum is not None:
                            p.emit("tensor",
                                   lambda t, _b=stageb, _rr=rr, _cc=cc,
                                   _st=(ri == 0), _sp=(ri == nrc - 1):
                                   t.matmul(P0S[0:1, 0:_cc], ONESB[0:_rr, :],
                                            _b[0:_rr, 0:_cc], start=_st, stop=_sp),
                                   reads=[B(sb + "b"), B("ONESB")],
                                   writes=([B("P0S")] if ri == 0 else []))
                    if colsum is not None:
                        kindcs, off = colsum
                        p.emit("scalar",
                               lambda s, _cc=cc: s.copy(ROWA[0:1, 0:_cc],
                                                        P0S[0:1, 0:_cc]),
                               reads=[B("P0S")], writes=[B("csrow")],
                               waits=p.tick())
                        if kindcs == "sbuf":
                            p.emit("vector",
                                   lambda v, _c0=c0, _cc=cc, _o=off:
                                   v.tensor_copy(CSF[0:1, _o + _c0:_o + _c0 + _cc],
                                                 ROWA[0:1, 0:_cc]),
                                   reads=[B("csrow")], writes=[B("csf")])
                        else:
                            p.dma("sync", colsd[off:off + 1, c0:c0 + cc],
                                  ROWA[0:1, 0:cc], reads=[B("csrow")],
                                  writes=[B("colsd")])

            cast_weight(fwqt, fwqtb, 4096, 64, colsum=("sbuf", 0))
            cast_weight(fwrt, fwrtb, 64, 48)
            cast_weight(akt, aktb, 64, 1024)
            cast_weight(wvdt, wvdtb, 4096, 256)
            cast_weight(avt, avtb, 4096, 1024)
            cast_weight(wvut, wvutb, 256, 4096)
            cast_weight(wu1t, wu1tb, 4096, 1024, colsum=("sbuf", 64))
            cast_weight(wu2t, wu2tb, 1024, 1)

            p.dma("sync",
                  bass.AP(WD0, 0,
                          [[8192, 128], [256, 32], [1, 256]]),
                  bass.AP(wvdtb.tensor, 0, [[256, 128], [128 * 256, 32], [1, 256]]),
                  reads=[B("wvdtb")], writes=[B("wd0")])
            for nch in range(8):
                p.dma("sync",
                      bass.AP(WA0, 0,
                              [[4096, 128], [128, 32], [1, 128]]),
                      bass.AP(avtb.tensor, nch * 128,
                              [[1024, 128], [128 * 1024, 32], [1, 128]]),
                      reads=[B("avtb")], writes=[B("wa0")])

                def mmA(t):
                    ins = None
                    for kc in range(32):
                        ins = t.matmul(P0S[:, 0:256],
                                       WA0[:, kc * 128:(kc + 1) * 128],
                                       WD0[:, kc * 256:(kc + 1) * 256],
                                       start=(kc == 0), stop=(kc == 31))
                    return ins

                p.emit("tensor", mmA, reads=[B("wa0"), B("wd0")], writes=[B("P0S")])
                p.emit("scalar", lambda s: s.copy(EV0[:, 0:256], P0S[:, 0:256]),
                       reads=[B("P0S")], writes=[B("ev0")], waits=p.tick())
                p.dma("sync", apd[nch * 128:(nch + 1) * 128, 0:256], EV0[:, 0:256],
                      reads=[B("ev0")], writes=[B("apd")])

            p.dma("sync", FWR[:, :], fwrtb, reads=[B("fwrtb")], writes=[B("FWR")])
            p.dma("sync", AKT[:, :], aktb, reads=[B("aktb")], writes=[B("AKT")])
            for c2 in range(2):
                p.emit("tensor", lambda t, _c=c2: t.matmul(
                    P0S[0:48, :], FWR[:, :], AKT[:, _c * 512:(_c + 1) * 512],
                    start=True, stop=True),
                    reads=[B("FWR"), B("AKT")], writes=[B("P0S")])
                p.emit("vector", lambda v, _c=c2: v.tensor_copy(
                    RKBF[:, _c * 512:(_c + 1) * 512], P0S[0:48, :]),
                    reads=[B("P0S")], writes=[B("RKBF")], waits=p.tick())

            p.dma("sync",
                  bass.AP(FWQ, 0,
                          [[2048, 128], [64, 32], [1, 64]]),
                  bass.AP(fwqtb.tensor, 0, [[64, 128], [128 * 64, 32], [1, 64]]),
                  reads=[B("fwqtb")], writes=[B("FWQ")])
            p.dma("sync",
                  bass.AP(APR, 0,
                          [[2048, 128], [256, 8], [1, 256]]),
                  bass.AP(apd.tensor, 0, [[256, 128], [128 * 256, 8], [1, 256]]),
                  reads=[B("apd")], writes=[B("APR")])
            p.dma("sync",
                  bass.AP(WVU, 0,
                          [[8192, 128], [4096, 2], [1, 4096]]),
                  bass.AP(wvutb.tensor, 0,
                          [[4096, 128], [128 * 4096, 2], [1, 4096]]),
                  reads=[B("wvutb")], writes=[B("WVU")])
            p.dma("sync", W2S[:, 0:8],
                  bass.AP(wu2tb.tensor, 0, [[1, 128], [128, 8]]),
                  reads=[B("wu2tb")], writes=[B("W2S")], slow=True)

        p.fence()

        deferred_units = []
        for srcw, dstw, rows, cols in ((awq, wqb, 4096, 4096),
                                       (awk, wkb, 4096, 1024),
                                       (awv, wvb, 4096, 1024),
                                       (awo, wob, 4096, 4096)):
            for c0 in range(0, cols, 512):
                for r0 in range(0, rows, 128):
                    deferred_units.append((srcw, dstw, r0, c0))

        def emit_cast_unit(i, unit):
            srcw, dstw, r0, c0 = unit
            stage, stageb, sb = ((DCA, DCB, "dc0") if i % 2 == 0
                                 else (DCA2, DCB2, "dc1"))
            p.dma("scalar", stage[:, :], srcw[r0:r0 + 128, c0:c0 + 512],
                  writes=[B(sb)])
            p.emit("gpsimd", lambda g, _s=stage, _b=stageb: g.tensor_copy(
                _b[:, :], _s[:, :]), reads=[B(sb)], writes=[B(sb + "b")])
            p.dma("scalar", dstw[r0:r0 + 128, c0:c0 + 512], stageb[:, :],
                  reads=[B(sb + "b")], writes=[B(dstw.tensor.name)])

        dc_state = [0]

        def cast_filler(n):
            i0 = dc_state[0]
            for j in range(i0, min(i0 + n, len(deferred_units))):
                emit_cast_unit(j, deferred_units[j])
            dc_state[0] = min(i0 + n, len(deferred_units))

        def cast_drain():
            cast_filler(len(deferred_units))

        LVACC = ROWD[0:1, 0:1]
        LVTMP = ROWD[0:1, 1:2]
        LVST = ROWD[0:1, 8:16]
        MIV = ROWD[0:1, 16:17]
        LVACC2 = ROWD[0:1, 24:25]
        LVTMP2 = ROWD[0:1, 25:26]
        LVST2 = ROWD[0:1, 32:40]
        BQ = ROWD[0:1, 40:41]
        BK = ROWD[0:1, 41:42]
        BB_ = ROWD[0:1, 42:43]

        def emit_ffn(xsrc, dst, rowbank, lvg, lvc, prepass, tag, filler=None):
            xsrc_b = B(xsrc.tensor.name)

            def fill(n):
                if filler is not None:
                    filler(n)

            with ExitStack() as fs:
                fe = fs.enter_context
                FA = fe(nc.psum_tensor(tag + "FA", [128, 512], F32))
                FB = fe(nc.psum_tensor(tag + "FB", [128, 512], F32))
                FC = fe(nc.psum_tensor(tag + "FC", [128, 1024], F32))
                FD = fe(nc.psum_tensor(tag + "FD", [128, 1024], F32))
                FE = fe(nc.psum_tensor(tag + "FE", [128, 512], F32))
                FT = fe(nc.psum_tensor(tag + "FT", [128, 1024], BF16))
                XT4 = fe(nc.sbuf_tensor(tag + "XT4", [128, 2048], F32))
                XT4B = fe(nc.sbuf_tensor(tag + "XT4B", [128, 2048], F32))
                XB = fe(nc.sbuf_tensor(tag + "XB", [128, 16384], BF16))
                SQ4 = fe(nc.sbuf_tensor(tag + "SQ4", [128, 2048], BF16))
                W1 = fe(nc.sbuf_tensor(tag + "W1", [128, 8192], BF16))
                PTS = fe(nc.sbuf_tensor(tag + "PTS", [128, 4096], BF16))
                PB = fe(nc.sbuf_tensor(tag + "PB", [128, 1024], BF16))
                QB = fe(nc.sbuf_tensor(tag + "QB", [64, 512], BF16))
                RQ = fe(nc.sbuf_tensor(tag + "RQ", [48, 512], BF16))
                RTS = fe(nc.sbuf_tensor(tag + "RTS", [128, 1024], F32))
                FSS = fe(nc.sbuf_tensor(tag + "FSS", [128, 1024], F32))
                SELS = fe(nc.sbuf_tensor(tag + "SELS", [128, 1024], F32))
                MSK8 = fe(nc.sbuf_tensor(tag + "MSK8", [128, 1024],
                                         mybir.dt.uint8))
                NEGB = fe(nc.sbuf_tensor(tag + "NEGB", [128, 2], F32))
                T8 = fe(nc.sbuf_tensor(tag + "T8", [128, 8], F32))
                NB2 = fe(nc.sbuf_tensor(tag + "NB2", [128, 2], F32))
                SRC_ = fe(nc.sbuf_tensor(tag + "SRC", [128, 2], F32))
                RB512 = fe(nc.sbuf_tensor(tag + "RB512", [128, 512], F32))
                GAB = fe(nc.sbuf_tensor(tag + "GAB", [128, 1536], F32))
                ASB = fe(nc.sbuf_tensor(tag + "ASB", [128, 1024], BF16))
                AXE = fe(nc.sbuf_tensor(tag + "AXE", [128, 512], BF16))
                UF = fe(nc.sbuf_tensor(tag + "UF", [128, 512], F32))
                UB = fe(nc.sbuf_tensor(tag + "UB", [128, 512], BF16))
                AX4 = fe(nc.sbuf_tensor(tag + "AX4", [128, 2048], BF16))
                CF1 = fe(nc.sbuf_tensor(tag + "CF1", [128, 2048], F32))
                CF2 = fe(nc.sbuf_tensor(tag + "CF2", [128, 2048], F32))

                p.emit("vector", lambda v: v.memset(NEGB[:, :], NEGBIG),
                       writes=[B(tag + "negb")])
                NEGBB = NEGB[:, 0:1].to_broadcast([128, 1024])

                def oct_ap(dram, tile, o):
                    return bass.AP(dram.tensor, (o * 4 * 128) * TOK + tile * 512,
                                   [[TOK, 128], [128 * TOK, 4], [1, 512]])

                def xbfd_tile_ap(tile):
                    return bass.AP(xbfd.tensor, tile * 512,
                                   [[TOK, 128], [128 * TOK, 32], [1, 512]])

                s_negmu = ROWA[0:1, 0:512]
                s_r = ROWA[0:1, 512:1024]
                s_t1 = ROWA[0:1, 1024:1536]
                s_t2 = ROWA[0:1, 1536:2048]
                s_v = ROWA[0:1, 2048:2560]
                s_sr = ROWA[0:1, 2560:3072]
                s_le = ROWA[0:1, 3072:3584]
                s_g = ROWA[0:1, 3584:4096]
                s_a = ROWB[0:1, 0:512]
                s_b2 = ROWB[0:1, 512:1024]
                s_lv = ROWB[0:1, 1024:1536]
                s_g2 = ROWB[0:1, 1536:2048]

                def ln_rows(tile, store_bank, lvacc, lvtmp):
                    tk = p.tick()
                    p.emit("vector", lambda v: v.tensor_scalar(
                        s_negmu, FA[0:1, :], -1.0 / 4096.0, None, ALU.mult),
                        reads=[B(tag + "FA")], writes=[B(tag + "negmu")], waits=tk)
                    p.emit("vector", lambda v: v.tensor_scalar(
                        s_t1, FB[0:1, :], 1.0 / 4096.0, None, ALU.mult),
                        reads=[B(tag + "FB")], writes=[B(tag + "t1")], waits=tk)
                    p.emit("vector", lambda v: v.tensor_mul(
                        s_t2, s_negmu, s_negmu),
                        reads=[B(tag + "negmu")], writes=[B(tag + "t2")])
                    p.emit("vector", lambda v: v.tensor_sub(s_v, s_t1, s_t2),
                           reads=[B(tag + "t1"), B(tag + "t2")],
                           writes=[B(tag + "v")])
                    p.emit("scalar", lambda s: s.activation(
                        s_t1, s_v, AF.Ln, bias=EPS),
                        reads=[B(tag + "v")], writes=[B(tag + "t1")])
                    p.emit("scalar", lambda s: s.activation(
                        s_r, s_t1, AF.Exp, scale=-0.5),
                        reads=[B(tag + "t1")], writes=[B(tag + "r")])
                    tsl = slice(tile * 512, (tile + 1) * 512)
                    p.dma("sync", rowsd[store_bank:store_bank + 1, tsl], s_negmu,
                          reads=[B(tag + "negmu")],
                          writes=[B("rw%d" % store_bank)])
                    p.dma("sync", rowsd[store_bank + 1:store_bank + 2, tsl], s_r,
                          reads=[B(tag + "r")],
                          writes=[B("rw%d" % (store_bank + 1))])
                    if lvacc is not None:
                        p.emit("vector", lambda v: v.tensor_mul(s_t2, s_r, s_r),
                               reads=[B(tag + "r")], writes=[B(tag + "t2")])
                        p.emit("vector", lambda v: v.tensor_mul(s_t1, s_v, s_t2),
                               reads=[B(tag + "v"), B(tag + "t2")],
                               writes=[B(tag + "t1")])
                        p.emit("scalar", lambda s: s.activation(
                            s_t2, s_t1, AF.Ln, bias=1.0),
                            reads=[B(tag + "t1")], writes=[B(tag + "t2")])
                        p.dma("sync", rowsd[store_bank + 7:store_bank + 8, tsl],
                              s_t2, reads=[B(tag + "t2")],
                              writes=[B("rw%d" % (store_bank + 7))])
                        p.emit("vector", lambda v: v.reduce_sum(
                            lvtmp, s_t2, axis=AX),
                            reads=[B(tag + "t2")], writes=[B(tag + "lvt")])
                        p.emit("vector", lambda v: v.tensor_add(
                            lvacc, lvacc, lvtmp),
                            reads=[B(tag + "lvt"), B(tag + "lvacc")],
                            writes=[B(tag + "lvacc")])

                def stats_chain(first, src_sb, sq_sb, src_buf, nchunks=4):
                    def mm(t, _f=first, _src=src_sb, _sq=sq_sb, _n=nchunks):
                        ins = None
                        for c8 in range(_n):
                            st = (_f and c8 == 0)
                            sl = slice(c8 * 512, (c8 + 1) * 512)
                            ins = t.matmul(FA[0:1, :], ONESB[:, :], _src[:, sl],
                                           start=st, stop=False)
                            ins = t.matmul(FB[0:1, :], ONESB[:, :], _sq[:, sl],
                                           start=st, stop=False)
                        return ins
                    p.emit("tensor", mm,
                           reads=[src_buf, B(tag + "sq"), B("ONESB")],
                           writes=([B(tag + "FA"), B(tag + "FB")]
                                   if first else []))

                if prepass:
                    p.emit("vector", lambda v: v.memset(LVACC, 0.0),
                           writes=[B(tag + "lvacc")])
                    for tile in range(NT):
                        for o in range(8):
                            xt = XT4 if o % 2 == 0 else XT4B
                            xtb = B(tag + "xt4%d" % (o % 2))
                            xbsl = XB[:, o * 2048:(o + 1) * 2048]
                            p.dma("sync", xt[:, :], oct_ap(xsrc, tile, o),
                                  reads=[xsrc_b], writes=[xtb])
                            p.emit("vector", lambda v, _x=xbsl, _xt=xt:
                                   v.tensor_copy(_x, _xt[:, :]),
                                   reads=[xtb],
                                   writes=[B(tag + "xb")])
                            p.dma("sync", oct_ap(xbfd, tile, o), xbsl,
                                  reads=[B(tag + "xb")], writes=[B("xbfd")])
                            p.emit("gpsimd", lambda g, _x=xbsl: g.tensor_mul(
                                SQ4[:, :], _x, _x),
                                reads=[B(tag + "xb")], writes=[B(tag + "sq")])
                            stats_chain(o == 0, xbsl, SQ4, B(tag + "xb"))
                        ln_rows(tile, rowbank, LVACC, LVTMP)
                    p.emit("vector", lambda v: v.memset(LVST, 0.0),
                           writes=[B(tag + "lvst")])
                    p.emit("vector", lambda v: v.tensor_copy(
                        ROWD[0:1, 8:9], LVACC),
                        reads=[B(tag + "lvacc")], writes=[B(tag + "lvst")])
                    p.dma("sync", lvc, LVST, reads=[B(tag + "lvst")],
                          writes=[B("lvc" + tag)])

                    def coll_lv(g):
                        return g.collective_compute(
                            "AllReduce", ALU.add, replica_groups=groups_all,
                            ins=[lvc.opt()], outs=[lvg.opt()])
                    p.emit("gpsimd", coll_lv, reads=[B("lvc" + tag)],
                           writes=[B("lvg" + tag)], semkey="gpsimdD")

                donext = dst is not y_out
                for tile in range(NT):
                    tsl = slice(tile * 512, (tile + 1) * 512)
                    p.dma("sync", bass.AP(XB, 0,
                                          [[16384, 128], [512, 32], [1, 512]]),
                          xbfd_tile_ap(tile), reads=[B("xbfd")],
                          writes=[B(tag + "xbt")])
                    p.dma("sync", s_negmu, rowsd[rowbank:rowbank + 1, tsl],
                          reads=[B("rw%d" % rowbank)], writes=[B(tag + "negmu")])
                    p.dma("sync", s_r, rowsd[rowbank + 1:rowbank + 2, tsl],
                          reads=[B("rw%d" % (rowbank + 1))],
                          writes=[B(tag + "r")])
                    p.emit("tensor", lambda t: t.matmul(
                        FE[:, :], ONESF[:, :], s_r, start=True, stop=True),
                        reads=[B(tag + "r"), B("ONESF")], writes=[B(tag + "FE")])
                    p.emit("scalar", lambda s: s.copy(RB512[:, :], FE[:, :]),
                           reads=[B(tag + "FE")], writes=[B(tag + "rb")],
                           waits=p.tick())

                    def mmq(t):
                        ins = None
                        for kc in range(32):
                            ins = t.matmul(FA[0:64, :],
                                           FWQ[:, kc * 64:(kc + 1) * 64],
                                           XB[:, kc * 512:(kc + 1) * 512],
                                           start=(kc == 0), stop=False)
                        ins = t.matmul(FA[0:64, :], CSF[0:1, 0:64], s_negmu,
                                       start=False, stop=True)
                        return ins
                    p.emit("tensor", mmq,
                           reads=[B(tag + "xbt"), B("FWQ"), B("csf"),
                                  B(tag + "negmu")],
                           writes=[B(tag + "FA")])
                    p.emit("vector", lambda v: v.tensor_mul(
                        QB[:, :], FA[0:64, :], RB512[0:64, :]),
                        reads=[B(tag + "FA"), B(tag + "rb")],
                        writes=[B(tag + "qb")], waits=p.tick())

                    p.emit("tensor", lambda t: t.matmul(
                        FB[0:48, :], FWR[:, :], QB[:, :], start=True, stop=True),
                        reads=[B(tag + "qb"), B("FWR")], writes=[B(tag + "FB")])
                    p.emit("scalar", lambda s: s.copy(RQ[:, :], FB[0:48, :]),
                           reads=[B(tag + "FB")], writes=[B(tag + "rq")],
                           waits=p.tick())
                    fill(24)

                    for sc in range(4):
                        csl = slice(sc * 128, (sc + 1) * 128)

                        def mmrout(t, _c=csl):
                            i = t.matmul(FC[:, 0:512], RQ[:, _c], RKBF[:, 0:512],
                                         start=True, stop=False)
                            i = t.matmul(FC[:, 0:512], ONESF[:, :],
                                         LR[0:1, 0:512], start=False, stop=True)
                            i = t.matmul(FC[:, 512:1024], RQ[:, _c],
                                         RKBF[:, 512:1024], start=True,
                                         stop=False)
                            i = t.matmul(FC[:, 512:1024], ONESF[:, :],
                                         LR[0:1, 512:1024], start=False,
                                         stop=True)
                            return i
                        p.emit("tensor", mmrout,
                               reads=[B(tag + "rq"), B("RKBF"), B("LRR"),
                                      B("ONESF")],
                               writes=[B(tag + "FC")])

                        def mmf(t, _c=csl):
                            i = t.matmul(FD[:, 0:512], QB[:, _c], AKT[:, 0:512],
                                         start=True, stop=False)
                            i = t.matmul(FD[:, 0:512], ONESF[:, :],
                                         LR[0:1, 0:512], start=False, stop=True)
                            i = t.matmul(FD[:, 512:1024], QB[:, _c],
                                         AKT[:, 512:1024], start=True,
                                         stop=False)
                            i = t.matmul(FD[:, 512:1024], ONESF[:, :],
                                         LR[0:1, 512:1024], start=False,
                                         stop=True)
                            return i
                        p.emit("tensor", mmf,
                               reads=[B(tag + "qb"), B("AKT"), B("LRR"),
                                      B("ONESF")],
                               writes=[B(tag + "FD")])

                        p.emit("scalar", lambda s: s.copy(RTS[:, :], FC[:, :]),
                               reads=[B(tag + "FC")], writes=[B(tag + "rts")],
                               waits=p.tick())
                        p.emit("scalar", lambda s: s.copy(FSS[:, :], FD[:, :]),
                               reads=[B(tag + "FD")], writes=[B(tag + "fss")],
                               waits=p.tick())
                        p.emit("vector", lambda v: v.max(out=T8[:, :],
                                                         in_=RTS[:, :]),
                               reads=[B(tag + "rts")], writes=[B(tag + "t8")])
                        p.emit("vector", lambda v: v.reduce_max(
                            NB2[:, 0:1], FSS[:, :], axis=AX, negate=True),
                            reads=[B(tag + "fss")], writes=[B(tag + "nb2")])
                        p.emit("vector", lambda v: v.tensor_scalar(
                            MSK8[:, :], RTS[:, :], T8[:, 7:8], None, ALU.is_ge),
                            reads=[B(tag + "t8"), B(tag + "rts")],
                            writes=[B(tag + "msk8")])
                        p.emit("vector", lambda v: v.select(
                            SELS[:, :], MSK8[:, :], FSS[:, :], NEGBB),
                            reads=[B(tag + "msk8"), B(tag + "fss"),
                                   B(tag + "negb")],
                            writes=[B(tag + "sels")])
                        p.emit("scalar", lambda s: s.activation(
                            PB[:, :], SELS[:, :], AF.Exp, bias=NB2[:, 0:1],
                            accum_out=SRC_[:, 0:1]),
                            reads=[B(tag + "sels"), B(tag + "nb2")],
                            writes=[B(tag + "pb"), B(tag + "srcol")])
                        p.emit("tensor", lambda t: t.transpose(
                            FC[0:1, 0:128], SRC_[:, 0:1], IDF),
                            reads=[B(tag + "srcol"), B("IDF")],
                            writes=[B(tag + "FC")])
                        p.emit("scalar", lambda s, _sc=sc: s.copy(
                            s_sr[0:1, _sc * 128:(_sc + 1) * 128],
                            FC[0:1, 0:128]),
                            reads=[B(tag + "FC")], writes=[B(tag + "srrow")],
                            waits=p.tick())

                        def mmtr(t):
                            ins = None
                            for j in range(8):
                                ins = t.transpose(FT[:, j * 128:(j + 1) * 128],
                                                  PB[:, j * 128:(j + 1) * 128],
                                                  IDB)
                            return ins
                        p.emit("tensor", mmtr, reads=[B(tag + "pb"), B("IDB")],
                               writes=[B(tag + "FT")])
                        p.emit("scalar", lambda s, _sc=sc: s.copy(
                            bass.AP(PTS, _sc * 128,
                                    [[4096, 128], [512, 8], [1, 128]]),
                            FT[:, :]),
                            reads=[B(tag + "FT")],
                            writes=[B(tag + "pts%d" % sc)],
                            waits=p.tick())

                    pts_reads = [B(tag + "pts%d" % s4) for s4 in range(4)]
                    for vc in range(2):
                        tgt, tgn = ((FA, tag + "FA") if vc == 0
                                    else (FB, tag + "FB"))

                        def mmas(t, _vc=vc, _tg=tgt):
                            ins = None
                            for n8 in range(8):
                                ins = t.matmul(
                                    _tg[:, :],
                                    APR[:, n8 * 256 + _vc * 128:
                                        n8 * 256 + _vc * 128 + 128],
                                    PTS[:, n8 * 512:(n8 + 1) * 512],
                                    start=(n8 == 0), stop=(n8 == 7))
                            return ins
                        p.emit("tensor", mmas, reads=pts_reads + [B("APR")],
                               writes=[B(tgn)])
                        p.emit("scalar", lambda s, _vc=vc, _tg=tgt: s.copy(
                            ASB[:, _vc * 512:(_vc + 1) * 512], _tg[:, :]),
                            reads=[B(tgn)], writes=[B(tag + "asb")],
                            waits=p.tick())

                    for cc in range(32):
                        tgt, tgn = ((FC, tag + "FC") if cc % 2 == 0
                                    else (FD, tag + "FD"))

                        def mmau(t, _cc=cc, _tg=tgt):
                            i = t.matmul(_tg[:, 0:512],
                                         WVU[:, _cc * 128:_cc * 128 + 128],
                                         ASB[:, 0:512], start=True, stop=False)
                            i = t.matmul(_tg[:, 0:512],
                                         WVU[:, 4096 + _cc * 128:
                                             4096 + _cc * 128 + 128],
                                         ASB[:, 512:1024], start=False,
                                         stop=True)
                            return i
                        p.emit("tensor", mmau, reads=[B(tag + "asb"), B("WVU")],
                               writes=[B(tgn)])
                        p.emit("scalar", lambda s, _tg=tgt: s.copy(
                            AXE[:, :], _tg[:, 0:512]),
                            reads=[B(tgn)], writes=[B(tag + "axe")],
                            waits=p.tick())
                        p.dma("sync",
                              auxd[cc * 128:(cc + 1) * 128, tsl],
                              AXE[:, :], reads=[B(tag + "axe")],
                              writes=[B("auxd")])
                    fill(24)

                    for uc in range(8):
                        ws = uc % 2
                        wb_ = B(tag + "w1%d" % ws)
                        p.dma("scalar",
                              bass.AP(W1, ws * 4096,
                                      [[8192, 128], [128, 32], [1, 128]]),
                              bass.AP(wu1tb.tensor, uc * 128,
                                      [[1024, 128], [128 * 1024, 32], [1, 128]]),
                              reads=[B("wu1tb")], writes=[wb_])
                        tgt, tgn = ((FA, tag + "FA") if uc % 2 == 0
                                    else (FB, tag + "FB"))

                        def mmu(t, _uc=uc, _tg=tgt, _ws=ws):
                            ins = None
                            for kc in range(32):
                                ins = t.matmul(
                                    _tg[:, :],
                                    W1[:, _ws * 4096 + kc * 128:
                                       _ws * 4096 + kc * 128 + 128],
                                    XB[:, kc * 512:(kc + 1) * 512],
                                    start=(kc == 0), stop=False)
                            ins = t.matmul(_tg[:, :],
                                           CSF[0:1, 64 + _uc * 128:
                                               64 + (_uc + 1) * 128],
                                           s_negmu, start=False, stop=True)
                            return ins
                        p.emit("tensor", mmu,
                               reads=[wb_, B(tag + "xbt"), B("csf"),
                                      B(tag + "negmu")],
                               writes=[B(tgn)])
                        p.emit("vector", lambda v, _tg=tgt: v.tensor_mul(
                            UF[:, :], _tg[:, :], RB512[:, :]),
                            reads=[B(tgn), B(tag + "rb")],
                            writes=[B(tag + "uf")], waits=p.tick())
                        p.emit("scalar", lambda s, _uc=uc: s.activation(
                            UB[:, :], UF[:, :], AF.Gelu_apprx_tanh,
                            bias=BO2[:, 32 + _uc:33 + _uc]),
                            reads=[B(tag + "uf"), B("BU1")],
                            writes=[B(tag + "ub")])
                        p.emit("tensor", lambda t, _uc=uc: t.matmul(
                            FE[0:1, :], W2S[:, _uc:_uc + 1], UB[:, :],
                            start=(_uc == 0), stop=(_uc == 7)),
                            reads=[B(tag + "ub"), B("W2S")],
                            writes=([B(tag + "FE")] if uc == 0 else []))
                    p.emit("vector", lambda v: v.tensor_scalar(
                        s_le, FE[0:1, :], SCAL[0:1, 0:1], None, ALU.add),
                        reads=[B(tag + "FE"), B("SCAL")],
                        writes=[B(tag + "le")], waits=p.tick())
                    fill(24)

                    if tile == 0:
                        p.dma("sync", LVST, lvg, reads=[B("lvg" + tag)],
                              writes=[B(tag + "lvst")])
                        p.emit("vector", lambda v: v.tensor_scalar(
                            MIV, ROWD[0:1, 8:9], 1.0 / NTOK_ALL, 1e-6,
                            ALU.mult, ALU.add),
                            reads=[B(tag + "lvst")], writes=[B(tag + "miv")])
                        p.emit("scalar", lambda s: s.activation(
                            MIV, MIV, AF.Ln),
                            reads=[B(tag + "miv")], writes=[B(tag + "miv")])
                        p.emit("scalar", lambda s: s.activation(
                            MIV, MIV, AF.Exp, scale=-1.0),
                            reads=[B(tag + "miv")], writes=[B(tag + "miv")])
                    p.emit("scalar", lambda s: s.activation(
                        s_g, s_le, AF.Tanh, scale=0.5),
                        reads=[B(tag + "le")], writes=[B(tag + "g")])
                    p.emit("vector", lambda v: v.tensor_scalar(
                        s_g, s_g, 0.5, 0.5, ALU.mult, ALU.add),
                        reads=[B(tag + "g")], writes=[B(tag + "g")])
                    p.dma("sync", s_lv, rowsd[rowbank + 7:rowbank + 8, tsl],
                          reads=[B("rw%d" % (rowbank + 7))],
                          writes=[B(tag + "lvr")])
                    p.emit("vector", lambda v: v.tensor_scalar(
                        s_lv, s_lv, MIV[0:1, 0:1], 0.5, ALU.mult, ALU.mult),
                        reads=[B(tag + "lvr"), B(tag + "miv")],
                        writes=[B(tag + "lvr")])
                    p.emit("vector", lambda v: v.scalar_tensor_tensor(
                        s_g, s_g, 2.5, s_lv, ALU.mult, ALU.add),
                        reads=[B(tag + "g"), B(tag + "lvr")],
                        writes=[B(tag + "g")])
                    p.emit("vector", lambda v: v.tensor_scalar(
                        s_g, s_g, 0.0, 5.0, ALU.max, ALU.min),
                        reads=[B(tag + "g")], writes=[B(tag + "g")])
                    p.emit("vector", lambda v: v.tensor_scalar(
                        s_g, s_g, -0.5, 1.0 / 1.5, ALU.add, ALU.mult),
                        reads=[B(tag + "g")], writes=[B(tag + "g")])
                    p.emit("vector", lambda v: v.tensor_scalar(
                        s_g, s_g, 0.0, 1.0, ALU.max, ALU.min),
                        reads=[B(tag + "g")], writes=[B(tag + "g")])
                    p.emit("vector", lambda v: v.tensor_scalar(
                        s_g, s_g, SCAL[0:1, 1:2], SCAL[0:1, 2:3],
                        ALU.mult, ALU.add),
                        reads=[B(tag + "g"), B("SCAL")], writes=[B(tag + "g")])
                    p.emit("scalar", lambda s: s.activation(
                        s_g, s_g, AF.Tanh, scale=0.5),
                        reads=[B(tag + "g")], writes=[B(tag + "g")])
                    p.emit("vector", lambda v: v.tensor_scalar(
                        s_g, s_g, 0.5, 0.5, ALU.mult, ALU.add),
                        reads=[B(tag + "g")], writes=[B(tag + "g")])
                    p.emit("vector", lambda v: v.tensor_scalar(
                        s_g2, s_g, 0.05, None, ALU.is_ge),
                        reads=[B(tag + "g")], writes=[B(tag + "g2")])
                    p.emit("vector", lambda v: v.tensor_mul(s_g, s_g, s_g2),
                           reads=[B(tag + "g2")], writes=[B(tag + "g")])
                    p.emit("scalar", lambda s: s.activation(s_g2, s_sr, AF.Ln),
                           reads=[B(tag + "srrow")], writes=[B(tag + "g2")])
                    p.emit("scalar", lambda s: s.activation(
                        s_g2, s_g2, AF.Exp, scale=-1.0),
                        reads=[B(tag + "g2")], writes=[B(tag + "g2")])
                    p.emit("vector", lambda v: v.scalar_tensor_tensor(
                        s_g, s_g, 0.5, s_g2, ALU.mult, ALU.mult),
                        reads=[B(tag + "g2"), B(tag + "g")],
                        writes=[B(tag + "g")])
                    p.emit("vector", lambda v: v.scalar_tensor_tensor(
                        s_b2, s_negmu, 0.5, s_r, ALU.mult, ALU.mult),
                        reads=[B(tag + "negmu"), B(tag + "r")],
                        writes=[B(tag + "b2")])
                    p.emit("vector", lambda v: v.tensor_scalar(
                        s_a, s_r, 0.5, 1.0, ALU.mult, ALU.add),
                        reads=[B(tag + "r")], writes=[B(tag + "a")])

                    for i, (nm, row) in enumerate((("gb", s_g), ("ab", s_a),
                                                   ("bb", s_b2))):
                        rb_ = B(tag + ("g" if i == 0 else ("a" if i == 1
                                                           else "b2")))
                        p.emit("tensor", lambda t, _r=row: t.matmul(
                            FE[:, :], ONESF[:, :], _r, start=True, stop=True),
                            reads=[rb_, B("ONESF")], writes=[B(tag + "FE")])
                        p.emit("scalar", lambda s, _i=i: s.copy(
                            GAB[:, _i * 512:(_i + 1) * 512], FE[:, :]),
                            reads=[B(tag + "FE")], writes=[B(tag + nm)],
                            waits=p.tick())

                    def b3(i):
                        return bass.AP(GAB, i * 512,
                                       [[1536, 128], [0, 4], [1, 512]])

                    for o in range(8):
                        xt = XT4 if o % 2 == 0 else XT4B
                        xtb = B(tag + "xt4%d" % (o % 2))
                        p.dma("sync", xt[:, :], oct_ap(xsrc, tile, o),
                              reads=[xsrc_b], writes=[xtb])
                        p.dma("sync", AX4[:, :], oct_ap(auxd, tile, o),
                              reads=[B("auxd")], writes=[B(tag + "ax4")])
                        p.emit("vector", lambda v: v.tensor_tensor(
                            CF1[:, :], AX4[:, :], b3(0), ALU.mult),
                            reads=[B(tag + "ax4"), B(tag + "gb")],
                            writes=[B(tag + "cf1")])
                        p.emit("gpsimd", lambda g, _xt=xt: g.tensor_tensor(
                            CF2[:, :], _xt[:, :], b3(1), ALU.mult),
                            reads=[xtb, B(tag + "ab")],
                            writes=[B(tag + "cf2")])
                        p.emit("vector", lambda v: v.tensor_add(
                            CF1[:, :], CF1[:, :], CF2[:, :]),
                            reads=[B(tag + "cf2"), B(tag + "cf1")],
                            writes=[B(tag + "cf1")])
                        p.emit("vector", lambda v: v.tensor_tensor(
                            CF2[:, :], CF1[:, :], b3(2), ALU.add),
                            reads=[B(tag + "cf1"), B(tag + "bb")],
                            writes=[B(tag + "cf2")])
                        p.dma("sync", oct_ap(dst, tile, o), CF2[:, :],
                              reads=[B(tag + "cf2")],
                              writes=[B(dst.tensor.name)])
                        if donext:
                            p.emit("gpsimd", lambda g: g.tensor_copy(
                                AX4[:, :], CF2[:, :]),
                                reads=[B(tag + "cf2")],
                                writes=[B(tag + "ax4")])
                            p.dma("sync", oct_ap(xbfd, tile, o), AX4[:, :],
                                  reads=[B(tag + "ax4")], writes=[B("xbfd")])
                            p.emit("gpsimd", lambda g: g.tensor_mul(
                                SQ4[:, :], AX4[:, :], AX4[:, :]),
                                reads=[B(tag + "ax4")], writes=[B(tag + "sq")])
                            stats_chain(o == 0, AX4, SQ4, B(tag + "ax4"))
                    if donext:
                        ln_rows(tile, 16, None, None)
                    fill(24)

        emit_ffn(x_in, x1d, rowbank=0, lvg=lv_g, lvc=lv_c, prepass=True,
                 tag="f1", filler=cast_filler)
        cast_drain()
        p.fence()

        with ExitStack() as s2:
            e2 = s2.enter_context
            QA = e2(nc.psum_tensor("QA", [128, 512], F32))
            QB_ = e2(nc.psum_tensor("QBp", [128, 512], F32))
            QE = e2(nc.psum_tensor("QE", [128, 512], F32))
            QS0 = e2(nc.psum_tensor("QS0", [128, 512], F32))
            QS1 = e2(nc.psum_tensor("QS1", [128, 512], F32))
            XBH = e2(nc.sbuf_tensor("XBH", [128, 32768], BF16))
            WQB = e2(nc.sbuf_tensor("WQBs", [128, 8192], BF16))
            WVH = e2(nc.sbuf_tensor("WVH", [128, 8192], BF16))
            QF = e2(nc.sbuf_tensor("QF", [128, 1024], F32))
            QSW = e2(nc.sbuf_tensor("QSW", [128, 1024], F32))
            QRO = e2(nc.sbuf_tensor("QRO", [128, 1024], BF16))
            QSQ = e2(nc.sbuf_tensor("QSQ", [128, 1024], BF16))
            TBC = e2(nc.sbuf_tensor("TBC", [128, 1024], F32))
            TBS = e2(nc.sbuf_tensor("TBS", [128, 1024], F32))
            R2B = e2(nc.sbuf_tensor("R2B", [128, 1024], F32))
            R2C = e2(nc.sbuf_tensor("R2C", [128, 8], F32))
            CSL = e2(nc.sbuf_tensor("CSL", [1, 128], F32))
            CSLV = e2(nc.sbuf_tensor("CSLV", [1, 256], F32))
            VBE = e2(nc.sbuf_tensor("VBE", [128, 256], BF16))
            QMX = e2(nc.sbuf_tensor("QMX", [128, 16], F32))
            STR = e2(nc.sbuf_tensor("STR", [1, 128], F32))
            ONEQK = e2(nc.sbuf_tensor("ONEQK", [128, 1280], BF16))
            ONEQ = ONEQK[:, 0:1024]
            ONEK = ONEQK[:, 1024:1280]
            p.dma("sync", ONEQK[:, 0:1024], oneq, writes=[B("ONEQ")])
            p.dma("sync", ONEQK[:, 1024:1280], onek, writes=[B("ONEK")])

            s2_negmu = ROWA[0:1, 0:1024]
            s2_r = ROWA[0:1, 1024:2048]

            p.emit("vector", lambda v: v.memset(QMX[:, 0:4], 0.0),
                   writes=[B("qmx")])

            for half in range(NH):
                hsl = slice(half * 1024, (half + 1) * 1024)
                p.dma("sync",
                      bass.AP(XBH, 0,
                              [[32768, 128], [1024, 32], [1, 1024]]),
                      bass.AP(xbfd.tensor, half * 1024,
                              [[TOK, 128], [128 * TOK, 32], [1, 1024]]),
                      reads=[B("xbfd")], writes=[B("xbh")])
                p.dma("sync", s2_negmu, rowsd[16:17, hsl],
                      reads=[B("rw16")], writes=[B("s2negmu")])
                p.dma("sync", s2_r, rowsd[17:18, hsl],
                      reads=[B("rw17")], writes=[B("s2r")])
                for c2 in range(2):
                    p.emit("tensor", lambda t, _c=c2: t.matmul(
                        QE[:, :], ONESF[:, :],
                        s2_r[0:1, _c * 512:(_c + 1) * 512],
                        start=True, stop=True),
                        reads=[B("s2r"), B("ONESF")], writes=[B("QE")])
                    p.emit("scalar", lambda s, _c=c2: s.copy(
                        R2B[:, _c * 512:(_c + 1) * 512], QE[:, :]),
                        reads=[B("QE")], writes=[B("r2b")], waits=p.tick())
                for tc in range(8):
                    p.emit("tensor", lambda t, _tc=tc: t.transpose(
                        QE[:, 0:1], s2_r[0:1, _tc * 128:(_tc + 1) * 128],
                        IDF[0:1, 0:1]),
                        reads=[B("s2r"), B("IDF")], writes=[B("QE")])
                    p.emit("scalar", lambda s, _tc=tc: s.copy(
                        R2C[:, _tc:_tc + 1], QE[:, 0:1]),
                        reads=[B("QE")], writes=[B("r2c")], waits=p.tick())
                p.dma("sync", TBC[:, :], tabc[:, hsl], writes=[B("tbc")])
                p.dma("sync", TBS[:, :], tabs[:, hsl], writes=[B("tbs")])
                p.emit("vector", lambda v: v.tensor_mul(
                    TBC[:, :], TBC[:, :], R2B[:, :]),
                    reads=[B("r2b"), B("tbc")], writes=[B("tbc")])
                p.emit("vector", lambda v: v.tensor_mul(
                    TBS[:, :], TBS[:, :], R2B[:, :]),
                    reads=[B("r2b"), B("tbs")], writes=[B("tbs")])

                for kind, nocs, wsrc, csrow, dst_c, onh, onb, qsb in (
                        ("q", 32, wqb, 0, qc_d, ONEQ, "ONEQ", QS0),
                        ("k", 8, wkb, 1, kc_d, ONEK, "ONEK", QS1)):
                    qsn = qsb.name
                    for oc in range(nocs):
                        ws = oc % 2
                        wb_ = B("wqb%d" % ws)
                        p.dma("scalar",
                              bass.AP(WQB, ws * 4096,
                                      [[8192, 128], [128, 32], [1, 128]]),
                              bass.AP(wsrc.tensor, oc * 128,
                                      [[wsrc.tensor.shape[1], 128],
                                       [128 * wsrc.tensor.shape[1], 32],
                                       [1, 128]]),
                              reads=[B(wsrc.tensor.name)], writes=[wb_])

                        def mmcs(t, _ws=ws):
                            ins = None
                            for kc in range(32):
                                ins = t.matmul(
                                    QE[0:1, 0:128], ONESB[:, :],
                                    WQB[:, _ws * 4096 + kc * 128:
                                        _ws * 4096 + kc * 128 + 128],
                                    start=(kc == 0), stop=(kc == 31))
                            return ins
                        p.emit("tensor", mmcs, reads=[wb_, B("ONESB")],
                               writes=[B("QE")])
                        p.emit("scalar", lambda s: s.copy(CSL[0:1, :],
                                                          QE[0:1, 0:128]),
                               reads=[B("QE")], writes=[B("csl")],
                               waits=p.tick())
                        for tcol in range(2):
                            tgt, tgn = (QA, "QA") if tcol == 0 else (QB_, "QBp")

                            def mmp(t, _tg=tgt, _ws=ws, _tc=tcol):
                                ins = None
                                for kc in range(32):
                                    ins = t.matmul(
                                        _tg[:, :],
                                        WQB[:, _ws * 4096 + kc * 128:
                                            _ws * 4096 + kc * 128 + 128],
                                        XBH[:, kc * 1024 + _tc * 512:
                                            kc * 1024 + _tc * 512 + 512],
                                        start=(kc == 0), stop=False)
                                ins = t.matmul(
                                    _tg[:, :], CSL[0:1, :],
                                    s2_negmu[0:1, _tc * 512:(_tc + 1) * 512],
                                    start=False, stop=True)
                                return ins
                            p.emit("tensor", mmp,
                                   reads=[wb_, B("xbh"), B("csl"), B("s2negmu")],
                                   writes=[B(tgn)])
                            p.emit("scalar", lambda s, _tg=tgt, _tc=tcol: s.copy(
                                QF[:, _tc * 512:(_tc + 1) * 512], _tg[:, :]),
                                reads=[B(tgn)], writes=[B("qf")],
                                waits=p.tick())
                        p.emit("gpsimd", lambda g: g.tensor_copy(
                            QSW[0:64, :], QF[64:128, :]),
                            reads=[B("qf")], writes=[B("qsw")])
                        p.emit("gpsimd", lambda g: g.tensor_copy(
                            QSW[64:128, :], QF[0:64, :]),
                            reads=[B("qf")], writes=[B("qsw")])
                        p.emit("vector", lambda v: v.tensor_mul(
                            QF[:, :], QF[:, :], TBC[:, :]),
                            reads=[B("tbc"), B("qsw"), B("qf")],
                            writes=[B("qf")])
                        p.emit("vector", lambda v: v.tensor_mul(
                            QSW[:, :], QSW[:, :], TBS[:, :]),
                            reads=[B("tbs"), B("qsw")], writes=[B("qsw")])
                        p.emit("vector", lambda v: v.tensor_add(
                            QRO[:, :], QF[:, :], QSW[:, :]),
                            reads=[B("qf"), B("qsw")], writes=[B("qro")])
                        p.emit("gpsimd", lambda g: g.tensor_mul(
                            QSQ[:, :], QRO[:, :], QRO[:, :]),
                            reads=[B("qro")], writes=[B("qsq")])
                        for tcol in range(2):
                            p.emit("tensor", lambda t, _oc=oc, _tc=tcol,
                                   _on=onh, _qs=qsb: t.matmul(
                                       _qs[0:32, :] if _tc == 0 else _qs[32:64, :],
                                       _on[:, _oc * 32:(_oc + 1) * 32],
                                       QSQ[:, _tc * 512:(_tc + 1) * 512],
                                       start=(_oc == 0), stop=False,
                                       tile_position=((0, 0) if _tc == 0
                                                      else (0, 32))),
                                   reads=[B("qsq"), B(onb)],
                                   writes=([B(qsn)] if (oc == 0 and tcol == 0)
                                           else []))
                        p.dma("sync", dst_c[oc * 128:(oc + 1) * 128, hsl],
                              QRO[:, :], reads=[B("qro")],
                              writes=[B(dst_c.tensor.name)])

                    col0 = 0 if kind == "q" else 1
                    tk = p.tick()
                    p.emit("vector", lambda v, _qs=qsb, _c=col0: v.reduce_max(
                        QMX[0:32, 8 + _c:9 + _c], _qs[0:32, :], axis=AX),
                        reads=[B(qsn)], writes=[B("qmxr")], waits=tk)
                    p.emit("vector", lambda v, _qs=qsb, _c=col0: v.reduce_max(
                        QMX[32:64, 8 + _c:9 + _c], _qs[32:64, :], axis=AX),
                        reads=[B(qsn)], writes=[B("qmxr")], waits=tk)
                    p.emit("vector", lambda v, _c=col0: v.tensor_copy(
                        QMX[0:32, 10 + _c:11 + _c], QMX[32:64, 8 + _c:9 + _c]),
                        reads=[B("qmxr")], writes=[B("qmxr")])
                    p.emit("vector", lambda v, _c=col0: v.tensor_max(
                        QMX[0:32, 12 + _c:13 + _c], QMX[0:32, 8 + _c:9 + _c],
                        QMX[0:32, 10 + _c:11 + _c]),
                        reads=[B("qmxr")], writes=[B("qmxr")])
                    p.emit("vector", lambda v, _c=col0: v.tensor_max(
                        QMX[0:32, _c:_c + 1], QMX[0:32, _c:_c + 1],
                        QMX[0:32, 12 + _c:13 + _c]),
                        reads=[B("qmxr"), B("qmx")], writes=[B("qmx")])

                for dh in range(4):
                    p.dma("scalar",
                          bass.AP(WVH, 0,
                                  [[8192, 128], [256, 32], [1, 256]]),
                          bass.AP(wvb.tensor, dh * 256,
                                  [[1024, 128], [128 * 1024, 32], [1, 256]]),
                          reads=[B("wvb")], writes=[B("wvh")])

                    def mmcsv(t):
                        ins = None
                        for kc in range(32):
                            ins = t.matmul(QE[0:1, 0:256], ONESB[:, :],
                                           WVH[:, kc * 256:(kc + 1) * 256],
                                           start=(kc == 0), stop=(kc == 31))
                        return ins
                    p.emit("tensor", mmcsv, reads=[B("wvh"), B("ONESB")],
                           writes=[B("QE")])
                    p.emit("scalar", lambda s: s.copy(CSLV[0:1, :],
                                                      QE[0:1, 0:256]),
                           reads=[B("QE")], writes=[B("cslv")],
                           waits=p.tick())
                    for tc in range(8):
                        tgt, tgn = (QA, "QA") if tc % 2 == 0 else (QB_, "QBp")

                        def mmv(t, _tg=tgt, _tc=tc):
                            ins = None
                            for kc in range(32):
                                ins = t.matmul(
                                    _tg[:, 0:256],
                                    XBH[:, kc * 1024 + _tc * 128:
                                        kc * 1024 + _tc * 128 + 128],
                                    WVH[:, kc * 256:(kc + 1) * 256],
                                    start=(kc == 0), stop=False)
                            ins = t.matmul(
                                _tg[:, 0:256],
                                s2_negmu[0:1, _tc * 128:(_tc + 1) * 128],
                                CSLV[0:1, :], start=False, stop=True)
                            return ins
                        p.emit("tensor", mmv,
                               reads=[B("wvh"), B("xbh"), B("cslv"),
                                      B("s2negmu")],
                               writes=[B(tgn)])
                        p.emit("vector", lambda v, _tg=tgt, _tc=tc:
                               v.tensor_scalar(
                                   VBE[:, :], _tg[:, 0:256],
                                   R2C[:, _tc:_tc + 1], None, ALU.mult),
                               reads=[B(tgn), B("r2c")], writes=[B("vbe")],
                               waits=p.tick())
                        p.dma("sync",
                              vc_d[half * 1024 + tc * 128:
                                   half * 1024 + (tc + 1) * 128,
                                   dh * 256:(dh + 1) * 256],
                              VBE[:, :], reads=[B("vbe")], writes=[B("vc_d")])

            p.emit("tensor", lambda t: t.transpose(
                QE[0:1, 0:32], QMX[0:32, 0:1], IDF[0:32, 0:32]),
                reads=[B("qmx"), B("IDF")], writes=[B("QE")])
            p.emit("scalar", lambda s: s.copy(STR[0:1, 0:32], QE[0:1, 0:32]),
                   reads=[B("QE")], writes=[B("str")], waits=p.tick())
            p.emit("tensor", lambda t: t.transpose(
                QE[0:1, 0:32], QMX[0:32, 1:2], IDF[0:32, 0:32]),
                reads=[B("qmx"), B("IDF"), B("str")], writes=[B("QE")])
            p.emit("scalar", lambda s: s.copy(STR[0:1, 32:64], QE[0:1, 0:32]),
                   reads=[B("QE")], writes=[B("str")], waits=p.tick())
            p.dma("sync", st_c, STR[0:1, 0:64], reads=[B("str")],
                  writes=[B("st_c")])

            def coll_st(g):
                return g.collective_compute(
                    "AllReduce", ALU.max, replica_groups=groups_all,
                    ins=[st_c.opt()], outs=[st_g.opt()])
            p.emit("gpsimd", coll_st, reads=[B("st_c")], writes=[B("st_g")], semkey="gpsimdD")
            p.dma("sync", STR[0:1, 0:64], st_g, reads=[B("st_g")],
                  writes=[B("str")])
            p.emit("vector", lambda v: v.reduce_max(
                BQ, STR[0:1, 0:32], axis=AX),
                reads=[B("str")], writes=[B("bq")])
            p.emit("vector", lambda v: v.reduce_max(
                BK, STR[0:1, 32:64], axis=AX),
                reads=[B("str")], writes=[B("bk")])
            p.emit("vector", lambda v: v.tensor_mul(BB_, BQ, BK),
                   reads=[B("bq"), B("bk")], writes=[B("bbv")])
            p.emit("scalar", lambda s: s.activation(BB_, BB_, AF.Ln),
                   reads=[B("bbv")], writes=[B("bbv")])
            p.emit("scalar", lambda s: s.activation(
                BB_, BB_, AF.Exp, scale=0.5, bias=LN_CSC_MARGIN),
                reads=[B("bbv")], writes=[B("bbv")])
            p.emit("vector", lambda v: v.tensor_scalar(
                BB_, BB_, -1.0, None, ALU.mult),
                reads=[B("bbv")], writes=[B("bbv")])
            p.emit("tensor", lambda t: t.matmul(
                QE[:, 0:1], ONESF[:, :], BB_, start=True, stop=True),
                reads=[B("bbv"), B("ONESF")], writes=[B("QE")])
            p.emit("scalar", lambda s: s.copy(BIAS[:, 0:1], QE[:, 0:1]),
                   reads=[B("QE")], writes=[B("bias")], waits=p.tick())

            for nm, src, dst_ in (("q", qc_d, qg_d), ("k", kc_d, kg_d),
                                  ("v", vc_d, vg_d)):
                def coll(g, _s=src, _d=dst_):
                    return g.collective_compute(
                        "AllGather", ALU.bypass, replica_groups=groups_pair,
                        ins=[_s.opt()], outs=[_d.opt()])
                p.emit("gpsimd", coll, reads=[B(src.tensor.name)],
                       writes=[B(dst_.tensor.name)], semkey="gpsimdD")

        p.fence()

        with ExitStack() as s3:
            e3 = s3.enter_context
            ST0 = e3(nc.psum_tensor("ST0", [128, 1024], F32))
            ST1 = e3(nc.psum_tensor("ST1", [128, 1024], F32))
            OA = e3(nc.psum_tensor("OAp", [128, 512], F32))
            OB_ = e3(nc.psum_tensor("OBp", [128, 512], F32))
            SS = e3(nc.psum_tensor("SSp", [128, 512], F32))
            QH = e3(nc.sbuf_tensor("QH", [128, 2 * T], BF16))
            KH = e3(nc.sbuf_tensor("KH", [128, 2 * T], BF16))
            VH = e3(nc.sbuf_tensor("VH", [128, 2 * T], BF16))
            PTR = e3(nc.sbuf_tensor("PTR", [128, 4096], BF16))
            SACC = e3(nc.sbuf_tensor("SACC", [128, 1024], F32))
            SACB = e3(nc.sbuf_tensor("SACB", [128, 1024], BF16))
            RSB = e3(nc.sbuf_tensor("RSB", [128, 1024], F32))
            OEV = e3(nc.sbuf_tensor("OEV", [128, 1024], BF16))
            SRW = e3(nc.sbuf_tensor("SRW", [1, 1024], F32))

            rs = p.regs

            def dyn_dma(out_ap, tensor, base_reg, const_off, dims, reads, writes):
                def fn(engine):
                    engine.reg_alu(rs["rt"], rs[base_reg], const_off, ALU.add)
                    return engine.dma_start(
                        out=out_ap, in_=bass.AP(tensor, rs["rt"], dims))
                return p.emit("sync", fn, reads=reads, writes=writes,
                              inc=16, semkey="syncD")

            for r in range(2):
                dyn_dma(qm_d[:, r * TOK:(r + 1) * TOK], qg_d.tensor, "rq",
                        r * 4096 * TOK, [[TOK, 2048], [1, TOK]],
                        reads=[B("qg_d")], writes=[B("qm_d")])
                dyn_dma(km_d[:, r * TOK:(r + 1) * TOK], kg_d.tensor, "rk",
                        r * 1024 * TOK, [[TOK, 512], [1, TOK]],
                        reads=[B("kg_d")], writes=[B("km_d")])
                dyn_dma(vm_d[r * TOK:(r + 1) * TOK, :], vg_d.tensor, "rv",
                        r * TOK * 1024, [[1024, TOK], [1, 512]],
                        reads=[B("vg_d")], writes=[B("vm_d")])

            for gl in range(4):
                gs = gl % 2
                p.dma("sync", KH[:, gs * T:(gs + 1) * T],
                      km_d[gl * 128:(gl + 1) * 128, :],
                      reads=[B("km_d")], writes=[B("kh%d" % gs)])
                p.dma("sync",
                      bass.AP(VH, gs * T, [[2 * T, 128], [128, NKC], [1, 128]]),
                      bass.AP(vm_d.tensor, gl * 128,
                              [[512, 128], [128 * 512, NKC], [1, 128]]),
                      reads=[B("vm_d")], writes=[B("vh%d" % gs)])
                for h4 in range(4):
                    hl = gl * 4 + h4
                    qs_ = hl % 2
                    p.dma("sync", QH[:, qs_ * T:(qs_ + 1) * T],
                          qm_d[hl * 128:(hl + 1) * 128, :],
                          reads=[B("qm_d")], writes=[B("qh%d" % qs_)])
                    for qsb in range(NQSB):
                        C = 8 * qsb + 8
                        for c in range(C):
                            SP, spn = (ST0, "ST0") if c % 2 == 0 else (ST1, "ST1")
                            slot = c % 4
                            diag_i = c - 8 * qsb
                            lo = 128 * diag_i if diag_i >= 0 else 0

                            def mmst_(t, _SP=SP, _c=c, _lo=lo, _g=gs, _q=qs_,
                                      _qsb=qsb):
                                ins = None
                                for hf in range(2):
                                    a = max(_lo, hf * 512)
                                    bnd = (hf + 1) * 512
                                    if a >= bnd:
                                        continue
                                    ins = t.matmul(
                                        _SP[:, a:bnd],
                                        KH[:, _g * T + _c * 128:
                                           _g * T + (_c + 1) * 128],
                                        QH[:, _q * T + _qsb * 1024 + a:
                                           _q * T + _qsb * 1024 + bnd],
                                        start=True, stop=True)
                                return ins
                            p.emit("tensor", mmst_,
                                   reads=[B("kh%d" % gs), B("qh%d" % qs_)],
                                   writes=[B(spn)])
                            p.emit("scalar", lambda s, _SP=SP, _lo=lo, _sl=slot:
                                   s.activation(
                                       PTR[:, _sl * 1024 + _lo:_sl * 1024 + 1024],
                                       _SP[:, _lo:1024], AF.Exp,
                                       bias=BIAS[:, 0:1], scale=CSC),
                                   reads=[B(spn), B("bias")],
                                   writes=[B("ptr%d" % slot)],
                                   waits=p.tick())
                            if diag_i >= 0:
                                p.emit("vector", lambda v, _sl=slot, _lo=lo:
                                       v.tensor_mul(
                                           PTR[:, _sl * 1024 + _lo:
                                               _sl * 1024 + _lo + 128],
                                           PTR[:, _sl * 1024 + _lo:
                                               _sl * 1024 + _lo + 128],
                                           TRIB),
                                       reads=[B("ptr%d" % slot), B("TRI")],
                                       writes=[B("ptr%d" % slot)])

                            if c == 0:
                                p.emit("vector", lambda v, _sl=slot:
                                       v.tensor_copy(
                                           SACC[:, :],
                                           PTR[:, _sl * 1024:
                                               _sl * 1024 + 1024]),
                                       reads=[B("ptr%d" % slot)],
                                       writes=[B("sacc")])
                            else:
                                p.emit("vector", lambda v, _sl=slot, _lo=lo:
                                       v.tensor_tensor(
                                           SACC[:, _lo:1024],
                                           SACC[:, _lo:1024],
                                           PTR[:, _sl * 1024 + _lo:
                                               _sl * 1024 + 1024],
                                           ALU.add),
                                       reads=[B("ptr%d" % slot), B("sacc")],
                                       writes=[B("sacc")])

                            def mmpv(t, _c=c, _lo=lo, _sl=slot, _g=gs,
                                     _st=(c == 0)):
                                ins = None
                                for hf in range(2):
                                    a = max(_lo, hf * 512)
                                    bnd = (hf + 1) * 512
                                    if a >= bnd:
                                        continue
                                    OP = OA if hf == 0 else OB_
                                    ins = t.matmul(
                                        OP[:, a - hf * 512:bnd - hf * 512],
                                        VH[:, _g * T + _c * 128:
                                           _g * T + (_c + 1) * 128],
                                        PTR[:, _sl * 1024 + a:_sl * 1024 + bnd],
                                        start=_st, stop=False)
                                return ins
                            p.emit("tensor", mmpv,
                                   reads=[B("ptr%d" % slot), B("vh%d" % gs)],
                                   writes=([B("OAp"), B("OBp")]
                                           if c == 0 else []))

                        p.emit("vector", lambda v: v.tensor_copy(
                            SACB[:, :], SACC[:, :]),
                            reads=[B("sacc")], writes=[B("sacb")])

                        def mmss(t):
                            i = t.matmul(SS[0:1, :], ONESB[:, :],
                                         SACB[:, 0:512], start=True, stop=True)
                            i = t.matmul(SS[32:33, :], ONESB[:, :],
                                         SACB[:, 512:1024], start=True,
                                         stop=True, tile_position=(0, 32))
                            return i
                        p.emit("tensor", mmss, reads=[B("sacb"), B("ONESB")],
                               writes=[B("SSp")])
                        tk = p.tick()
                        p.emit("scalar", lambda s: s.copy(
                            SRW[0:1, 0:512], SS[0:1, :]),
                            reads=[B("SSp")], writes=[B("srw")], waits=tk)
                        p.emit("scalar", lambda s: s.copy(
                            SRW[0:1, 512:1024], SS[32:33, :]),
                            reads=[B("SSp")], writes=[B("srw")], waits=tk)
                        p.emit("scalar", lambda s: s.activation(
                            SRW[0:1, :], SRW[0:1, :], AF.Ln),
                            reads=[B("srw")], writes=[B("srw")])
                        p.emit("scalar", lambda s: s.activation(
                            SRW[0:1, :], SRW[0:1, :], AF.Exp, scale=-1.0),
                            reads=[B("srw")], writes=[B("srw")])
                        for c2 in range(2):
                            p.emit("tensor", lambda t, _c=c2: t.matmul(
                                ST0[:, _c * 512:(_c + 1) * 512], ONESF[:, :],
                                SRW[0:1, _c * 512:(_c + 1) * 512],
                                start=True, stop=True),
                                reads=[B("srw"), B("ONESF")], writes=[B("ST0")])
                        p.emit("scalar", lambda s: s.copy(RSB[:, :], ST0[:, :]),
                               reads=[B("ST0")], writes=[B("rsb")],
                               waits=p.tick())
                        p.emit("vector", lambda v: v.tensor_mul(
                            OEV[:, 0:512], OA[:, :], RSB[:, 0:512]),
                            reads=[B("OAp"), B("rsb")], writes=[B("oev")])
                        p.emit("vector", lambda v: v.tensor_mul(
                            OEV[:, 512:1024], OB_[:, :], RSB[:, 512:1024]),
                            reads=[B("OBp"), B("rsb")], writes=[B("oev")])
                        p.dma("sync",
                              oc_d[hl * 128:(hl + 1) * 128,
                                   qsb * 1024:(qsb + 1) * 1024],
                              OEV[:, :], reads=[B("oev")], writes=[B("oc_d")])

            def coll_o(g):
                return g.collective_compute(
                    "AllGather", ALU.bypass, replica_groups=groups_pair,
                    ins=[oc_d.opt()], outs=[og_d.opt()])
            p.emit("gpsimd", coll_o, reads=[B("oc_d")], writes=[B("og_d")], semkey="gpsimdD")

        p.fence()

        with ExitStack() as s4:
            e4 = s4.enter_context
            WA_ = e4(nc.psum_tensor("WAp", [128, 512], F32))
            WB2 = e4(nc.psum_tensor("WBp", [128, 512], F32))
            SX = e4(nc.psum_tensor("SXp", [128, 512], F32))
            SX2 = e4(nc.psum_tensor("SX2p", [128, 512], F32))
            OGH = e4(nc.sbuf_tensor("OGH", [128, 32768], BF16))
            WOB = e4(nc.sbuf_tensor("WOBs", [128, 8192], BF16))
            X1C = e4(nc.sbuf_tensor("X1C", [128, 1024], F32))
            X2C = e4(nc.sbuf_tensor("X2C", [128, 1024], F32))
            X2B = e4(nc.sbuf_tensor("X2B", [128, 1024], BF16))
            SQ2 = e4(nc.sbuf_tensor("SQ2", [128, 1024], BF16))

            s4_negmu = ROWA[0:1, 0:512]
            s4_r = ROWA[0:1, 512:1024]
            s4_t1 = ROWA[0:1, 1024:1536]
            s4_t2 = ROWA[0:1, 1536:2048]
            s4_v = ROWA[0:1, 2048:2560]
            p.emit("vector", lambda v: v.memset(LVACC2, 0.0), writes=[B("lvacc2")])

            rs = p.regs
            for r4 in range(4):
                def omdma(engine, _r=r4):
                    engine.reg_alu(rs["rt"], rs["ro"], _r * 1024 * T, ALU.add)
                    return engine.dma_start(
                        out=om_d[_r * 1024:(_r + 1) * 1024, :],
                        in_=bass.AP(og_d.tensor, rs["rt"],
                                    [[T, 1024], [1, TOK]]))
                p.emit("sync", omdma, reads=[B("og_d")], writes=[B("om_d")],
                       inc=16, semkey="syncD")
            for half in range(NH):
                p.dma("sync",
                      bass.AP(OGH, 0, [[32768, 128], [1024, 32], [1, 1024]]),
                      bass.AP(om_d.tensor, half * 1024,
                              [[TOK, 128], [128 * TOK, 32], [1, 1024]]),
                      reads=[B("om_d")], writes=[B("ogh")])
                for cc in range(32):
                    ws = cc % 2
                    wb_ = B("wob%d" % ws)
                    p.dma("scalar",
                          bass.AP(WOB, ws * 4096,
                                  [[8192, 128], [128, 32], [1, 128]]),
                          bass.AP(wob.tensor, cc * 128,
                                  [[4096, 128], [128 * 4096, 32], [1, 128]]),
                          reads=[B("wob")], writes=[wb_])
                    p.dma("scalar", X1C[:, :],
                          bass.AP(x1d.tensor, cc * 128 * TOK + half * 1024,
                                  [[TOK, 128], [1, 1024]]),
                          reads=[B("x1d")], writes=[B("x1c")])
                    for tcol in range(2):
                        tgt, tgn = (WA_, "WAp") if tcol == 0 else (WB2, "WBp")

                        def mmo(t, _tg=tgt, _ws=ws, _tc=tcol):
                            ins = None
                            for hd in range(32):
                                ins = t.matmul(
                                    _tg[:, :],
                                    WOB[:, _ws * 4096 + hd * 128:
                                        _ws * 4096 + hd * 128 + 128],
                                    OGH[:, hd * 1024 + _tc * 512:
                                        hd * 1024 + _tc * 512 + 512],
                                    start=(hd == 0), stop=(hd == 31))
                            return ins
                        p.emit("tensor", mmo, reads=[wb_, B("ogh")],
                               writes=[B(tgn)])
                        p.emit("vector", lambda v, _tg=tgt, _tc=tcol, _cc=cc:
                               v.scalar_tensor_tensor(
                                   X2C[:, _tc * 512:(_tc + 1) * 512], _tg[:, :],
                                   BO2[:, _cc:_cc + 1],
                                   X1C[:, _tc * 512:(_tc + 1) * 512],
                                   ALU.add, ALU.add),
                               reads=[B(tgn), B("x1c"), B("BO2")],
                               writes=[B("x2c")], waits=p.tick())
                    p.dma("sync",
                          bass.AP(x2d.tensor, cc * 128 * TOK + half * 1024,
                                  [[TOK, 128], [1, 1024]]),
                          X2C[:, :], reads=[B("x2c")], writes=[B("x2d")])
                    p.emit("gpsimd", lambda g: g.tensor_copy(X2B[:, :],
                                                             X2C[:, :]),
                           reads=[B("x2c")], writes=[B("x2b")])
                    p.dma("sync",
                          bass.AP(xbfd.tensor, cc * 128 * TOK + half * 1024,
                                  [[TOK, 128], [1, 1024]]),
                          X2B[:, :], reads=[B("x2b")], writes=[B("xbfd")])
                    p.emit("gpsimd", lambda g: g.tensor_mul(
                        SQ2[:, :], X2B[:, :], X2B[:, :]),
                        reads=[B("x2b")], writes=[B("sq2")])

                    def mmsx(t, _cc=cc):
                        ins = None
                        for tc in range(2):
                            st = (_cc == 0)
                            tp = (0, 0) if tc == 0 else (0, 32)
                            tsl = slice(tc * 512, (tc + 1) * 512)
                            ins = t.matmul(
                                SX[0:1, :] if tc == 0 else SX[32:33, :],
                                ONESB[:, :], X2B[:, tsl],
                                start=st, stop=False, tile_position=tp)
                            ins = t.matmul(
                                SX2[0:1, :] if tc == 0 else SX2[32:33, :],
                                ONESB[:, :], SQ2[:, tsl],
                                start=st, stop=False, tile_position=tp)
                        return ins
                    p.emit("tensor", mmsx,
                           reads=[B("x2b"), B("sq2"), B("ONESB")],
                           writes=([B("SXp"), B("SX2p")] if cc == 0 else []))

                for tc in range(2):
                    tile = half * 2 + tc
                    tsl = slice(tile * 512, (tile + 1) * 512)
                    srcS = SX[0:1, :] if tc == 0 else SX[32:33, :]
                    srcS2 = SX2[0:1, :] if tc == 0 else SX2[32:33, :]
                    tk = p.tick()
                    p.emit("vector", lambda v, _s=srcS: v.tensor_scalar(
                        s4_negmu, _s, -1.0 / 4096.0, None, ALU.mult),
                        reads=[B("SXp")], writes=[B("s4negmu")], waits=tk)
                    p.emit("vector", lambda v, _s=srcS2: v.tensor_scalar(
                        s4_t1, _s, 1.0 / 4096.0, None, ALU.mult),
                        reads=[B("SX2p")], writes=[B("s4t1")], waits=tk)
                    p.emit("vector", lambda v: v.tensor_mul(
                        s4_t2, s4_negmu, s4_negmu),
                        reads=[B("s4negmu")], writes=[B("s4t2")])
                    p.emit("vector", lambda v: v.tensor_sub(s4_v, s4_t1, s4_t2),
                           reads=[B("s4t1"), B("s4t2")], writes=[B("s4v")])
                    p.emit("scalar", lambda s: s.activation(
                        s4_t1, s4_v, AF.Ln, bias=EPS),
                        reads=[B("s4v")], writes=[B("s4t1")])
                    p.emit("scalar", lambda s: s.activation(
                        s4_r, s4_t1, AF.Exp, scale=-0.5),
                        reads=[B("s4t1")], writes=[B("s4r")])
                    p.dma("sync", rowsd[8:9, tsl], s4_negmu,
                          reads=[B("s4negmu")], writes=[B("rw8")])
                    p.dma("sync", rowsd[9:10, tsl], s4_r,
                          reads=[B("s4r")], writes=[B("rw9")])
                    p.emit("vector", lambda v: v.tensor_mul(s4_t2, s4_r, s4_r),
                           reads=[B("s4r")], writes=[B("s4t2")])
                    p.emit("vector", lambda v: v.tensor_mul(s4_t1, s4_v, s4_t2),
                           reads=[B("s4v"), B("s4t2")], writes=[B("s4t1")])
                    p.emit("scalar", lambda s: s.activation(
                        s4_t2, s4_t1, AF.Ln, bias=1.0),
                        reads=[B("s4t1")], writes=[B("s4t2")])
                    p.dma("sync", rowsd[15:16, tsl], s4_t2,
                          reads=[B("s4t2")], writes=[B("rw15")])
                    p.emit("vector", lambda v: v.reduce_sum(
                        LVTMP2, s4_t2, axis=AX),
                        reads=[B("s4t2")], writes=[B("lvt2")])
                    p.emit("vector", lambda v: v.tensor_add(
                        LVACC2, LVACC2, LVTMP2),
                        reads=[B("lvt2"), B("lvacc2")], writes=[B("lvacc2")])

            p.emit("vector", lambda v: v.memset(LVST2, 0.0), writes=[B("lvst2")])
            p.emit("vector", lambda v: v.tensor_copy(
                ROWD[0:1, 32:33], LVACC2),
                reads=[B("lvacc2")], writes=[B("lvst2")])
            p.dma("sync", lv_c2, LVST2, reads=[B("lvst2")], writes=[B("lv_c2")])

            def coll_lv2(g):
                return g.collective_compute(
                    "AllReduce", ALU.add, replica_groups=groups_all,
                    ins=[lv_c2.opt()], outs=[lv_g2.opt()])
            p.emit("gpsimd", coll_lv2, reads=[B("lv_c2")], writes=[B("lvgf2")], semkey="gpsimdD")

        p.fence()

        emit_ffn(x2d, y_out, rowbank=8, lvg=lv_g2, lvc=lv_c2, prepass=False,
                 tag="f2")

        for name in dbg:
            src = io[name]
            dst = io["dbg_" + name]
            p.dma("sync", dst, src, reads=[B(name)], writes=[B("dbg_" + name)])

        snap = [(k, p.cnt[k]) for k in SEMKEYS if p.cnt[k] > 0]
        p.emit("sync", lambda s: s.nop(), waits=snap, inc=1)

        with nc.Block() as block:
            p.replay(block)

    return nc



def _rope_tables(half, TOK):
    pos = np.arange(half * TOK, (half + 1) * TOK, dtype=np.float64)
    freqs = 1.0 / (10000.0 ** (np.arange(0, 128, 2, dtype=np.float64) / 128.0))
    ang = freqs[:, None] * pos[None, :]
    c = np.cos(ang).astype(np.float32)
    s = np.sin(ang).astype(np.float32)
    tabc = np.concatenate([c, c], axis=0)
    tabs = np.concatenate([-s, s], axis=0)
    return np.ascontiguousarray(tabc), np.ascontiguousarray(tabs)


def _perm_deinterleave(ncols):
    perm = np.zeros(ncols, dtype=np.int64)
    for h in range(ncols // 128):
        b = h * 128
        perm[b:b + 64] = b + np.arange(64) * 2
        perm[b + 64:b + 128] = b + np.arange(64) * 2 + 1
    return perm


def prepare_inputs(inputs, npairs, T):
    TOK = T // 2
    bf = ml_dtypes.bfloat16
    x = np.asarray(inputs["L0_attn_x"], dtype=np.float32)

    shared = dict(
        awq=np.ascontiguousarray(
            np.asarray(inputs["L0_attn_Wq"], np.float32)[:,
                                                         _perm_deinterleave(4096)]),
        awk=np.ascontiguousarray(
            np.asarray(inputs["L0_attn_Wk"], np.float32)[:,
                                                         _perm_deinterleave(1024)]),
        awv=np.ascontiguousarray(np.asarray(inputs["L0_attn_Wv"], np.float32)),
        awo=np.ascontiguousarray(np.asarray(inputs["L0_attn_Wo"], np.float32)),
        fwqt=np.ascontiguousarray(np.asarray(inputs["L0_ffn_Wq"], np.float32).T),
        fwrt=np.ascontiguousarray(np.asarray(inputs["L0_ffn_Wr"], np.float32).T),
        akt=np.ascontiguousarray(
            np.asarray(inputs["L0_ffn_aux_keys"], np.float32).T),
        avt=np.ascontiguousarray(
            np.asarray(inputs["L0_ffn_aux_values"], np.float32).T),
        wvdt=np.ascontiguousarray(np.asarray(inputs["L0_ffn_Wvd"], np.float32).T),
        wvut=np.ascontiguousarray(np.asarray(inputs["L0_ffn_Wvu"], np.float32).T),
        wu1t=np.ascontiguousarray(np.asarray(inputs["L0_ffn_Wu1"], np.float32).T),
        wu2t=np.ascontiguousarray(np.asarray(inputs["L0_ffn_Wu2"], np.float32).T),
        bo2=np.ascontiguousarray(
            np.asarray(inputs["L0_attn_bo"], np.float32).reshape(32, 128).T),
        bu12=np.ascontiguousarray(
            np.asarray(inputs["L0_ffn_bu1"], np.float32).reshape(8, 128).T),
        scal=np.array([[float(np.asarray(inputs["L0_ffn_bu2"]).ravel()[0]),
                        float(np.asarray(inputs["L0_ffn_gate_w1"])),
                        float(np.asarray(inputs["L0_ffn_gate_bias"])), 0.0]],
                      np.float32),
        logrel=np.ascontiguousarray(
            np.asarray(inputs["L0_ffn_log_rel"], np.float32).reshape(1, 1024)),
        identf=np.eye(128, dtype=np.float32),
        identb=np.eye(128, dtype=np.float32).astype(bf),
        tri=np.ascontiguousarray(
            (np.arange(128)[:, None] <= np.arange(128)[None, :])
            .astype(np.float32)).astype(bf),
    )
    onq = np.zeros((128, 1024), np.float32)
    for h in range(32):
        onq[:, h * 32 + h] = 1.0
    onk = np.zeros((128, 256), np.float32)
    for g in range(8):
        onk[:, g * 32 + 4 * g:g * 32 + 4 * g + 4] = 1.0
    shared["oneq"] = onq.astype(bf)
    shared["onek"] = onk.astype(bf)

    in_maps = []
    for core in range(2 * npairs):
        b, half = core // 2, core % 2
        m = dict(shared)
        m["x"] = np.ascontiguousarray(
            x[b, half * TOK:(half + 1) * TOK, :].T.astype(np.float32))
        tc, ts = _rope_tables(half, TOK)
        m["tabc"] = tc
        m["tabs"] = ts
        m["pari"] = np.full((1, 1), half, np.int32)
        in_maps.append(m)
    return in_maps


def kernel(**inputs):
    npairs, T = 4, 4096
    TOK = T // 2
    in_maps = prepare_inputs(inputs, npairs, T)
    nc = build_program(npairs, T)
    res = run_bass_kernel_spmd(nc, in_maps, core_ids=list(range(2 * npairs)))
    out = np.empty((npairs, T, 4096), dtype=np.float32)
    for core in range(2 * npairs):
        b, half = core // 2, core % 2
        out[b, half * TOK:(half + 1) * TOK, :] = res.results[core]["y"].T
    return out


# version 03
# speedup vs baseline: 1.0235x; 1.0235x over previous
import numpy as np
import ml_dtypes
from contextlib import ExitStack

import concourse.bass as bass
import concourse.mybir as mybir
from concourse.bass_utils import run_bass_kernel_spmd

F32 = mybir.dt.float32
BF16 = mybir.dt.bfloat16
I32 = mybir.dt.int32
AF = mybir.ActivationFunctionType
ALU = mybir.AluOpType
AX = mybir.AxisListType.X

ENGINES = ("sync", "vector", "scalar", "tensor", "gpsimd")
NEGBIG = -1.0e30
EPS = 1e-5
CSC = float(128.0 ** -0.5)
LN_CSC_MARGIN = float(np.log(1.02 * CSC))


class Buf:
    __slots__ = ("ready", "users", "name")

    def __init__(self, name=""):
        self.ready = None
        self.users = []
        self.name = name


SEMKEYS = list(ENGINES) + [e + "D" for e in ENGINES]


class Prog:
    def __init__(self, nc, ctx):
        self.nc = nc
        self.q = {e: [] for e in ENGINES}
        self.sem = {k: ctx.enter_context(nc.semaphore("S" + k)) for k in SEMKEYS}
        self.cnt = {k: 0 for k in SEMKEYS}
        self.clock = {e: {k: 0 for k in SEMKEYS} for e in ENGINES}
        self.regs = {}
        self.bufs = {}

    def B(self, name):
        if name not in self.bufs:
            self.bufs[name] = Buf(name)
        return self.bufs[name]

    def emit(self, eng, fn, reads=(), writes=(), waits=(), inc=1, semkey=None):
        key = semkey or eng
        need = {}

        def add(t):
            if t is None:
                return
            f, v = t
            if v > need.get(f, 0):
                need[f] = v

        for b in reads:
            add(b.ready)
        for b in writes:
            add(b.ready)
            for t in b.users:
                add(t)
        for t in waits:
            add(t)

        cl = self.clock[eng]
        real = tuple((f, v) for f, v in need.items() if v > cl[f])
        for f, v in real:
            cl[f] = v

        self.cnt[key] += inc
        ticket = (key, self.cnt[key])
        sems = self.sem
        my_sem = sems[key]

        def g(engine, _fn=fn, _real=real, _inc=inc, _sem=my_sem):
            for f, v in _real:
                engine.wait_ge(sems[f], v)
            ins = _fn(engine)
            if _inc:
                ins.then_inc(_sem, _inc)

        self.q[eng].append(g)
        if inc:
            for b in reads:
                b.users.append(ticket)
            for b in writes:
                b.ready = ticket
                b.users = []
        return ticket

    def dma(self, eng, out_ap, in_ap, reads=(), writes=(), waits=(), slow=False):
        def fn(engine, _o=out_ap, _i=in_ap, _s=slow):
            return engine.dma_start(out=_o, in_=_i, allow_slow_non_contiguous=_s)
        return self.emit(eng, fn, reads=reads, writes=writes, waits=waits,
                         inc=16, semkey=eng + "D")

    def tick(self, eng="tensor"):
        return [(eng, self.cnt[eng])]

    def fence(self):
        snap = [(k, self.cnt[k]) for k in SEMKEYS if self.cnt[k] > 0]
        for e in ENGINES:
            self.emit(e, lambda engine: engine.nop(), waits=snap, inc=1)

    def replay(self, block):
        for e in ENGINES:
            qe = self.q[e]
            if not qe:
                continue

            def body(engine, _qe=qe):
                for f in _qe:
                    f(engine)

            getattr(block, e)(body)


def build_program(npairs, T, dbg=()):
    TOK = T // 2
    NT = TOK // 512
    NQSB = T // 1024
    NKC = T // 128
    NH = TOK // 1024
    NCORE = 2 * npairs
    NTOK_ALL = npairs * T

    nc = bass.Bass("TRN2", target_bir_lowering=False)
    ctx = ExitStack()
    io = {}

    def din(name, shape, dt=F32):
        io[name] = nc.dram_tensor(name, list(shape), dt, kind="ExternalInput").ap()
        return io[name]

    def dout(name, shape, dt=F32):
        io[name] = nc.dram_tensor(name, list(shape), dt, kind="ExternalOutput").ap()
        return io[name]

    def dint(name, shape, dt=F32):
        io[name] = nc.dram_tensor(name, list(shape), dt, kind="Internal").ap()
        return io[name]

    x_in = din("x", (4096, TOK))
    y_out = dout("y", (4096, TOK))
    awq = din("awq", (4096, 4096))
    awk = din("awk", (4096, 1024))
    awv = din("awv", (4096, 1024))
    awo = din("awo", (4096, 4096))
    fwqt = din("fwqt", (4096, 64))
    fwrt = din("fwrt", (64, 48))
    akt = din("akt", (64, 1024))
    avt = din("avt", (4096, 1024))
    wvdt = din("wvdt", (4096, 256))
    wvut = din("wvut", (256, 4096))
    wu1t = din("wu1t", (4096, 1024))
    wu2t = din("wu2t", (1024, 1))
    bo2 = din("bo2", (128, 32))
    bu12 = din("bu12", (128, 8))
    scal = din("scal", (1, 4))
    logrel = din("logrel", (1, 1024))
    tabc = din("tabc", (128, TOK))
    tabs = din("tabs", (128, TOK))
    identf = din("identf", (128, 128))
    identb = din("identb", (128, 128), BF16)
    tri = din("tri", (128, 128), BF16)
    oneq = din("oneq", (128, 1024), BF16)
    onek = din("onek", (128, 256), BF16)
    pari = din("pari", (1, 1), I32)

    x1d = dint("x1d", (4096, TOK))
    x2d = dint("x2d", (4096, TOK))
    xbfd = dint("xbfd", (4096, TOK), BF16)
    auxd = dint("auxd", (4096, TOK), BF16)
    rowsd = dint("rowsd", (32, TOK))
    wqb = dint("wqb", (4096, 4096), BF16)
    wkb = dint("wkb", (4096, 1024), BF16)
    wvb = dint("wvb", (4096, 1024), BF16)
    wob = dint("wob", (4096, 4096), BF16)
    fwqtb = dint("fwqtb", (4096, 64), BF16)
    fwrtb = dint("fwrtb", (64, 48), BF16)
    aktb = dint("aktb", (64, 1024), BF16)
    avtb = dint("avtb", (4096, 1024), BF16)
    wvdtb = dint("wvdtb", (4096, 256), BF16)
    wvutb = dint("wvutb", (256, 4096), BF16)
    wu1tb = dint("wu1tb", (4096, 1024), BF16)
    wu2tb = dint("wu2tb", (1024, 1), BF16)
    apd = dint("apd", (1024, 256), BF16)
    colsd = dint("colsd", (4, 4096))
    qc_d = dint("qc_d", (4096, TOK), BF16)
    qg_d = dint("qg_d", (2 * 4096, TOK), BF16)
    kc_d = dint("kc_d", (1024, TOK), BF16)
    kg_d = dint("kg_d", (2 * 1024, TOK), BF16)
    vc_d = dint("vc_d", (TOK, 1024), BF16)
    vg_d = dint("vg_d", (2 * TOK, 1024), BF16)
    oc_d = dint("oc_d", (2048, T), BF16)
    og_d = dint("og_d", (4096, T), BF16)
    qm_d = dint("qm_d", (2048, T), BF16)
    km_d = dint("km_d", (512, T), BF16)
    vm_d = dint("vm_d", (T, 512), BF16)
    om_d = dint("om_d", (4096, TOK), BF16)
    st_c = dint("st_c", (1, 64))
    st_g = dint("st_g", (1, 64))
    lv_c = dint("lv_c", (1, 8))
    lv_g = dint("lv_g", (1, 8))
    lv_c2 = dint("lv_c2", (1, 8))
    lv_g2 = dint("lv_g2", (1, 8))

    for name in dbg:
        src = io[name]
        dout("dbg_" + name, list(src.tensor.shape), src.tensor.dtype)

    groups_pair = [[2 * i, 2 * i + 1] for i in range(npairs)]
    groups_all = [list(range(NCORE))]

    with ctx:
        p = Prog(nc, ctx)
        B = p.B
        e = ctx.enter_context

        CONST = e(nc.sbuf_tensor("CONST", [128, 256], F32))
        IDF = CONST[:, 0:128]
        ONESF = CONST[0:1, 128:256]
        CONSTB = e(nc.sbuf_tensor("CONSTB", [128, 384], BF16))
        IDB = CONSTB[:, 0:128]
        TRIB = CONSTB[:, 128:256]
        ONESB = CONSTB[:, 256:257]
        BO2 = e(nc.sbuf_tensor("BO2", [128, 48], F32))
        SCAL = e(nc.sbuf_tensor("SCAL", [1, 16], F32))
        LR = e(nc.sbuf_tensor("LRR", [1, 1024], F32))
        PTI = e(nc.sbuf_tensor("PTI", [1, 8], I32))
        BIAS = e(nc.sbuf_tensor("BIAS", [128, 8], F32))
        ROWA = e(nc.sbuf_tensor("ROWA", [1, 4096], F32))
        ROWB = e(nc.sbuf_tensor("ROWB", [1, 4096], F32))
        ROWD = e(nc.sbuf_tensor("ROWD", [1, 128], F32))
        RKBF = e(nc.sbuf_tensor("RKBF", [48, 1024], BF16))
        FWQ = e(nc.sbuf_tensor("FWQ", [128, 2048], BF16))
        FWR = e(nc.sbuf_tensor("FWR", [64, 48], BF16))
        AKT = e(nc.sbuf_tensor("AKT", [64, 1024], BF16))
        APR = e(nc.sbuf_tensor("APR", [128, 2048], BF16))
        WVU = e(nc.sbuf_tensor("WVU", [128, 8192], BF16))
        W2S = e(nc.sbuf_tensor("W2S", [128, 8], BF16))
        CSF = e(nc.sbuf_tensor("CSF", [1, 1152], F32))
        DCA = e(nc.sbuf_tensor("DCA", [128, 512], F32))
        DCB = e(nc.sbuf_tensor("DCB", [128, 512], BF16))
        DCA2 = e(nc.sbuf_tensor("DCA2", [128, 512], F32))
        DCB2 = e(nc.sbuf_tensor("DCB2", [128, 512], BF16))

        p.dma("sync", CONST[:, 0:128], identf, writes=[B("IDF")])
        p.dma("sync", CONSTB[:, 0:128], identb, writes=[B("IDB")])
        p.dma("sync", CONSTB[:, 128:256], tri, writes=[B("TRI")])
        p.dma("sync", BO2[:, 0:32], bo2, writes=[B("BO2")])
        p.dma("sync", BO2[:, 32:40], bu12, writes=[B("BU1")])
        p.dma("sync", SCAL[0:1, 0:4], scal, writes=[B("SCAL")])
        p.dma("sync", LR[0:1, :], logrel, writes=[B("LRR")])
        p.dma("sync", PTI[0:1, 0:1], pari, writes=[B("PTI")])
        p.emit("vector", lambda v: v.memset(CONST[0:1, 128:256], 1.0),
               writes=[B("ONESF")])
        p.emit("vector", lambda v: v.memset(CONSTB[:, 256:257], 1.0),
               writes=[B("ONESB")])
        CONSTV = e(nc.sbuf_tensor("CONSTV", [128, 4], F32))
        for ci, cv in ((0, EPS), (1, 1.0), (2, LN_CSC_MARGIN)):
            p.emit("vector", lambda v, _i=ci, _v=cv: v.memset(
                CONSTV[:, _i:_i + 1], _v), writes=[B("constv")])
            nc.const_aps.register(F32, cv, CONSTV[:, ci:ci + 1])

        def mkregs(engine):
            rp = engine.alloc_register("rp")
            engine.reg_load(rp, PTI[0:1, 0:1])
            r1 = engine.alloc_register("rqr")
            engine.reg_mul(r1, rp, 2048 * TOK)
            r2 = engine.alloc_register("rkr")
            engine.reg_mul(r2, rp, 512 * TOK)
            r3 = engine.alloc_register("rvr")
            engine.reg_mul(r3, rp, 512)
            r4 = engine.alloc_register("ror")
            engine.reg_mul(r4, rp, TOK)
            rt = engine.alloc_register("rtt")
            p.regs.update(rq=r1, rk=r2, rv=r3, ro=r4, rt=rt)
            return engine.nop()

        p.emit("sync", mkregs, reads=[B("PTI")])

        with ExitStack() as s0:
            e0 = s0.enter_context
            P0S = e0(nc.psum_tensor("P0S", [128, 512], F32))
            CW = e0(nc.sbuf_tensor("CW", [128, 1024], F32))
            CW2 = e0(nc.sbuf_tensor("CW2", [128, 1024], F32))
            CWB = e0(nc.sbuf_tensor("CWB", [128, 1024], BF16))
            CWB2 = e0(nc.sbuf_tensor("CWB2", [128, 1024], BF16))
            WA0 = e0(nc.sbuf_tensor("WA0", [128, 4096], BF16))
            WD0 = e0(nc.sbuf_tensor("WD0", [128, 8192], BF16))
            EV0 = e0(nc.sbuf_tensor("EV0", [128, 256], BF16))

            def cast_weight(src, dst, rows, cols, colsum=None):
                nrc = (rows + 127) // 128
                for c0 in range(0, cols, 512):
                    cc = min(512, cols - c0)
                    for ri in range(nrc):
                        r0 = ri * 128
                        rr = min(128, rows - r0)
                        stage, stageb, sb = (
                            (CW, CWB, "cw0") if ri % 2 == 0 else (CW2, CWB2, "cw1"))
                        p.dma("sync", stage[0:rr, 0:cc],
                              src[r0:r0 + rr, c0:c0 + cc], writes=[B(sb)])
                        p.emit("vector",
                               lambda v, _s=stage, _b=stageb, _rr=rr, _cc=cc:
                               v.tensor_copy(_b[0:_rr, 0:_cc], _s[0:_rr, 0:_cc]),
                               reads=[B(sb)], writes=[B(sb + "b")])
                        p.dma("sync", dst[r0:r0 + rr, c0:c0 + cc],
                              stageb[0:rr, 0:cc], reads=[B(sb + "b")],
                              writes=[B(dst.tensor.name)])
                        if colsum is not None:
                            p.emit("tensor",
                                   lambda t, _b=stageb, _rr=rr, _cc=cc,
                                   _st=(ri == 0), _sp=(ri == nrc - 1):
                                   t.matmul(P0S[0:1, 0:_cc], ONESB[0:_rr, :],
                                            _b[0:_rr, 0:_cc], start=_st, stop=_sp),
                                   reads=[B(sb + "b"), B("ONESB")],
                                   writes=([B("P0S")] if ri == 0 else []))
                    if colsum is not None:
                        kindcs, off = colsum
                        p.emit("scalar",
                               lambda s, _cc=cc: s.copy(ROWA[0:1, 0:_cc],
                                                        P0S[0:1, 0:_cc]),
                               reads=[B("P0S")], writes=[B("csrow")],
                               waits=p.tick())
                        if kindcs == "sbuf":
                            p.emit("vector",
                                   lambda v, _c0=c0, _cc=cc, _o=off:
                                   v.tensor_copy(CSF[0:1, _o + _c0:_o + _c0 + _cc],
                                                 ROWA[0:1, 0:_cc]),
                                   reads=[B("csrow")], writes=[B("csf")])
                        else:
                            p.dma("sync", colsd[off:off + 1, c0:c0 + cc],
                                  ROWA[0:1, 0:cc], reads=[B("csrow")],
                                  writes=[B("colsd")])

            cast_weight(fwqt, fwqtb, 4096, 64, colsum=("sbuf", 0))
            cast_weight(fwrt, fwrtb, 64, 48)
            cast_weight(akt, aktb, 64, 1024)
            cast_weight(wvdt, wvdtb, 4096, 256)
            cast_weight(avt, avtb, 4096, 1024)
            cast_weight(wvut, wvutb, 256, 4096)
            cast_weight(wu1t, wu1tb, 4096, 1024, colsum=("sbuf", 64))
            cast_weight(wu2t, wu2tb, 1024, 1)

            p.dma("sync",
                  bass.AP(WD0, 0,
                          [[8192, 128], [256, 32], [1, 256]]),
                  bass.AP(wvdtb.tensor, 0, [[256, 128], [128 * 256, 32], [1, 256]]),
                  reads=[B("wvdtb")], writes=[B("wd0")])
            for nch in range(8):
                p.dma("sync",
                      bass.AP(WA0, 0,
                              [[4096, 128], [128, 32], [1, 128]]),
                      bass.AP(avtb.tensor, nch * 128,
                              [[1024, 128], [128 * 1024, 32], [1, 128]]),
                      reads=[B("avtb")], writes=[B("wa0")])

                def mmA(t):
                    ins = None
                    for kc in range(32):
                        ins = t.matmul(P0S[:, 0:256],
                                       WA0[:, kc * 128:(kc + 1) * 128],
                                       WD0[:, kc * 256:(kc + 1) * 256],
                                       start=(kc == 0), stop=(kc == 31))
                    return ins

                p.emit("tensor", mmA, reads=[B("wa0"), B("wd0")], writes=[B("P0S")])
                p.emit("scalar", lambda s: s.copy(EV0[:, 0:256], P0S[:, 0:256]),
                       reads=[B("P0S")], writes=[B("ev0")], waits=p.tick())
                p.dma("sync", apd[nch * 128:(nch + 1) * 128, 0:256], EV0[:, 0:256],
                      reads=[B("ev0")], writes=[B("apd")])

            p.dma("sync", FWR[:, :], fwrtb, reads=[B("fwrtb")], writes=[B("FWR")])
            p.dma("sync", AKT[:, :], aktb, reads=[B("aktb")], writes=[B("AKT")])
            for c2 in range(2):
                p.emit("tensor", lambda t, _c=c2: t.matmul(
                    P0S[0:48, :], FWR[:, :], AKT[:, _c * 512:(_c + 1) * 512],
                    start=True, stop=True),
                    reads=[B("FWR"), B("AKT")], writes=[B("P0S")])
                p.emit("vector", lambda v, _c=c2: v.tensor_copy(
                    RKBF[:, _c * 512:(_c + 1) * 512], P0S[0:48, :]),
                    reads=[B("P0S")], writes=[B("RKBF")], waits=p.tick())

            p.dma("sync",
                  bass.AP(FWQ, 0,
                          [[2048, 128], [64, 32], [1, 64]]),
                  bass.AP(fwqtb.tensor, 0, [[64, 128], [128 * 64, 32], [1, 64]]),
                  reads=[B("fwqtb")], writes=[B("FWQ")])
            p.dma("sync",
                  bass.AP(APR, 0,
                          [[2048, 128], [256, 8], [1, 256]]),
                  bass.AP(apd.tensor, 0, [[256, 128], [128 * 256, 8], [1, 256]]),
                  reads=[B("apd")], writes=[B("APR")])
            p.dma("sync",
                  bass.AP(WVU, 0,
                          [[8192, 128], [4096, 2], [1, 4096]]),
                  bass.AP(wvutb.tensor, 0,
                          [[4096, 128], [128 * 4096, 2], [1, 4096]]),
                  reads=[B("wvutb")], writes=[B("WVU")])
            p.dma("sync", W2S[:, 0:8],
                  bass.AP(wu2tb.tensor, 0, [[1, 128], [128, 8]]),
                  reads=[B("wu2tb")], writes=[B("W2S")], slow=True)

        p.fence()

        deferred_units = []
        for srcw, dstw, rows, cols in ((awq, wqb, 4096, 4096),
                                       (awk, wkb, 4096, 1024),
                                       (awv, wvb, 4096, 1024),
                                       (awo, wob, 4096, 4096)):
            for c0 in range(0, cols, 512):
                for r0 in range(0, rows, 128):
                    deferred_units.append((srcw, dstw, r0, c0))

        def emit_cast_unit(i, unit):
            srcw, dstw, r0, c0 = unit
            stage, stageb, sb = ((DCA, DCB, "dc0") if i % 2 == 0
                                 else (DCA2, DCB2, "dc1"))
            p.dma("scalar", stage[:, :], srcw[r0:r0 + 128, c0:c0 + 512],
                  writes=[B(sb)])
            p.emit("scalar", lambda s, _s=stage, _b=stageb: s.copy(
                _b[:, :], _s[:, :]), reads=[B(sb)], writes=[B(sb + "b")])
            p.dma("scalar", dstw[r0:r0 + 128, c0:c0 + 512], stageb[:, :],
                  reads=[B(sb + "b")],
                  writes=[B("%s_c%d" % (dstw.tensor.name, c0 // 512))])

        dc_state = [0]

        def cast_filler(n):
            i0 = dc_state[0]
            for j in range(i0, min(i0 + n, len(deferred_units))):
                emit_cast_unit(j, deferred_units[j])
            dc_state[0] = min(i0 + n, len(deferred_units))

        def cast_drain():
            cast_filler(len(deferred_units))

        LVACC = ROWD[0:1, 0:1]
        LVTMP = ROWD[0:1, 1:2]
        LVST = ROWD[0:1, 8:16]
        MIV = ROWD[0:1, 16:17]
        LVACC2 = ROWD[0:1, 24:25]
        LVTMP2 = ROWD[0:1, 25:26]
        LVST2 = ROWD[0:1, 32:40]
        BQ = ROWD[0:1, 40:41]
        BK = ROWD[0:1, 41:42]
        BB_ = ROWD[0:1, 42:43]

        def emit_ffn(xsrc, dst, rowbank, lvg, lvc, prepass, tag, filler=None):
            xsrc_b = B(xsrc.tensor.name)

            def fill(n):
                if filler is not None:
                    filler(n)

            with ExitStack() as fs:
                fe = fs.enter_context
                FA = fe(nc.psum_tensor(tag + "FA", [128, 512], F32))
                FB = fe(nc.psum_tensor(tag + "FB", [128, 512], F32))
                FC = fe(nc.psum_tensor(tag + "FC", [128, 1024], F32))
                FD = fe(nc.psum_tensor(tag + "FD", [128, 1024], F32))
                FE = fe(nc.psum_tensor(tag + "FE", [128, 512], F32))
                FT = fe(nc.psum_tensor(tag + "FT", [128, 1024], BF16))
                XT4 = fe(nc.sbuf_tensor(tag + "XT4", [128, 2048], F32))
                XT4B = fe(nc.sbuf_tensor(tag + "XT4B", [128, 2048], F32))
                XB = fe(nc.sbuf_tensor(tag + "XB", [128, 16384], BF16))
                SQ4 = fe(nc.sbuf_tensor(tag + "SQ4", [128, 2048], BF16))
                W1 = fe(nc.sbuf_tensor(tag + "W1", [128, 8192], BF16))
                PTS = fe(nc.sbuf_tensor(tag + "PTS", [128, 4096], BF16))
                PB = fe(nc.sbuf_tensor(tag + "PB", [128, 1024], BF16))
                QB = fe(nc.sbuf_tensor(tag + "QB", [64, 512], BF16))
                RQ = fe(nc.sbuf_tensor(tag + "RQ", [48, 512], BF16))
                RTS = fe(nc.sbuf_tensor(tag + "RTS", [128, 1024], F32))
                FSS = fe(nc.sbuf_tensor(tag + "FSS", [128, 1024], F32))
                SELS = fe(nc.sbuf_tensor(tag + "SELS", [128, 1024], F32))
                MSK8 = fe(nc.sbuf_tensor(tag + "MSK8", [128, 1024],
                                         mybir.dt.uint8))
                NEGB = fe(nc.sbuf_tensor(tag + "NEGB", [128, 2], F32))
                T8 = fe(nc.sbuf_tensor(tag + "T8", [128, 8], F32))
                NB2 = fe(nc.sbuf_tensor(tag + "NB2", [128, 2], F32))
                SRC_ = fe(nc.sbuf_tensor(tag + "SRC", [128, 2], F32))
                RB512 = fe(nc.sbuf_tensor(tag + "RB512", [128, 512], F32))
                GAB = fe(nc.sbuf_tensor(tag + "GAB", [128, 1536], F32))
                ASB = fe(nc.sbuf_tensor(tag + "ASB", [128, 1024], BF16))
                AXE = fe(nc.sbuf_tensor(tag + "AXE", [128, 512], BF16))
                UF = fe(nc.sbuf_tensor(tag + "UF", [128, 512], F32))
                UB = fe(nc.sbuf_tensor(tag + "UB", [128, 512], BF16))
                AX4 = fe(nc.sbuf_tensor(tag + "AX4", [128, 2048], BF16))
                CF1 = fe(nc.sbuf_tensor(tag + "CF1", [128, 2048], F32))
                CF2 = fe(nc.sbuf_tensor(tag + "CF2", [128, 2048], F32))

                p.emit("vector", lambda v: v.memset(NEGB[:, :], NEGBIG),
                       writes=[B(tag + "negb")])
                NEGBB = NEGB[:, 0:1].to_broadcast([128, 1024])

                def oct_ap(dram, tile, o):
                    return bass.AP(dram.tensor, (o * 4 * 128) * TOK + tile * 512,
                                   [[TOK, 128], [128 * TOK, 4], [1, 512]])

                def xbfd_tile_ap(tile):
                    return bass.AP(xbfd.tensor, tile * 512,
                                   [[TOK, 128], [128 * TOK, 32], [1, 512]])

                s_negmu = ROWA[0:1, 0:512]
                s_r = ROWA[0:1, 512:1024]
                s_t1 = ROWA[0:1, 1024:1536]
                s_t2 = ROWA[0:1, 1536:2048]
                s_v = ROWA[0:1, 2048:2560]
                s_sr = ROWA[0:1, 2560:3072]
                s_le = ROWA[0:1, 3072:3584]
                s_g = ROWA[0:1, 3584:4096]
                s_a = ROWB[0:1, 0:512]
                s_b2 = ROWB[0:1, 512:1024]
                s_lv = ROWB[0:1, 1024:1536]
                s_g2 = ROWB[0:1, 1536:2048]

                def ln_rows(tile, store_bank, lvacc, lvtmp):
                    tk = p.tick()
                    p.emit("vector", lambda v: v.tensor_scalar(
                        s_negmu, FA[0:1, :], -1.0 / 4096.0, None, ALU.mult),
                        reads=[B(tag + "FA")], writes=[B(tag + "negmu")], waits=tk)
                    p.emit("vector", lambda v: v.tensor_scalar(
                        s_t1, FB[0:1, :], 1.0 / 4096.0, None, ALU.mult),
                        reads=[B(tag + "FB")], writes=[B(tag + "t1")], waits=tk)
                    p.emit("vector", lambda v: v.tensor_mul(
                        s_t2, s_negmu, s_negmu),
                        reads=[B(tag + "negmu")], writes=[B(tag + "t2")])
                    p.emit("vector", lambda v: v.tensor_sub(s_v, s_t1, s_t2),
                           reads=[B(tag + "t1"), B(tag + "t2")],
                           writes=[B(tag + "v")])
                    p.emit("scalar", lambda s: s.activation(
                        s_t1, s_v, AF.Ln, bias=EPS),
                        reads=[B(tag + "v")], writes=[B(tag + "t1")])
                    p.emit("scalar", lambda s: s.activation(
                        s_r, s_t1, AF.Exp, scale=-0.5),
                        reads=[B(tag + "t1")], writes=[B(tag + "r")])
                    tsl = slice(tile * 512, (tile + 1) * 512)
                    p.dma("sync", rowsd[store_bank:store_bank + 1, tsl], s_negmu,
                          reads=[B(tag + "negmu")],
                          writes=[B("rw%d" % store_bank)])
                    p.dma("sync", rowsd[store_bank + 1:store_bank + 2, tsl], s_r,
                          reads=[B(tag + "r")],
                          writes=[B("rw%d" % (store_bank + 1))])
                    if lvacc is not None:
                        p.emit("vector", lambda v: v.tensor_mul(s_t2, s_r, s_r),
                               reads=[B(tag + "r")], writes=[B(tag + "t2")])
                        p.emit("vector", lambda v: v.tensor_mul(s_t1, s_v, s_t2),
                               reads=[B(tag + "v"), B(tag + "t2")],
                               writes=[B(tag + "t1")])
                        p.emit("scalar", lambda s: s.activation(
                            s_t2, s_t1, AF.Ln, bias=1.0),
                            reads=[B(tag + "t1")], writes=[B(tag + "t2")])
                        p.dma("sync", rowsd[store_bank + 7:store_bank + 8, tsl],
                              s_t2, reads=[B(tag + "t2")],
                              writes=[B("rw%d" % (store_bank + 7))])
                        p.emit("vector", lambda v: v.reduce_sum(
                            lvtmp, s_t2, axis=AX),
                            reads=[B(tag + "t2")], writes=[B(tag + "lvt")])
                        p.emit("vector", lambda v: v.tensor_add(
                            lvacc, lvacc, lvtmp),
                            reads=[B(tag + "lvt"), B(tag + "lvacc")],
                            writes=[B(tag + "lvacc")])

                def stats_chain2(first, src_sb, sq_sb, src_buf, sq_buf):
                    def mm(t, _f=first, _src=src_sb, _sq=sq_sb):
                        ins = None
                        for c8 in range(2):
                            st = (_f and c8 == 0)
                            sl = slice(c8 * 512, (c8 + 1) * 512)
                            ins = t.matmul(FA[0:1, :], ONESB[:, :], _src[:, sl],
                                           start=st, stop=False)
                            ins = t.matmul(FB[0:1, :], ONESB[:, :], _sq[:, sl],
                                           start=st, stop=False)
                        return ins
                    p.emit("tensor", mm,
                           reads=[src_buf, sq_buf, B("ONESB")],
                           writes=([B(tag + "FA"), B(tag + "FB")]
                                   if first else []))

                def stats_chain(first, src_sb, sq_sb, src_buf, nchunks=4):
                    def mm(t, _f=first, _src=src_sb, _sq=sq_sb, _n=nchunks):
                        ins = None
                        for c8 in range(_n):
                            st = (_f and c8 == 0)
                            sl = slice(c8 * 512, (c8 + 1) * 512)
                            ins = t.matmul(FA[0:1, :], ONESB[:, :], _src[:, sl],
                                           start=st, stop=False)
                            ins = t.matmul(FB[0:1, :], ONESB[:, :], _sq[:, sl],
                                           start=st, stop=False)
                        return ins
                    p.emit("tensor", mm,
                           reads=[src_buf, B(tag + "sq"), B("ONESB")],
                           writes=([B(tag + "FA"), B(tag + "FB")]
                                   if first else []))

                if prepass:
                    p.emit("vector", lambda v: v.memset(LVACC, 0.0),
                           writes=[B(tag + "lvacc")])
                    for tile in range(NT):
                        for o in range(8):
                            xt = XT4 if o % 2 == 0 else XT4B
                            xtb = B(tag + "xt4%d" % (o % 2))
                            xbsl = XB[:, o * 2048:(o + 1) * 2048]
                            p.dma("sync", xt[:, :], oct_ap(xsrc, tile, o),
                                  reads=[xsrc_b], writes=[xtb])
                            p.emit("vector", lambda v, _x=xbsl, _xt=xt:
                                   v.tensor_copy(_x, _xt[:, :]),
                                   reads=[xtb],
                                   writes=[B(tag + "xb")])
                            p.dma("sync", oct_ap(xbfd, tile, o), xbsl,
                                  reads=[B(tag + "xb")], writes=[B("xbfd")])
                            p.emit("gpsimd", lambda g, _x=xbsl: g.tensor_mul(
                                SQ4[:, :], _x, _x),
                                reads=[B(tag + "xb")], writes=[B(tag + "sq")])
                            stats_chain(o == 0, xbsl, SQ4, B(tag + "xb"))
                        fill(48)
                        ln_rows(tile, rowbank, LVACC, LVTMP)
                    p.emit("vector", lambda v: v.memset(LVST, 0.0),
                           writes=[B(tag + "lvst")])
                    p.emit("vector", lambda v: v.tensor_copy(
                        ROWD[0:1, 8:9], LVACC),
                        reads=[B(tag + "lvacc")], writes=[B(tag + "lvst")])
                    p.dma("sync", lvc, LVST, reads=[B(tag + "lvst")],
                          writes=[B("lvc" + tag)])

                    def coll_lv(g):
                        return g.collective_compute(
                            "AllReduce", ALU.add, replica_groups=groups_all,
                            ins=[lvc.opt()], outs=[lvg.opt()])
                    p.emit("gpsimd", coll_lv, reads=[B("lvc" + tag)],
                           writes=[B("lvg" + tag)], semkey="gpsimdD")

                donext = dst is not y_out
                for tile in range(NT):
                    tsl = slice(tile * 512, (tile + 1) * 512)
                    p.dma("sync", bass.AP(XB, 0,
                                          [[16384, 128], [512, 32], [1, 512]]),
                          xbfd_tile_ap(tile), reads=[B("xbfd")],
                          writes=[B(tag + "xbt")])
                    p.dma("sync", s_negmu, rowsd[rowbank:rowbank + 1, tsl],
                          reads=[B("rw%d" % rowbank)], writes=[B(tag + "negmu")])
                    p.dma("sync", s_r, rowsd[rowbank + 1:rowbank + 2, tsl],
                          reads=[B("rw%d" % (rowbank + 1))],
                          writes=[B(tag + "r")])
                    p.emit("tensor", lambda t: t.matmul(
                        FE[:, :], ONESF[:, :], s_r, start=True, stop=True),
                        reads=[B(tag + "r"), B("ONESF")], writes=[B(tag + "FE")])
                    p.emit("scalar", lambda s: s.copy(RB512[:, :], FE[:, :]),
                           reads=[B(tag + "FE")], writes=[B(tag + "rb")],
                           waits=p.tick())

                    def mmq(t):
                        ins = None
                        for kc in range(32):
                            ins = t.matmul(FA[0:64, :],
                                           FWQ[:, kc * 64:(kc + 1) * 64],
                                           XB[:, kc * 512:(kc + 1) * 512],
                                           start=(kc == 0), stop=False)
                        ins = t.matmul(FA[0:64, :], CSF[0:1, 0:64], s_negmu,
                                       start=False, stop=True)
                        return ins
                    p.emit("tensor", mmq,
                           reads=[B(tag + "xbt"), B("FWQ"), B("csf"),
                                  B(tag + "negmu")],
                           writes=[B(tag + "FA")])
                    p.emit("vector", lambda v: v.tensor_mul(
                        QB[:, :], FA[0:64, :], RB512[0:64, :]),
                        reads=[B(tag + "FA"), B(tag + "rb")],
                        writes=[B(tag + "qb")], waits=p.tick())

                    p.emit("tensor", lambda t: t.matmul(
                        FB[0:48, :], FWR[:, :], QB[:, :], start=True, stop=True),
                        reads=[B(tag + "qb"), B("FWR")], writes=[B(tag + "FB")])
                    p.emit("scalar", lambda s: s.copy(RQ[:, :], FB[0:48, :]),
                           reads=[B(tag + "FB")], writes=[B(tag + "rq")],
                           waits=p.tick())
                    fill(48)

                    for sc in range(4):
                        csl = slice(sc * 128, (sc + 1) * 128)

                        def mmrout(t, _c=csl):
                            i = t.matmul(FC[:, 0:512], RQ[:, _c], RKBF[:, 0:512],
                                         start=True, stop=False)
                            i = t.matmul(FC[:, 0:512], ONESF[:, :],
                                         LR[0:1, 0:512], start=False, stop=True)
                            i = t.matmul(FC[:, 512:1024], RQ[:, _c],
                                         RKBF[:, 512:1024], start=True,
                                         stop=False)
                            i = t.matmul(FC[:, 512:1024], ONESF[:, :],
                                         LR[0:1, 512:1024], start=False,
                                         stop=True)
                            return i
                        p.emit("tensor", mmrout,
                               reads=[B(tag + "rq"), B("RKBF"), B("LRR"),
                                      B("ONESF")],
                               writes=[B(tag + "FC")])

                        def mmf(t, _c=csl):
                            i = t.matmul(FD[:, 0:512], QB[:, _c], AKT[:, 0:512],
                                         start=True, stop=False)
                            i = t.matmul(FD[:, 0:512], ONESF[:, :],
                                         LR[0:1, 0:512], start=False, stop=True)
                            i = t.matmul(FD[:, 512:1024], QB[:, _c],
                                         AKT[:, 512:1024], start=True,
                                         stop=False)
                            i = t.matmul(FD[:, 512:1024], ONESF[:, :],
                                         LR[0:1, 512:1024], start=False,
                                         stop=True)
                            return i
                        p.emit("tensor", mmf,
                               reads=[B(tag + "qb"), B("AKT"), B("LRR"),
                                      B("ONESF")],
                               writes=[B(tag + "FD")])

                        p.emit("scalar", lambda s: s.copy(RTS[:, :], FC[:, :]),
                               reads=[B(tag + "FC")], writes=[B(tag + "rts")],
                               waits=p.tick())
                        p.emit("scalar", lambda s: s.copy(FSS[:, :], FD[:, :]),
                               reads=[B(tag + "FD")], writes=[B(tag + "fss")],
                               waits=p.tick())
                        p.emit("vector", lambda v: v.max(out=T8[:, :],
                                                         in_=RTS[:, :]),
                               reads=[B(tag + "rts")], writes=[B(tag + "t8")])
                        p.emit("vector", lambda v: v.reduce_max(
                            NB2[:, 0:1], FSS[:, :], axis=AX, negate=True),
                            reads=[B(tag + "fss")], writes=[B(tag + "nb2")])
                        p.emit("vector", lambda v: v.tensor_scalar(
                            MSK8[:, :], RTS[:, :], T8[:, 7:8], None, ALU.is_ge),
                            reads=[B(tag + "t8"), B(tag + "rts")],
                            writes=[B(tag + "msk8")])
                        p.emit("vector", lambda v: v.select(
                            SELS[:, :], MSK8[:, :], FSS[:, :], NEGBB),
                            reads=[B(tag + "msk8"), B(tag + "fss"),
                                   B(tag + "negb")],
                            writes=[B(tag + "sels")])
                        p.emit("scalar", lambda s: s.activation(
                            PB[:, :], SELS[:, :], AF.Exp, bias=NB2[:, 0:1],
                            accum_out=SRC_[:, 0:1]),
                            reads=[B(tag + "sels"), B(tag + "nb2")],
                            writes=[B(tag + "pb"), B(tag + "srcol")])
                        p.emit("tensor", lambda t: t.transpose(
                            FC[0:1, 0:128], SRC_[:, 0:1], IDF),
                            reads=[B(tag + "srcol"), B("IDF")],
                            writes=[B(tag + "FC")])
                        p.emit("scalar", lambda s, _sc=sc: s.copy(
                            s_sr[0:1, _sc * 128:(_sc + 1) * 128],
                            FC[0:1, 0:128]),
                            reads=[B(tag + "FC")], writes=[B(tag + "srrow")],
                            waits=p.tick())

                        def mmtr(t):
                            ins = None
                            for j in range(8):
                                ins = t.transpose(FT[:, j * 128:(j + 1) * 128],
                                                  PB[:, j * 128:(j + 1) * 128],
                                                  IDB)
                            return ins
                        p.emit("tensor", mmtr, reads=[B(tag + "pb"), B("IDB")],
                               writes=[B(tag + "FT")])
                        p.emit("scalar", lambda s, _sc=sc: s.copy(
                            bass.AP(PTS, _sc * 128,
                                    [[4096, 128], [512, 8], [1, 128]]),
                            FT[:, :]),
                            reads=[B(tag + "FT")],
                            writes=[B(tag + "pts%d" % sc)],
                            waits=p.tick())

                    pts_reads = [B(tag + "pts%d" % s4) for s4 in range(4)]
                    for vc in range(2):
                        tgt, tgn = ((FA, tag + "FA") if vc == 0
                                    else (FB, tag + "FB"))

                        def mmas(t, _vc=vc, _tg=tgt):
                            ins = None
                            for n8 in range(8):
                                ins = t.matmul(
                                    _tg[:, :],
                                    APR[:, n8 * 256 + _vc * 128:
                                        n8 * 256 + _vc * 128 + 128],
                                    PTS[:, n8 * 512:(n8 + 1) * 512],
                                    start=(n8 == 0), stop=(n8 == 7))
                            return ins
                        p.emit("tensor", mmas, reads=pts_reads + [B("APR")],
                               writes=[B(tgn)])
                        p.emit("scalar", lambda s, _vc=vc, _tg=tgt: s.copy(
                            ASB[:, _vc * 512:(_vc + 1) * 512], _tg[:, :]),
                            reads=[B(tgn)], writes=[B(tag + "asb")],
                            waits=p.tick())

                    for cc in range(32):
                        tgt, tgn = ((FC, tag + "FC") if cc % 2 == 0
                                    else (FD, tag + "FD"))

                        def mmau(t, _cc=cc, _tg=tgt):
                            i = t.matmul(_tg[:, 0:512],
                                         WVU[:, _cc * 128:_cc * 128 + 128],
                                         ASB[:, 0:512], start=True, stop=False)
                            i = t.matmul(_tg[:, 0:512],
                                         WVU[:, 4096 + _cc * 128:
                                             4096 + _cc * 128 + 128],
                                         ASB[:, 512:1024], start=False,
                                         stop=True)
                            return i
                        p.emit("tensor", mmau, reads=[B(tag + "asb"), B("WVU")],
                               writes=[B(tgn)])
                        p.emit("scalar", lambda s, _tg=tgt: s.copy(
                            AXE[:, :], _tg[:, 0:512]),
                            reads=[B(tgn)], writes=[B(tag + "axe")],
                            waits=p.tick())
                        p.dma("sync",
                              auxd[cc * 128:(cc + 1) * 128, tsl],
                              AXE[:, :], reads=[B(tag + "axe")],
                              writes=[B("auxd")])
                    fill(48)

                    for uc in range(8):
                        ws = uc % 2
                        wb_ = B(tag + "w1%d" % ws)
                        p.dma("scalar",
                              bass.AP(W1, ws * 4096,
                                      [[8192, 128], [128, 32], [1, 128]]),
                              bass.AP(wu1tb.tensor, uc * 128,
                                      [[1024, 128], [128 * 1024, 32], [1, 128]]),
                              reads=[B("wu1tb")], writes=[wb_])
                        tgt, tgn = ((FA, tag + "FA") if uc % 2 == 0
                                    else (FB, tag + "FB"))

                        def mmu(t, _uc=uc, _tg=tgt, _ws=ws):
                            ins = None
                            for kc in range(32):
                                ins = t.matmul(
                                    _tg[:, :],
                                    W1[:, _ws * 4096 + kc * 128:
                                       _ws * 4096 + kc * 128 + 128],
                                    XB[:, kc * 512:(kc + 1) * 512],
                                    start=(kc == 0), stop=False)
                            ins = t.matmul(_tg[:, :],
                                           CSF[0:1, 64 + _uc * 128:
                                               64 + (_uc + 1) * 128],
                                           s_negmu, start=False, stop=True)
                            return ins
                        p.emit("tensor", mmu,
                               reads=[wb_, B(tag + "xbt"), B("csf"),
                                      B(tag + "negmu")],
                               writes=[B(tgn)])
                        p.emit("vector", lambda v, _tg=tgt: v.tensor_mul(
                            UF[:, :], _tg[:, :], RB512[:, :]),
                            reads=[B(tgn), B(tag + "rb")],
                            writes=[B(tag + "uf")], waits=p.tick())
                        p.emit("scalar", lambda s, _uc=uc: s.activation(
                            UB[:, :], UF[:, :], AF.Gelu_apprx_tanh,
                            bias=BO2[:, 32 + _uc:33 + _uc]),
                            reads=[B(tag + "uf"), B("BU1")],
                            writes=[B(tag + "ub")])
                        p.emit("tensor", lambda t, _uc=uc: t.matmul(
                            FE[0:1, :], W2S[:, _uc:_uc + 1], UB[:, :],
                            start=(_uc == 0), stop=(_uc == 7)),
                            reads=[B(tag + "ub"), B("W2S")],
                            writes=([B(tag + "FE")] if uc == 0 else []))
                    p.emit("vector", lambda v: v.tensor_scalar(
                        s_le, FE[0:1, :], SCAL[0:1, 0:1], None, ALU.add),
                        reads=[B(tag + "FE"), B("SCAL")],
                        writes=[B(tag + "le")], waits=p.tick())
                    fill(48)

                    if tile == 0:
                        p.dma("sync", LVST, lvg, reads=[B("lvg" + tag)],
                              writes=[B(tag + "lvst")])
                        p.emit("vector", lambda v: v.tensor_scalar(
                            MIV, ROWD[0:1, 8:9], 1.0 / NTOK_ALL, 1e-6,
                            ALU.mult, ALU.add),
                            reads=[B(tag + "lvst")], writes=[B(tag + "miv")])
                        p.emit("scalar", lambda s: s.activation(
                            MIV, MIV, AF.Ln),
                            reads=[B(tag + "miv")], writes=[B(tag + "miv")])
                        p.emit("scalar", lambda s: s.activation(
                            MIV, MIV, AF.Exp, scale=-1.0),
                            reads=[B(tag + "miv")], writes=[B(tag + "miv")])
                    p.emit("scalar", lambda s: s.activation(
                        s_g, s_le, AF.Tanh, scale=0.5),
                        reads=[B(tag + "le")], writes=[B(tag + "g")])
                    p.emit("vector", lambda v: v.tensor_scalar(
                        s_g, s_g, 0.5, 0.5, ALU.mult, ALU.add),
                        reads=[B(tag + "g")], writes=[B(tag + "g")])
                    p.dma("sync", s_lv, rowsd[rowbank + 7:rowbank + 8, tsl],
                          reads=[B("rw%d" % (rowbank + 7))],
                          writes=[B(tag + "lvr")])
                    p.emit("vector", lambda v: v.tensor_scalar(
                        s_lv, s_lv, MIV[0:1, 0:1], 0.5, ALU.mult, ALU.mult),
                        reads=[B(tag + "lvr"), B(tag + "miv")],
                        writes=[B(tag + "lvr")])
                    p.emit("vector", lambda v: v.scalar_tensor_tensor(
                        s_g, s_g, 2.5, s_lv, ALU.mult, ALU.add),
                        reads=[B(tag + "g"), B(tag + "lvr")],
                        writes=[B(tag + "g")])
                    p.emit("vector", lambda v: v.tensor_scalar(
                        s_g, s_g, 0.0, 5.0, ALU.max, ALU.min),
                        reads=[B(tag + "g")], writes=[B(tag + "g")])
                    p.emit("vector", lambda v: v.tensor_scalar(
                        s_g, s_g, -0.5, 1.0 / 1.5, ALU.add, ALU.mult),
                        reads=[B(tag + "g")], writes=[B(tag + "g")])
                    p.emit("vector", lambda v: v.tensor_scalar(
                        s_g, s_g, 0.0, 1.0, ALU.max, ALU.min),
                        reads=[B(tag + "g")], writes=[B(tag + "g")])
                    p.emit("vector", lambda v: v.tensor_scalar(
                        s_g, s_g, SCAL[0:1, 1:2], SCAL[0:1, 2:3],
                        ALU.mult, ALU.add),
                        reads=[B(tag + "g"), B("SCAL")], writes=[B(tag + "g")])
                    p.emit("scalar", lambda s: s.activation(
                        s_g, s_g, AF.Tanh, scale=0.5),
                        reads=[B(tag + "g")], writes=[B(tag + "g")])
                    p.emit("vector", lambda v: v.tensor_scalar(
                        s_g, s_g, 0.5, 0.5, ALU.mult, ALU.add),
                        reads=[B(tag + "g")], writes=[B(tag + "g")])
                    p.emit("vector", lambda v: v.tensor_scalar(
                        s_g2, s_g, 0.05, None, ALU.is_ge),
                        reads=[B(tag + "g")], writes=[B(tag + "g2")])
                    p.emit("vector", lambda v: v.tensor_mul(s_g, s_g, s_g2),
                           reads=[B(tag + "g2")], writes=[B(tag + "g")])
                    p.emit("scalar", lambda s: s.activation(s_g2, s_sr, AF.Ln),
                           reads=[B(tag + "srrow")], writes=[B(tag + "g2")])
                    p.emit("scalar", lambda s: s.activation(
                        s_g2, s_g2, AF.Exp, scale=-1.0),
                        reads=[B(tag + "g2")], writes=[B(tag + "g2")])
                    p.emit("vector", lambda v: v.scalar_tensor_tensor(
                        s_g, s_g, 0.5, s_g2, ALU.mult, ALU.mult),
                        reads=[B(tag + "g2"), B(tag + "g")],
                        writes=[B(tag + "g")])
                    p.emit("vector", lambda v: v.scalar_tensor_tensor(
                        s_b2, s_negmu, 0.5, s_r, ALU.mult, ALU.mult),
                        reads=[B(tag + "negmu"), B(tag + "r")],
                        writes=[B(tag + "b2")])
                    p.emit("vector", lambda v: v.tensor_scalar(
                        s_a, s_r, 0.5, 1.0, ALU.mult, ALU.add),
                        reads=[B(tag + "r")], writes=[B(tag + "a")])

                    for i, (nm, row) in enumerate((("gb", s_g), ("ab", s_a),
                                                   ("bb", s_b2))):
                        rb_ = B(tag + ("g" if i == 0 else ("a" if i == 1
                                                           else "b2")))
                        p.emit("tensor", lambda t, _r=row: t.matmul(
                            FE[:, :], ONESF[:, :], _r, start=True, stop=True),
                            reads=[rb_, B("ONESF")], writes=[B(tag + "FE")])
                        p.emit("scalar", lambda s, _i=i: s.copy(
                            GAB[:, _i * 512:(_i + 1) * 512], FE[:, :]),
                            reads=[B(tag + "FE")], writes=[B(tag + nm)],
                            waits=p.tick())

                    def b3(i, w):
                        return bass.AP(GAB, i * 512,
                                       [[1536, 128], [0, w // 512], [1, 512]])

                    def hex_ap(dram, tile, h):
                        return bass.AP(dram.tensor,
                                       (h * 2 * 128) * TOK + tile * 512,
                                       [[TOK, 128], [128 * TOK, 2], [1, 512]])

                    for h in range(16):
                        sl2 = h % 2
                        xt = (XT4 if sl2 == 0 else XT4B)[:, 0:1024]
                        ax = AX4[:, sl2 * 1024:(sl2 + 1) * 1024]
                        cf1 = CF1[:, sl2 * 1024:(sl2 + 1) * 1024]
                        cf2 = CF2[:, sl2 * 1024:(sl2 + 1) * 1024]
                        xtb = B(tag + "xt4%d" % sl2)
                        axb = B(tag + "ax4%d" % sl2)
                        c1b = B(tag + "cf1%d" % sl2)
                        c2b = B(tag + "cf2%d" % sl2)
                        p.dma("sync", xt, hex_ap(xsrc, tile, h),
                              reads=[xsrc_b], writes=[xtb])
                        p.dma("sync", ax, hex_ap(auxd, tile, h),
                              reads=[B("auxd")], writes=[axb])
                        p.emit("vector", lambda v, _a=ax, _c=cf1: v.tensor_tensor(
                            _c, _a, b3(0, 1024), ALU.mult),
                            reads=[axb, B(tag + "gb")], writes=[c1b])
                        p.emit("vector", lambda v, _x=xt, _c=cf2: v.tensor_tensor(
                            _c, _x, b3(1, 1024), ALU.mult),
                            reads=[xtb, B(tag + "ab")], writes=[c2b])
                        p.emit("vector", lambda v, _c1=cf1, _c2=cf2: v.tensor_add(
                            _c1, _c1, _c2),
                            reads=[c2b, c1b], writes=[c1b])
                        p.emit("vector", lambda v, _c1=cf1, _c2=cf2: v.tensor_tensor(
                            _c2, _c1, b3(2, 1024), ALU.add),
                            reads=[c1b, B(tag + "bb")], writes=[c2b])
                        p.dma("sync", hex_ap(dst, tile, h), cf2,
                              reads=[c2b], writes=[B(dst.tensor.name)])
                        if donext:
                            p.emit("gpsimd", lambda g, _a=ax, _c=cf2:
                                   g.tensor_copy(_a, _c),
                                   reads=[c2b], writes=[axb])
                            p.dma("sync", hex_ap(xbfd, tile, h), ax,
                                  reads=[axb], writes=[B("xbfd")])
                            p.emit("gpsimd", lambda g, _a=ax, _s=sl2:
                                   g.tensor_mul(
                                       SQ4[:, _s * 1024:(_s + 1) * 1024],
                                       _a, _a),
                                   reads=[axb], writes=[B(tag + "sq%d" % sl2)])
                            stats_chain2(h == 0, ax,
                                         SQ4[:, sl2 * 1024:(sl2 + 1) * 1024],
                                         axb, B(tag + "sq%d" % sl2))
                    if donext:
                        ln_rows(tile, 16, None, None)
                    fill(48)

        emit_ffn(x_in, x1d, rowbank=0, lvg=lv_g, lvc=lv_c, prepass=True,
                 tag="f1", filler=cast_filler)
        cast_drain()
        p.fence()

        with ExitStack() as s2:
            e2 = s2.enter_context
            QA = e2(nc.psum_tensor("QA", [128, 512], F32))
            QB_ = e2(nc.psum_tensor("QBp", [128, 512], F32))
            QE = e2(nc.psum_tensor("QE", [128, 512], F32))
            QS0 = e2(nc.psum_tensor("QS0", [128, 512], F32))
            QS1 = e2(nc.psum_tensor("QS1", [128, 512], F32))
            XBH = e2(nc.sbuf_tensor("XBH", [128, 32768], BF16))
            WQB = e2(nc.sbuf_tensor("WQBs", [128, 8192], BF16))
            WVH = e2(nc.sbuf_tensor("WVH", [128, 8192], BF16))
            QF = e2(nc.sbuf_tensor("QF", [128, 1024], F32))
            QSW = e2(nc.sbuf_tensor("QSW", [128, 1024], F32))
            QRO = e2(nc.sbuf_tensor("QRO", [128, 1024], BF16))
            QSQ = e2(nc.sbuf_tensor("QSQ", [128, 1024], BF16))
            TBC = e2(nc.sbuf_tensor("TBC", [128, 1024], F32))
            TBS = e2(nc.sbuf_tensor("TBS", [128, 1024], F32))
            R2B = e2(nc.sbuf_tensor("R2B", [128, 1024], F32))
            R2C = e2(nc.sbuf_tensor("R2C", [128, 8], F32))
            CSL = e2(nc.sbuf_tensor("CSL", [1, 128], F32))
            CSLV = e2(nc.sbuf_tensor("CSLV", [1, 256], F32))
            VBE = e2(nc.sbuf_tensor("VBE", [128, 256], BF16))
            QMX = e2(nc.sbuf_tensor("QMX", [128, 16], F32))
            STR = e2(nc.sbuf_tensor("STR", [1, 128], F32))
            ONEQK = e2(nc.sbuf_tensor("ONEQK", [128, 1280], BF16))
            ONEQ = ONEQK[:, 0:1024]
            ONEK = ONEQK[:, 1024:1280]
            p.dma("sync", ONEQK[:, 0:1024], oneq, writes=[B("ONEQ")])
            p.dma("sync", ONEQK[:, 1024:1280], onek, writes=[B("ONEK")])

            s2_negmu = ROWA[0:1, 0:1024]
            s2_r = ROWA[0:1, 1024:2048]

            p.emit("vector", lambda v: v.memset(QMX[:, 0:4], 0.0),
                   writes=[B("qmx")])

            for half in range(NH):
                hsl = slice(half * 1024, (half + 1) * 1024)
                p.dma("sync",
                      bass.AP(XBH, 0,
                              [[32768, 128], [1024, 32], [1, 1024]]),
                      bass.AP(xbfd.tensor, half * 1024,
                              [[TOK, 128], [128 * TOK, 32], [1, 1024]]),
                      reads=[B("xbfd")], writes=[B("xbh")])
                p.dma("sync", s2_negmu, rowsd[16:17, hsl],
                      reads=[B("rw16")], writes=[B("s2negmu")])
                p.dma("sync", s2_r, rowsd[17:18, hsl],
                      reads=[B("rw17")], writes=[B("s2r")])
                for c2 in range(2):
                    p.emit("tensor", lambda t, _c=c2: t.matmul(
                        QE[:, :], ONESF[:, :],
                        s2_r[0:1, _c * 512:(_c + 1) * 512],
                        start=True, stop=True),
                        reads=[B("s2r"), B("ONESF")], writes=[B("QE")])
                    p.emit("scalar", lambda s, _c=c2: s.copy(
                        R2B[:, _c * 512:(_c + 1) * 512], QE[:, :]),
                        reads=[B("QE")], writes=[B("r2b")], waits=p.tick())
                for tc in range(8):
                    p.emit("tensor", lambda t, _tc=tc: t.transpose(
                        QE[:, 0:1], s2_r[0:1, _tc * 128:(_tc + 1) * 128],
                        IDF[0:1, 0:1]),
                        reads=[B("s2r"), B("IDF")], writes=[B("QE")])
                    p.emit("scalar", lambda s, _tc=tc: s.copy(
                        R2C[:, _tc:_tc + 1], QE[:, 0:1]),
                        reads=[B("QE")], writes=[B("r2c")], waits=p.tick())
                p.dma("sync", TBC[:, :], tabc[:, hsl], writes=[B("tbc")])
                p.dma("sync", TBS[:, :], tabs[:, hsl], writes=[B("tbs")])
                p.emit("vector", lambda v: v.tensor_mul(
                    TBC[:, :], TBC[:, :], R2B[:, :]),
                    reads=[B("r2b"), B("tbc")], writes=[B("tbc")])
                p.emit("vector", lambda v: v.tensor_mul(
                    TBS[:, :], TBS[:, :], R2B[:, :]),
                    reads=[B("r2b"), B("tbs")], writes=[B("tbs")])

                for kind, nocs, wsrc, csrow, dst_c, onh, onb, qsb in (
                        ("q", 32, wqb, 0, qc_d, ONEQ, "ONEQ", QS0),
                        ("k", 8, wkb, 1, kc_d, ONEK, "ONEK", QS1)):
                    qsn = qsb.name
                    for oc in range(nocs):
                        ws = oc % 2
                        wb_ = B("wqb%d" % ws)
                        p.dma("scalar",
                              bass.AP(WQB, ws * 4096,
                                      [[8192, 128], [128, 32], [1, 128]]),
                              bass.AP(wsrc.tensor, oc * 128,
                                      [[wsrc.tensor.shape[1], 128],
                                       [128 * wsrc.tensor.shape[1], 32],
                                       [1, 128]]),
                              reads=[B("%s_c%d" % (wsrc.tensor.name,
                                                    oc * 128 // 512))],
                              writes=[wb_])

                        def mmcs(t, _ws=ws):
                            ins = None
                            for kc in range(32):
                                ins = t.matmul(
                                    QE[0:1, 0:128], ONESB[:, :],
                                    WQB[:, _ws * 4096 + kc * 128:
                                        _ws * 4096 + kc * 128 + 128],
                                    start=(kc == 0), stop=(kc == 31))
                            return ins
                        p.emit("tensor", mmcs, reads=[wb_, B("ONESB")],
                               writes=[B("QE")])
                        p.emit("scalar", lambda s: s.copy(CSL[0:1, :],
                                                          QE[0:1, 0:128]),
                               reads=[B("QE")], writes=[B("csl")],
                               waits=p.tick())
                        for tcol in range(2):
                            tgt, tgn = (QA, "QA") if tcol == 0 else (QB_, "QBp")

                            def mmp(t, _tg=tgt, _ws=ws, _tc=tcol):
                                ins = None
                                for kc in range(32):
                                    ins = t.matmul(
                                        _tg[:, :],
                                        WQB[:, _ws * 4096 + kc * 128:
                                            _ws * 4096 + kc * 128 + 128],
                                        XBH[:, kc * 1024 + _tc * 512:
                                            kc * 1024 + _tc * 512 + 512],
                                        start=(kc == 0), stop=False)
                                ins = t.matmul(
                                    _tg[:, :], CSL[0:1, :],
                                    s2_negmu[0:1, _tc * 512:(_tc + 1) * 512],
                                    start=False, stop=True)
                                return ins
                            p.emit("tensor", mmp,
                                   reads=[wb_, B("xbh"), B("csl"), B("s2negmu")],
                                   writes=[B(tgn)])
                            p.emit("scalar", lambda s, _tg=tgt, _tc=tcol: s.copy(
                                QF[:, _tc * 512:(_tc + 1) * 512], _tg[:, :]),
                                reads=[B(tgn)], writes=[B("qf")],
                                waits=p.tick())
                        p.emit("gpsimd", lambda g: g.tensor_copy(
                            QSW[0:64, :], QF[64:128, :]),
                            reads=[B("qf")], writes=[B("qsw")])
                        p.emit("gpsimd", lambda g: g.tensor_copy(
                            QSW[64:128, :], QF[0:64, :]),
                            reads=[B("qf")], writes=[B("qsw")])
                        p.emit("vector", lambda v: v.tensor_mul(
                            QF[:, :], QF[:, :], TBC[:, :]),
                            reads=[B("tbc"), B("qsw"), B("qf")],
                            writes=[B("qf")])
                        p.emit("vector", lambda v: v.tensor_mul(
                            QSW[:, :], QSW[:, :], TBS[:, :]),
                            reads=[B("tbs"), B("qsw")], writes=[B("qsw")])
                        p.emit("vector", lambda v: v.tensor_add(
                            QRO[:, :], QF[:, :], QSW[:, :]),
                            reads=[B("qf"), B("qsw")], writes=[B("qro")])
                        p.emit("gpsimd", lambda g: g.tensor_mul(
                            QSQ[:, :], QRO[:, :], QRO[:, :]),
                            reads=[B("qro")], writes=[B("qsq")])
                        for tcol in range(2):
                            p.emit("tensor", lambda t, _oc=oc, _tc=tcol,
                                   _on=onh, _qs=qsb: t.matmul(
                                       _qs[0:32, :] if _tc == 0 else _qs[32:64, :],
                                       _on[:, _oc * 32:(_oc + 1) * 32],
                                       QSQ[:, _tc * 512:(_tc + 1) * 512],
                                       start=(_oc == 0), stop=False,
                                       tile_position=((0, 0) if _tc == 0
                                                      else (0, 32))),
                                   reads=[B("qsq"), B(onb)],
                                   writes=([B(qsn)] if (oc == 0 and tcol == 0)
                                           else []))
                        p.dma("sync", dst_c[oc * 128:(oc + 1) * 128, hsl],
                              QRO[:, :], reads=[B("qro")],
                              writes=[B(dst_c.tensor.name)])

                    col0 = 0 if kind == "q" else 1
                    tk = p.tick()
                    p.emit("vector", lambda v, _qs=qsb, _c=col0: v.reduce_max(
                        QMX[0:32, 8 + _c:9 + _c], _qs[0:32, :], axis=AX),
                        reads=[B(qsn)], writes=[B("qmxr")], waits=tk)
                    p.emit("vector", lambda v, _qs=qsb, _c=col0: v.reduce_max(
                        QMX[32:64, 8 + _c:9 + _c], _qs[32:64, :], axis=AX),
                        reads=[B(qsn)], writes=[B("qmxr")], waits=tk)
                    p.emit("vector", lambda v, _c=col0: v.tensor_copy(
                        QMX[0:32, 10 + _c:11 + _c], QMX[32:64, 8 + _c:9 + _c]),
                        reads=[B("qmxr")], writes=[B("qmxr")])
                    p.emit("vector", lambda v, _c=col0: v.tensor_max(
                        QMX[0:32, 12 + _c:13 + _c], QMX[0:32, 8 + _c:9 + _c],
                        QMX[0:32, 10 + _c:11 + _c]),
                        reads=[B("qmxr")], writes=[B("qmxr")])
                    p.emit("vector", lambda v, _c=col0: v.tensor_max(
                        QMX[0:32, _c:_c + 1], QMX[0:32, _c:_c + 1],
                        QMX[0:32, 12 + _c:13 + _c]),
                        reads=[B("qmxr"), B("qmx")], writes=[B("qmx")])

                for dh in range(4):
                    p.dma("scalar",
                          bass.AP(WVH, 0,
                                  [[8192, 128], [256, 32], [1, 256]]),
                          bass.AP(wvb.tensor, dh * 256,
                                  [[1024, 128], [128 * 1024, 32], [1, 256]]),
                          reads=[B("wvb_c%d" % (dh * 256 // 512))],
                          writes=[B("wvh")])

                    def mmcsv(t):
                        ins = None
                        for kc in range(32):
                            ins = t.matmul(QE[0:1, 0:256], ONESB[:, :],
                                           WVH[:, kc * 256:(kc + 1) * 256],
                                           start=(kc == 0), stop=(kc == 31))
                        return ins
                    p.emit("tensor", mmcsv, reads=[B("wvh"), B("ONESB")],
                           writes=[B("QE")])
                    p.emit("scalar", lambda s: s.copy(CSLV[0:1, :],
                                                      QE[0:1, 0:256]),
                           reads=[B("QE")], writes=[B("cslv")],
                           waits=p.tick())
                    for tc in range(8):
                        tgt, tgn = (QA, "QA") if tc % 2 == 0 else (QB_, "QBp")

                        def mmv(t, _tg=tgt, _tc=tc):
                            ins = None
                            for kc in range(32):
                                ins = t.matmul(
                                    _tg[:, 0:256],
                                    XBH[:, kc * 1024 + _tc * 128:
                                        kc * 1024 + _tc * 128 + 128],
                                    WVH[:, kc * 256:(kc + 1) * 256],
                                    start=(kc == 0), stop=False)
                            ins = t.matmul(
                                _tg[:, 0:256],
                                s2_negmu[0:1, _tc * 128:(_tc + 1) * 128],
                                CSLV[0:1, :], start=False, stop=True)
                            return ins
                        p.emit("tensor", mmv,
                               reads=[B("wvh"), B("xbh"), B("cslv"),
                                      B("s2negmu")],
                               writes=[B(tgn)])
                        p.emit("vector", lambda v, _tg=tgt, _tc=tc:
                               v.tensor_scalar(
                                   VBE[:, :], _tg[:, 0:256],
                                   R2C[:, _tc:_tc + 1], None, ALU.mult),
                               reads=[B(tgn), B("r2c")], writes=[B("vbe")],
                               waits=p.tick())
                        p.dma("sync",
                              vc_d[half * 1024 + tc * 128:
                                   half * 1024 + (tc + 1) * 128,
                                   dh * 256:(dh + 1) * 256],
                              VBE[:, :], reads=[B("vbe")], writes=[B("vc_d")])

            p.emit("tensor", lambda t: t.transpose(
                QE[0:1, 0:32], QMX[0:32, 0:1], IDF[0:32, 0:32]),
                reads=[B("qmx"), B("IDF")], writes=[B("QE")])
            p.emit("scalar", lambda s: s.copy(STR[0:1, 0:32], QE[0:1, 0:32]),
                   reads=[B("QE")], writes=[B("str")], waits=p.tick())
            p.emit("tensor", lambda t: t.transpose(
                QE[0:1, 0:32], QMX[0:32, 1:2], IDF[0:32, 0:32]),
                reads=[B("qmx"), B("IDF"), B("str")], writes=[B("QE")])
            p.emit("scalar", lambda s: s.copy(STR[0:1, 32:64], QE[0:1, 0:32]),
                   reads=[B("QE")], writes=[B("str")], waits=p.tick())
            p.dma("sync", st_c, STR[0:1, 0:64], reads=[B("str")],
                  writes=[B("st_c")])

            def coll_st(g):
                return g.collective_compute(
                    "AllReduce", ALU.max, replica_groups=groups_all,
                    ins=[st_c.opt()], outs=[st_g.opt()])
            p.emit("gpsimd", coll_st, reads=[B("st_c")], writes=[B("st_g")], semkey="gpsimdD")
            p.dma("sync", STR[0:1, 0:64], st_g, reads=[B("st_g")],
                  writes=[B("str")])
            p.emit("vector", lambda v: v.reduce_max(
                BQ, STR[0:1, 0:32], axis=AX),
                reads=[B("str")], writes=[B("bq")])
            p.emit("vector", lambda v: v.reduce_max(
                BK, STR[0:1, 32:64], axis=AX),
                reads=[B("str")], writes=[B("bk")])
            p.emit("vector", lambda v: v.tensor_mul(BB_, BQ, BK),
                   reads=[B("bq"), B("bk")], writes=[B("bbv")])
            p.emit("scalar", lambda s: s.activation(BB_, BB_, AF.Ln),
                   reads=[B("bbv")], writes=[B("bbv")])
            p.emit("scalar", lambda s: s.activation(
                BB_, BB_, AF.Exp, scale=0.5, bias=LN_CSC_MARGIN),
                reads=[B("bbv")], writes=[B("bbv")])
            p.emit("vector", lambda v: v.tensor_scalar(
                BB_, BB_, -1.0, None, ALU.mult),
                reads=[B("bbv")], writes=[B("bbv")])
            p.emit("tensor", lambda t: t.matmul(
                QE[:, 0:1], ONESF[:, :], BB_, start=True, stop=True),
                reads=[B("bbv"), B("ONESF")], writes=[B("QE")])
            p.emit("scalar", lambda s: s.copy(BIAS[:, 0:1], QE[:, 0:1]),
                   reads=[B("QE")], writes=[B("bias")], waits=p.tick())

            for nm, src, dst_ in (("q", qc_d, qg_d), ("k", kc_d, kg_d),
                                  ("v", vc_d, vg_d)):
                def coll(g, _s=src, _d=dst_):
                    return g.collective_compute(
                        "AllGather", ALU.bypass, replica_groups=groups_pair,
                        ins=[_s.opt()], outs=[_d.opt()])
                p.emit("gpsimd", coll, reads=[B(src.tensor.name)],
                       writes=[B(dst_.tensor.name)], semkey="gpsimdD")

        p.fence()

        with ExitStack() as s3:
            e3 = s3.enter_context
            ST0 = e3(nc.psum_tensor("ST0", [128, 1024], F32))
            ST1 = e3(nc.psum_tensor("ST1", [128, 1024], F32))
            OA = e3(nc.psum_tensor("OAp", [128, 512], F32))
            OB_ = e3(nc.psum_tensor("OBp", [128, 512], F32))
            SS = e3(nc.psum_tensor("SSp", [128, 512], F32))
            QH = e3(nc.sbuf_tensor("QH", [128, 2 * T], BF16))
            KH = e3(nc.sbuf_tensor("KH", [128, 2 * T], BF16))
            VH = e3(nc.sbuf_tensor("VH", [128, 2 * T], BF16))
            PTR = e3(nc.sbuf_tensor("PTR", [128, 4096], BF16))
            SACC = e3(nc.sbuf_tensor("SACC", [128, 1024], F32))
            SACB = e3(nc.sbuf_tensor("SACB", [128, 1024], BF16))
            RSB = e3(nc.sbuf_tensor("RSB", [128, 1024], F32))
            OEV = e3(nc.sbuf_tensor("OEV", [128, 1024], BF16))
            SRW = e3(nc.sbuf_tensor("SRW", [1, 1024], F32))

            rs = p.regs

            def dyn_dma(out_ap, tensor, base_reg, const_off, dims, reads, writes):
                def fn(engine):
                    engine.reg_alu(rs["rt"], rs[base_reg], const_off, ALU.add)
                    return engine.dma_start(
                        out=out_ap, in_=bass.AP(tensor, rs["rt"], dims))
                return p.emit("sync", fn, reads=reads, writes=writes,
                              inc=16, semkey="syncD")

            for r in range(2):
                dyn_dma(qm_d[:, r * TOK:(r + 1) * TOK], qg_d.tensor, "rq",
                        r * 4096 * TOK, [[TOK, 2048], [1, TOK]],
                        reads=[B("qg_d")], writes=[B("qm_d")])
                dyn_dma(km_d[:, r * TOK:(r + 1) * TOK], kg_d.tensor, "rk",
                        r * 1024 * TOK, [[TOK, 512], [1, TOK]],
                        reads=[B("kg_d")], writes=[B("km_d")])
                dyn_dma(vm_d[r * TOK:(r + 1) * TOK, :], vg_d.tensor, "rv",
                        r * TOK * 1024, [[1024, TOK], [1, 512]],
                        reads=[B("vg_d")], writes=[B("vm_d")])

            for gl in range(4):
                gs = gl % 2
                p.dma("sync", KH[:, gs * T:(gs + 1) * T],
                      km_d[gl * 128:(gl + 1) * 128, :],
                      reads=[B("km_d")], writes=[B("kh%d" % gs)])
                p.dma("sync",
                      bass.AP(VH, gs * T, [[2 * T, 128], [128, NKC], [1, 128]]),
                      bass.AP(vm_d.tensor, gl * 128,
                              [[512, 128], [128 * 512, NKC], [1, 128]]),
                      reads=[B("vm_d")], writes=[B("vh%d" % gs)])
                for h4 in range(4):
                    hl = gl * 4 + h4
                    qs_ = hl % 2
                    p.dma("sync", QH[:, qs_ * T:(qs_ + 1) * T],
                          qm_d[hl * 128:(hl + 1) * 128, :],
                          reads=[B("qm_d")], writes=[B("qh%d" % qs_)])
                    for qsb in range(NQSB):
                        C = 8 * qsb + 8
                        for c in range(C):
                            SP, spn = (ST0, "ST0") if c % 2 == 0 else (ST1, "ST1")
                            slot = c % 4
                            diag_i = c - 8 * qsb
                            lo = 128 * diag_i if diag_i >= 0 else 0

                            def mmst_(t, _SP=SP, _c=c, _lo=lo, _g=gs, _q=qs_,
                                      _qsb=qsb):
                                ins = None
                                for hf in range(2):
                                    a = max(_lo, hf * 512)
                                    bnd = (hf + 1) * 512
                                    if a >= bnd:
                                        continue
                                    ins = t.matmul(
                                        _SP[:, a:bnd],
                                        KH[:, _g * T + _c * 128:
                                           _g * T + (_c + 1) * 128],
                                        QH[:, _q * T + _qsb * 1024 + a:
                                           _q * T + _qsb * 1024 + bnd],
                                        start=True, stop=True)
                                return ins
                            p.emit("tensor", mmst_,
                                   reads=[B("kh%d" % gs), B("qh%d" % qs_)],
                                   writes=[B(spn)])
                            p.emit("scalar", lambda s, _SP=SP, _lo=lo, _sl=slot:
                                   s.activation(
                                       PTR[:, _sl * 1024 + _lo:_sl * 1024 + 1024],
                                       _SP[:, _lo:1024], AF.Exp,
                                       bias=BIAS[:, 0:1], scale=CSC),
                                   reads=[B(spn), B("bias")],
                                   writes=[B("ptr%d" % slot)],
                                   waits=p.tick())
                            if diag_i >= 0:
                                p.emit("vector", lambda v, _sl=slot, _lo=lo:
                                       v.tensor_mul(
                                           PTR[:, _sl * 1024 + _lo:
                                               _sl * 1024 + _lo + 128],
                                           PTR[:, _sl * 1024 + _lo:
                                               _sl * 1024 + _lo + 128],
                                           TRIB),
                                       reads=[B("ptr%d" % slot), B("TRI")],
                                       writes=[B("ptr%d" % slot)])

                            if c == 0:
                                p.emit("vector", lambda v, _sl=slot:
                                       v.tensor_copy(
                                           SACC[:, :],
                                           PTR[:, _sl * 1024:
                                               _sl * 1024 + 1024]),
                                       reads=[B("ptr%d" % slot)],
                                       writes=[B("sacc")])
                            else:
                                p.emit("vector", lambda v, _sl=slot, _lo=lo:
                                       v.tensor_tensor(
                                           SACC[:, _lo:1024],
                                           SACC[:, _lo:1024],
                                           PTR[:, _sl * 1024 + _lo:
                                               _sl * 1024 + 1024],
                                           ALU.add),
                                       reads=[B("ptr%d" % slot), B("sacc")],
                                       writes=[B("sacc")])

                            def mmpv(t, _c=c, _lo=lo, _sl=slot, _g=gs,
                                     _st=(c == 0)):
                                ins = None
                                for hf in range(2):
                                    a = max(_lo, hf * 512)
                                    bnd = (hf + 1) * 512
                                    if a >= bnd:
                                        continue
                                    OP = OA if hf == 0 else OB_
                                    ins = t.matmul(
                                        OP[:, a - hf * 512:bnd - hf * 512],
                                        VH[:, _g * T + _c * 128:
                                           _g * T + (_c + 1) * 128],
                                        PTR[:, _sl * 1024 + a:_sl * 1024 + bnd],
                                        start=_st, stop=False)
                                return ins
                            p.emit("tensor", mmpv,
                                   reads=[B("ptr%d" % slot), B("vh%d" % gs)],
                                   writes=([B("OAp"), B("OBp")]
                                           if c == 0 else []))

                        p.emit("vector", lambda v: v.tensor_copy(
                            SACB[:, :], SACC[:, :]),
                            reads=[B("sacc")], writes=[B("sacb")])

                        def mmss(t):
                            i = t.matmul(SS[0:1, :], ONESB[:, :],
                                         SACB[:, 0:512], start=True, stop=True)
                            i = t.matmul(SS[32:33, :], ONESB[:, :],
                                         SACB[:, 512:1024], start=True,
                                         stop=True, tile_position=(0, 32))
                            return i
                        p.emit("tensor", mmss, reads=[B("sacb"), B("ONESB")],
                               writes=[B("SSp")])
                        tk = p.tick()
                        p.emit("scalar", lambda s: s.copy(
                            SRW[0:1, 0:512], SS[0:1, :]),
                            reads=[B("SSp")], writes=[B("srw")], waits=tk)
                        p.emit("scalar", lambda s: s.copy(
                            SRW[0:1, 512:1024], SS[32:33, :]),
                            reads=[B("SSp")], writes=[B("srw")], waits=tk)
                        p.emit("scalar", lambda s: s.activation(
                            SRW[0:1, :], SRW[0:1, :], AF.Ln),
                            reads=[B("srw")], writes=[B("srw")])
                        p.emit("scalar", lambda s: s.activation(
                            SRW[0:1, :], SRW[0:1, :], AF.Exp, scale=-1.0),
                            reads=[B("srw")], writes=[B("srw")])
                        for c2 in range(2):
                            p.emit("tensor", lambda t, _c=c2: t.matmul(
                                ST0[:, _c * 512:(_c + 1) * 512], ONESF[:, :],
                                SRW[0:1, _c * 512:(_c + 1) * 512],
                                start=True, stop=True),
                                reads=[B("srw"), B("ONESF")], writes=[B("ST0")])
                        p.emit("scalar", lambda s: s.copy(RSB[:, :], ST0[:, :]),
                               reads=[B("ST0")], writes=[B("rsb")],
                               waits=p.tick())
                        p.emit("vector", lambda v: v.tensor_mul(
                            OEV[:, 0:512], OA[:, :], RSB[:, 0:512]),
                            reads=[B("OAp"), B("rsb")], writes=[B("oev")])
                        p.emit("vector", lambda v: v.tensor_mul(
                            OEV[:, 512:1024], OB_[:, :], RSB[:, 512:1024]),
                            reads=[B("OBp"), B("rsb")], writes=[B("oev")])
                        p.dma("sync",
                              oc_d[hl * 128:(hl + 1) * 128,
                                   qsb * 1024:(qsb + 1) * 1024],
                              OEV[:, :], reads=[B("oev")], writes=[B("oc_d")])

            def coll_o(g):
                return g.collective_compute(
                    "AllGather", ALU.bypass, replica_groups=groups_pair,
                    ins=[oc_d.opt()], outs=[og_d.opt()])
            p.emit("gpsimd", coll_o, reads=[B("oc_d")], writes=[B("og_d")], semkey="gpsimdD")

        p.fence()

        with ExitStack() as s4:
            e4 = s4.enter_context
            WA_ = e4(nc.psum_tensor("WAp", [128, 512], F32))
            WB2 = e4(nc.psum_tensor("WBp", [128, 512], F32))
            SX = e4(nc.psum_tensor("SXp", [128, 512], F32))
            SX2 = e4(nc.psum_tensor("SX2p", [128, 512], F32))
            OGH = e4(nc.sbuf_tensor("OGH", [128, 32768], BF16))
            WOB = e4(nc.sbuf_tensor("WOBs", [128, 8192], BF16))
            X1C = e4(nc.sbuf_tensor("X1C", [128, 1024], F32))
            X2C = e4(nc.sbuf_tensor("X2C", [128, 1024], F32))
            X2B = e4(nc.sbuf_tensor("X2B", [128, 1024], BF16))
            SQ2 = e4(nc.sbuf_tensor("SQ2", [128, 1024], BF16))

            s4_negmu = ROWA[0:1, 0:512]
            s4_r = ROWA[0:1, 512:1024]
            s4_t1 = ROWA[0:1, 1024:1536]
            s4_t2 = ROWA[0:1, 1536:2048]
            s4_v = ROWA[0:1, 2048:2560]
            p.emit("vector", lambda v: v.memset(LVACC2, 0.0), writes=[B("lvacc2")])

            rs = p.regs
            for r4 in range(4):
                def omdma(engine, _r=r4):
                    engine.reg_alu(rs["rt"], rs["ro"], _r * 1024 * T, ALU.add)
                    return engine.dma_start(
                        out=om_d[_r * 1024:(_r + 1) * 1024, :],
                        in_=bass.AP(og_d.tensor, rs["rt"],
                                    [[T, 1024], [1, TOK]]))
                p.emit("sync", omdma, reads=[B("og_d")], writes=[B("om_d")],
                       inc=16, semkey="syncD")
            for half in range(NH):
                p.dma("sync",
                      bass.AP(OGH, 0, [[32768, 128], [1024, 32], [1, 1024]]),
                      bass.AP(om_d.tensor, half * 1024,
                              [[TOK, 128], [128 * TOK, 32], [1, 1024]]),
                      reads=[B("om_d")], writes=[B("ogh")])
                for cc in range(32):
                    ws = cc % 2
                    wb_ = B("wob%d" % ws)
                    p.dma("scalar",
                          bass.AP(WOB, ws * 4096,
                                  [[8192, 128], [128, 32], [1, 128]]),
                          bass.AP(wob.tensor, cc * 128,
                                  [[4096, 128], [128 * 4096, 32], [1, 128]]),
                          reads=[B("wob_c%d" % (cc * 128 // 512))],
                          writes=[wb_])
                    p.dma("scalar", X1C[:, :],
                          bass.AP(x1d.tensor, cc * 128 * TOK + half * 1024,
                                  [[TOK, 128], [1, 1024]]),
                          reads=[B("x1d")], writes=[B("x1c")])
                    for tcol in range(2):
                        tgt, tgn = (WA_, "WAp") if tcol == 0 else (WB2, "WBp")

                        def mmo(t, _tg=tgt, _ws=ws, _tc=tcol):
                            ins = None
                            for hd in range(32):
                                ins = t.matmul(
                                    _tg[:, :],
                                    WOB[:, _ws * 4096 + hd * 128:
                                        _ws * 4096 + hd * 128 + 128],
                                    OGH[:, hd * 1024 + _tc * 512:
                                        hd * 1024 + _tc * 512 + 512],
                                    start=(hd == 0), stop=(hd == 31))
                            return ins
                        p.emit("tensor", mmo, reads=[wb_, B("ogh")],
                               writes=[B(tgn)])
                        p.emit("vector", lambda v, _tg=tgt, _tc=tcol, _cc=cc:
                               v.scalar_tensor_tensor(
                                   X2C[:, _tc * 512:(_tc + 1) * 512], _tg[:, :],
                                   BO2[:, _cc:_cc + 1],
                                   X1C[:, _tc * 512:(_tc + 1) * 512],
                                   ALU.add, ALU.add),
                               reads=[B(tgn), B("x1c"), B("BO2")],
                               writes=[B("x2c")], waits=p.tick())
                    p.dma("sync",
                          bass.AP(x2d.tensor, cc * 128 * TOK + half * 1024,
                                  [[TOK, 128], [1, 1024]]),
                          X2C[:, :], reads=[B("x2c")], writes=[B("x2d")])
                    p.emit("gpsimd", lambda g: g.tensor_copy(X2B[:, :],
                                                             X2C[:, :]),
                           reads=[B("x2c")], writes=[B("x2b")])
                    p.dma("sync",
                          bass.AP(xbfd.tensor, cc * 128 * TOK + half * 1024,
                                  [[TOK, 128], [1, 1024]]),
                          X2B[:, :], reads=[B("x2b")], writes=[B("xbfd")])
                    p.emit("gpsimd", lambda g: g.tensor_mul(
                        SQ2[:, :], X2B[:, :], X2B[:, :]),
                        reads=[B("x2b")], writes=[B("sq2")])

                    def mmsx(t, _cc=cc):
                        ins = None
                        for tc in range(2):
                            st = (_cc == 0)
                            tp = (0, 0) if tc == 0 else (0, 32)
                            tsl = slice(tc * 512, (tc + 1) * 512)
                            ins = t.matmul(
                                SX[0:1, :] if tc == 0 else SX[32:33, :],
                                ONESB[:, :], X2B[:, tsl],
                                start=st, stop=False, tile_position=tp)
                            ins = t.matmul(
                                SX2[0:1, :] if tc == 0 else SX2[32:33, :],
                                ONESB[:, :], SQ2[:, tsl],
                                start=st, stop=False, tile_position=tp)
                        return ins
                    p.emit("tensor", mmsx,
                           reads=[B("x2b"), B("sq2"), B("ONESB")],
                           writes=([B("SXp"), B("SX2p")] if cc == 0 else []))

                for tc in range(2):
                    tile = half * 2 + tc
                    tsl = slice(tile * 512, (tile + 1) * 512)
                    srcS = SX[0:1, :] if tc == 0 else SX[32:33, :]
                    srcS2 = SX2[0:1, :] if tc == 0 else SX2[32:33, :]
                    tk = p.tick()
                    p.emit("vector", lambda v, _s=srcS: v.tensor_scalar(
                        s4_negmu, _s, -1.0 / 4096.0, None, ALU.mult),
                        reads=[B("SXp")], writes=[B("s4negmu")], waits=tk)
                    p.emit("vector", lambda v, _s=srcS2: v.tensor_scalar(
                        s4_t1, _s, 1.0 / 4096.0, None, ALU.mult),
                        reads=[B("SX2p")], writes=[B("s4t1")], waits=tk)
                    p.emit("vector", lambda v: v.tensor_mul(
                        s4_t2, s4_negmu, s4_negmu),
                        reads=[B("s4negmu")], writes=[B("s4t2")])
                    p.emit("vector", lambda v: v.tensor_sub(s4_v, s4_t1, s4_t2),
                           reads=[B("s4t1"), B("s4t2")], writes=[B("s4v")])
                    p.emit("scalar", lambda s: s.activation(
                        s4_t1, s4_v, AF.Ln, bias=EPS),
                        reads=[B("s4v")], writes=[B("s4t1")])
                    p.emit("scalar", lambda s: s.activation(
                        s4_r, s4_t1, AF.Exp, scale=-0.5),
                        reads=[B("s4t1")], writes=[B("s4r")])
                    p.dma("sync", rowsd[8:9, tsl], s4_negmu,
                          reads=[B("s4negmu")], writes=[B("rw8")])
                    p.dma("sync", rowsd[9:10, tsl], s4_r,
                          reads=[B("s4r")], writes=[B("rw9")])
                    p.emit("vector", lambda v: v.tensor_mul(s4_t2, s4_r, s4_r),
                           reads=[B("s4r")], writes=[B("s4t2")])
                    p.emit("vector", lambda v: v.tensor_mul(s4_t1, s4_v, s4_t2),
                           reads=[B("s4v"), B("s4t2")], writes=[B("s4t1")])
                    p.emit("scalar", lambda s: s.activation(
                        s4_t2, s4_t1, AF.Ln, bias=1.0),
                        reads=[B("s4t1")], writes=[B("s4t2")])
                    p.dma("sync", rowsd[15:16, tsl], s4_t2,
                          reads=[B("s4t2")], writes=[B("rw15")])
                    p.emit("vector", lambda v: v.reduce_sum(
                        LVTMP2, s4_t2, axis=AX),
                        reads=[B("s4t2")], writes=[B("lvt2")])
                    p.emit("vector", lambda v: v.tensor_add(
                        LVACC2, LVACC2, LVTMP2),
                        reads=[B("lvt2"), B("lvacc2")], writes=[B("lvacc2")])

            p.emit("vector", lambda v: v.memset(LVST2, 0.0), writes=[B("lvst2")])
            p.emit("vector", lambda v: v.tensor_copy(
                ROWD[0:1, 32:33], LVACC2),
                reads=[B("lvacc2")], writes=[B("lvst2")])
            p.dma("sync", lv_c2, LVST2, reads=[B("lvst2")], writes=[B("lv_c2")])

            def coll_lv2(g):
                return g.collective_compute(
                    "AllReduce", ALU.add, replica_groups=groups_all,
                    ins=[lv_c2.opt()], outs=[lv_g2.opt()])
            p.emit("gpsimd", coll_lv2, reads=[B("lv_c2")], writes=[B("lvgf2")], semkey="gpsimdD")

        p.fence()

        emit_ffn(x2d, y_out, rowbank=8, lvg=lv_g2, lvc=lv_c2, prepass=False,
                 tag="f2")

        for name in dbg:
            src = io[name]
            dst = io["dbg_" + name]
            p.dma("sync", dst, src, reads=[B(name)], writes=[B("dbg_" + name)])

        snap = [(k, p.cnt[k]) for k in SEMKEYS if p.cnt[k] > 0]
        p.emit("sync", lambda s: s.nop(), waits=snap, inc=1)

        with nc.Block() as block:
            p.replay(block)

    return nc



def _rope_tables(half, TOK):
    pos = np.arange(half * TOK, (half + 1) * TOK, dtype=np.float64)
    freqs = 1.0 / (10000.0 ** (np.arange(0, 128, 2, dtype=np.float64) / 128.0))
    ang = freqs[:, None] * pos[None, :]
    c = np.cos(ang).astype(np.float32)
    s = np.sin(ang).astype(np.float32)
    tabc = np.concatenate([c, c], axis=0)
    tabs = np.concatenate([-s, s], axis=0)
    return np.ascontiguousarray(tabc), np.ascontiguousarray(tabs)


def _perm_deinterleave(ncols):
    perm = np.zeros(ncols, dtype=np.int64)
    for h in range(ncols // 128):
        b = h * 128
        perm[b:b + 64] = b + np.arange(64) * 2
        perm[b + 64:b + 128] = b + np.arange(64) * 2 + 1
    return perm


def prepare_inputs(inputs, npairs, T):
    TOK = T // 2
    bf = ml_dtypes.bfloat16
    x = np.asarray(inputs["L0_attn_x"], dtype=np.float32)

    shared = dict(
        awq=np.ascontiguousarray(
            np.asarray(inputs["L0_attn_Wq"], np.float32)[:,
                                                         _perm_deinterleave(4096)]),
        awk=np.ascontiguousarray(
            np.asarray(inputs["L0_attn_Wk"], np.float32)[:,
                                                         _perm_deinterleave(1024)]),
        awv=np.ascontiguousarray(np.asarray(inputs["L0_attn_Wv"], np.float32)),
        awo=np.ascontiguousarray(np.asarray(inputs["L0_attn_Wo"], np.float32)),
        fwqt=np.ascontiguousarray(np.asarray(inputs["L0_ffn_Wq"], np.float32).T),
        fwrt=np.ascontiguousarray(np.asarray(inputs["L0_ffn_Wr"], np.float32).T),
        akt=np.ascontiguousarray(
            np.asarray(inputs["L0_ffn_aux_keys"], np.float32).T),
        avt=np.ascontiguousarray(
            np.asarray(inputs["L0_ffn_aux_values"], np.float32).T),
        wvdt=np.ascontiguousarray(np.asarray(inputs["L0_ffn_Wvd"], np.float32).T),
        wvut=np.ascontiguousarray(np.asarray(inputs["L0_ffn_Wvu"], np.float32).T),
        wu1t=np.ascontiguousarray(np.asarray(inputs["L0_ffn_Wu1"], np.float32).T),
        wu2t=np.ascontiguousarray(np.asarray(inputs["L0_ffn_Wu2"], np.float32).T),
        bo2=np.ascontiguousarray(
            np.asarray(inputs["L0_attn_bo"], np.float32).reshape(32, 128).T),
        bu12=np.ascontiguousarray(
            np.asarray(inputs["L0_ffn_bu1"], np.float32).reshape(8, 128).T),
        scal=np.array([[float(np.asarray(inputs["L0_ffn_bu2"]).ravel()[0]),
                        float(np.asarray(inputs["L0_ffn_gate_w1"])),
                        float(np.asarray(inputs["L0_ffn_gate_bias"])), 0.0]],
                      np.float32),
        logrel=np.ascontiguousarray(
            np.asarray(inputs["L0_ffn_log_rel"], np.float32).reshape(1, 1024)),
        identf=np.eye(128, dtype=np.float32),
        identb=np.eye(128, dtype=np.float32).astype(bf),
        tri=np.ascontiguousarray(
            (np.arange(128)[:, None] <= np.arange(128)[None, :])
            .astype(np.float32)).astype(bf),
    )
    onq = np.zeros((128, 1024), np.float32)
    for h in range(32):
        onq[:, h * 32 + h] = 1.0
    onk = np.zeros((128, 256), np.float32)
    for g in range(8):
        onk[:, g * 32 + 4 * g:g * 32 + 4 * g + 4] = 1.0
    shared["oneq"] = onq.astype(bf)
    shared["onek"] = onk.astype(bf)

    in_maps = []
    for core in range(2 * npairs):
        b, half = core // 2, core % 2
        m = dict(shared)
        m["x"] = np.ascontiguousarray(
            x[b, half * TOK:(half + 1) * TOK, :].T.astype(np.float32))
        tc, ts = _rope_tables(half, TOK)
        m["tabc"] = tc
        m["tabs"] = ts
        m["pari"] = np.full((1, 1), half, np.int32)
        in_maps.append(m)
    return in_maps


def kernel(**inputs):
    npairs, T = 4, 4096
    TOK = T // 2
    in_maps = prepare_inputs(inputs, npairs, T)
    nc = build_program(npairs, T)
    res = run_bass_kernel_spmd(nc, in_maps, core_ids=list(range(2 * npairs)))
    out = np.empty((npairs, T, 4096), dtype=np.float32)
    for core in range(2 * npairs):
        b, half = core // 2, core % 2
        out[b, half * TOK:(half + 1) * TOK, :] = res.results[core]["y"].T
    return out


# version 04
# speedup vs baseline: 1.0875x; 1.0625x over previous
import numpy as np
import ml_dtypes
from contextlib import ExitStack

import concourse.bass as bass
import concourse.mybir as mybir
from concourse.bass_utils import run_bass_kernel_spmd

F32 = mybir.dt.float32
BF16 = mybir.dt.bfloat16
I32 = mybir.dt.int32
AF = mybir.ActivationFunctionType
ALU = mybir.AluOpType
AX = mybir.AxisListType.X

ENGINES = ("sync", "vector", "scalar", "tensor", "gpsimd")
NEGBIG = -1.0e30
EPS = 1e-5
CSC = float(128.0 ** -0.5)
LN_CSC_MARGIN = float(np.log(1.02 * CSC))


class Buf:
    __slots__ = ("ready", "users", "name")

    def __init__(self, name=""):
        self.ready = None
        self.users = []
        self.name = name


SEMKEYS = list(ENGINES) + [e + "D" for e in ENGINES]


class Prog:
    def __init__(self, nc, ctx):
        self.nc = nc
        self.q = {e: [] for e in ENGINES}
        self.sem = {k: ctx.enter_context(nc.semaphore("S" + k)) for k in SEMKEYS}
        self.cnt = {k: 0 for k in SEMKEYS}
        self.clock = {e: {k: 0 for k in SEMKEYS} for e in ENGINES}
        self.regs = {}
        self.bufs = {}

    def B(self, name):
        if name not in self.bufs:
            self.bufs[name] = Buf(name)
        return self.bufs[name]

    def emit(self, eng, fn, reads=(), writes=(), waits=(), inc=1, semkey=None):
        key = semkey or eng
        need = {}

        def add(t):
            if t is None:
                return
            f, v = t
            if v > need.get(f, 0):
                need[f] = v

        for b in reads:
            add(b.ready)
        for b in writes:
            add(b.ready)
            for t in b.users:
                add(t)
        for t in waits:
            add(t)

        cl = self.clock[eng]
        real = tuple((f, v) for f, v in need.items() if v > cl[f])
        for f, v in real:
            cl[f] = v

        self.cnt[key] += inc
        ticket = (key, self.cnt[key])
        sems = self.sem
        my_sem = sems[key]

        def g(engine, _fn=fn, _real=real, _inc=inc, _sem=my_sem):
            for f, v in _real:
                engine.wait_ge(sems[f], v)
            ins = _fn(engine)
            if _inc:
                ins.then_inc(_sem, _inc)

        self.q[eng].append(g)
        if inc:
            for b in reads:
                b.users.append(ticket)
            for b in writes:
                b.ready = ticket
                b.users = []
        return ticket

    def dma(self, eng, out_ap, in_ap, reads=(), writes=(), waits=(), slow=False):
        def fn(engine, _o=out_ap, _i=in_ap, _s=slow):
            return engine.dma_start(out=_o, in_=_i, allow_slow_non_contiguous=_s)
        return self.emit(eng, fn, reads=reads, writes=writes, waits=waits,
                         inc=16, semkey=eng + "D")

    def tick(self, eng="tensor"):
        return [(eng, self.cnt[eng])]

    def fence(self):
        snap = [(k, self.cnt[k]) for k in SEMKEYS if self.cnt[k] > 0]
        for e in ENGINES:
            self.emit(e, lambda engine: engine.nop(), waits=snap, inc=1)

    def replay(self, block):
        for e in ENGINES:
            qe = self.q[e]
            if not qe:
                continue

            def body(engine, _qe=qe):
                for f in _qe:
                    f(engine)

            getattr(block, e)(body)


def build_program(npairs, T, dbg=()):
    TOK = T // 2
    NT = TOK // 512
    NQSB = T // 1024
    NKC = T // 128
    NH = TOK // 1024
    NCORE = 2 * npairs
    NTOK_ALL = npairs * T

    nc = bass.Bass("TRN2", target_bir_lowering=False)
    ctx = ExitStack()
    io = {}

    def din(name, shape, dt=F32):
        io[name] = nc.dram_tensor(name, list(shape), dt, kind="ExternalInput").ap()
        return io[name]

    def dout(name, shape, dt=F32):
        io[name] = nc.dram_tensor(name, list(shape), dt, kind="ExternalOutput").ap()
        return io[name]

    def dint(name, shape, dt=F32):
        io[name] = nc.dram_tensor(name, list(shape), dt, kind="Internal").ap()
        return io[name]

    x_in = din("x", (4096, TOK))
    y_out = dout("y", (4096, TOK))
    awq = din("awq", (128, 131072))
    awk = din("awk", (128, 32768))
    awv = din("awv", (128, 32768))
    awo = din("awo", (128, 131072))
    fwqt = din("fwqt", (4096, 64))
    fwrt = din("fwrt", (64, 48))
    akt = din("akt", (64, 1024))
    avt = din("avt", (128, 32768))
    wvdt = din("wvdt", (4096, 256))
    wvut = din("wvut", (256, 4096))
    wu1t = din("wu1t", (128, 32768))
    wu2t = din("wu2t", (1024, 1))
    bo2 = din("bo2", (128, 32))
    bu12 = din("bu12", (128, 8))
    scal = din("scal", (1, 4))
    logrel = din("logrel", (1, 1024))
    tabc = din("tabc", (128, TOK))
    tabs = din("tabs", (128, TOK))
    identf = din("identf", (128, 128))
    identb = din("identb", (128, 128), BF16)
    tri = din("tri", (128, 128), BF16)
    oneq = din("oneq", (128, 1024), BF16)
    onek = din("onek", (128, 256), BF16)
    pari = din("pari", (1, 1), I32)

    x1d = dint("x1d", (4096, TOK))
    x2d = dint("x2d", (4096, TOK))
    xbfd = dint("xbfd", (4096, TOK), BF16)
    auxd = dint("auxd", (4096, TOK), BF16)
    rowsd = dint("rowsd", (32, TOK))
    wqb = dint("wqb", (128, 131072), BF16)
    wkb = dint("wkb", (128, 32768), BF16)
    wvb = dint("wvb", (128, 32768), BF16)
    wob = dint("wob", (128, 131072), BF16)
    fwqtb = dint("fwqtb", (4096, 64), BF16)
    fwrtb = dint("fwrtb", (64, 48), BF16)
    aktb = dint("aktb", (64, 1024), BF16)
    avtb = dint("avtb", (128, 32768), BF16)
    wvdtb = dint("wvdtb", (4096, 256), BF16)
    wvutb = dint("wvutb", (256, 4096), BF16)
    wu1tb = dint("wu1tb", (128, 32768), BF16)
    wu2tb = dint("wu2tb", (1024, 1), BF16)
    apd = dint("apd", (1024, 256), BF16)
    colsd = dint("colsd", (4, 4096))
    qc_d = dint("qc_d", (4096, TOK), BF16)
    qg_d = dint("qg_d", (2 * 4096, TOK), BF16)
    kc_d = dint("kc_d", (1024, TOK), BF16)
    kg_d = dint("kg_d", (2 * 1024, TOK), BF16)
    vc_d = dint("vc_d", (TOK, 1024), BF16)
    vg_d = dint("vg_d", (2 * TOK, 1024), BF16)
    oc_d = dint("oc_d", (2048, T), BF16)
    og_d = dint("og_d", (4096, T), BF16)
    qm_d = dint("qm_d", (2048, T), BF16)
    km_d = dint("km_d", (512, T), BF16)
    vm_d = dint("vm_d", (T, 512), BF16)
    om_d = dint("om_d", (4096, TOK), BF16)
    st_c = dint("st_c", (1, 64))
    st_g = dint("st_g", (1, 64))
    lv_c = dint("lv_c", (1, 8))
    lv_g = dint("lv_g", (1, 8))
    lv_c2 = dint("lv_c2", (1, 8))
    lv_g2 = dint("lv_g2", (1, 8))

    for name in dbg:
        src = io[name]
        dout("dbg_" + name, list(src.tensor.shape), src.tensor.dtype)

    groups_pair = [[2 * i, 2 * i + 1] for i in range(npairs)]
    groups_all = [list(range(NCORE))]

    with ctx:
        p = Prog(nc, ctx)
        B = p.B
        e = ctx.enter_context

        CONST = e(nc.sbuf_tensor("CONST", [128, 256], F32))
        IDF = CONST[:, 0:128]
        ONESF = CONST[0:1, 128:256]
        CONSTB = e(nc.sbuf_tensor("CONSTB", [128, 384], BF16))
        IDB = CONSTB[:, 0:128]
        TRIB = CONSTB[:, 128:256]
        ONESB = CONSTB[:, 256:257]
        BO2 = e(nc.sbuf_tensor("BO2", [128, 48], F32))
        SCAL = e(nc.sbuf_tensor("SCAL", [1, 16], F32))
        LR = e(nc.sbuf_tensor("LRR", [1, 1024], F32))
        PTI = e(nc.sbuf_tensor("PTI", [1, 8], I32))
        BIAS = e(nc.sbuf_tensor("BIAS", [128, 8], F32))
        ROWA = e(nc.sbuf_tensor("ROWA", [1, 4096], F32))
        ROWB = e(nc.sbuf_tensor("ROWB", [1, 4096], F32))
        ROWD = e(nc.sbuf_tensor("ROWD", [1, 128], F32))
        RKBF = e(nc.sbuf_tensor("RKBF", [48, 1024], BF16))
        FWQ = e(nc.sbuf_tensor("FWQ", [128, 2048], BF16))
        FWR = e(nc.sbuf_tensor("FWR", [64, 48], BF16))
        AKT = e(nc.sbuf_tensor("AKT", [64, 1024], BF16))
        APR = e(nc.sbuf_tensor("APR", [128, 2048], BF16))
        WVU = e(nc.sbuf_tensor("WVU", [128, 8192], BF16))
        W2S = e(nc.sbuf_tensor("W2S", [128, 8], BF16))
        CSF = e(nc.sbuf_tensor("CSF", [1, 1152], F32))
        DCA = e(nc.sbuf_tensor("DCA", [128, 1024], F32))
        DCB = e(nc.sbuf_tensor("DCB", [128, 1024], BF16))
        DCA2, DCB2 = DCA, DCB

        p.dma("sync", CONST[:, 0:128], identf, writes=[B("IDF")])
        p.dma("sync", CONSTB[:, 0:128], identb, writes=[B("IDB")])
        p.dma("sync", CONSTB[:, 128:256], tri, writes=[B("TRI")])
        p.dma("sync", BO2[:, 0:32], bo2, writes=[B("BO2")])
        p.dma("sync", BO2[:, 32:40], bu12, writes=[B("BU1")])
        p.dma("sync", SCAL[0:1, 0:4], scal, writes=[B("SCAL")])
        p.dma("sync", LR[0:1, :], logrel, writes=[B("LRR")])
        p.dma("sync", PTI[0:1, 0:1], pari, writes=[B("PTI")])
        p.emit("vector", lambda v: v.memset(CONST[0:1, 128:256], 1.0),
               writes=[B("ONESF")])
        p.emit("vector", lambda v: v.memset(CONSTB[:, 256:257], 1.0),
               writes=[B("ONESB")])
        CONSTV = e(nc.sbuf_tensor("CONSTV", [128, 4], F32))
        for ci, cv in ((0, EPS), (1, 1.0), (2, LN_CSC_MARGIN)):
            p.emit("vector", lambda v, _i=ci, _v=cv: v.memset(
                CONSTV[:, _i:_i + 1], _v), writes=[B("constv")])
            nc.const_aps.register(F32, cv, CONSTV[:, ci:ci + 1])

        def mkregs(engine):
            rp = engine.alloc_register("rp")
            engine.reg_load(rp, PTI[0:1, 0:1])
            r1 = engine.alloc_register("rqr")
            engine.reg_mul(r1, rp, 2048 * TOK)
            r2 = engine.alloc_register("rkr")
            engine.reg_mul(r2, rp, 512 * TOK)
            r3 = engine.alloc_register("rvr")
            engine.reg_mul(r3, rp, 512)
            r4 = engine.alloc_register("ror")
            engine.reg_mul(r4, rp, TOK)
            rt = engine.alloc_register("rtt")
            p.regs.update(rq=r1, rk=r2, rv=r3, ro=r4, rt=rt)
            return engine.nop()

        p.emit("sync", mkregs, reads=[B("PTI")])

        with ExitStack() as s0:
            e0 = s0.enter_context
            P0S = e0(nc.psum_tensor("P0S", [128, 512], F32))
            CW = e0(nc.sbuf_tensor("CW", [128, 1024], F32))
            CW2 = e0(nc.sbuf_tensor("CW2", [128, 1024], F32))
            CWB = e0(nc.sbuf_tensor("CWB", [128, 1024], BF16))
            CWB2 = e0(nc.sbuf_tensor("CWB2", [128, 1024], BF16))
            WA0 = e0(nc.sbuf_tensor("WA0", [128, 4096], BF16))
            WD0 = e0(nc.sbuf_tensor("WD0", [128, 8192], BF16))
            EV0 = e0(nc.sbuf_tensor("EV0", [128, 256], BF16))

            def cast_weight(src, dst, rows, cols, colsum=None):
                nrc = (rows + 127) // 128
                for c0 in range(0, cols, 512):
                    cc = min(512, cols - c0)
                    for ri in range(nrc):
                        r0 = ri * 128
                        rr = min(128, rows - r0)
                        stage, stageb, sb = (
                            (CW, CWB, "cw0") if ri % 2 == 0 else (CW2, CWB2, "cw1"))
                        p.dma("sync", stage[0:rr, 0:cc],
                              src[r0:r0 + rr, c0:c0 + cc], writes=[B(sb)])
                        p.emit("vector",
                               lambda v, _s=stage, _b=stageb, _rr=rr, _cc=cc:
                               v.tensor_copy(_b[0:_rr, 0:_cc], _s[0:_rr, 0:_cc]),
                               reads=[B(sb)], writes=[B(sb + "b")])
                        p.dma("sync", dst[r0:r0 + rr, c0:c0 + cc],
                              stageb[0:rr, 0:cc], reads=[B(sb + "b")],
                              writes=[B(dst.tensor.name)])
                        if colsum is not None:
                            p.emit("tensor",
                                   lambda t, _b=stageb, _rr=rr, _cc=cc,
                                   _st=(ri == 0), _sp=(ri == nrc - 1):
                                   t.matmul(P0S[0:1, 0:_cc], ONESB[0:_rr, :],
                                            _b[0:_rr, 0:_cc], start=_st, stop=_sp),
                                   reads=[B(sb + "b"), B("ONESB")],
                                   writes=([B("P0S")] if ri == 0 else []))
                    if colsum is not None:
                        kindcs, off = colsum
                        p.emit("scalar",
                               lambda s, _cc=cc: s.copy(ROWA[0:1, 0:_cc],
                                                        P0S[0:1, 0:_cc]),
                               reads=[B("P0S")], writes=[B("csrow")],
                               waits=p.tick())
                        if kindcs == "sbuf":
                            p.emit("vector",
                                   lambda v, _c0=c0, _cc=cc, _o=off:
                                   v.tensor_copy(CSF[0:1, _o + _c0:_o + _c0 + _cc],
                                                 ROWA[0:1, 0:_cc]),
                                   reads=[B("csrow")], writes=[B("csf")])
                        else:
                            p.dma("sync", colsd[off:off + 1, c0:c0 + cc],
                                  ROWA[0:1, 0:cc], reads=[B("csrow")],
                                  writes=[B("colsd")])

            cast_weight(fwqt, fwqtb, 4096, 64, colsum=("sbuf", 0))
            cast_weight(fwrt, fwrtb, 64, 48)
            cast_weight(akt, aktb, 64, 1024)
            cast_weight(wvdt, wvdtb, 4096, 256)
            cast_weight(wvut, wvutb, 256, 4096)
            cast_weight(wu2t, wu2tb, 1024, 1)
            for srcw, dstw, cols in ((wu1t, wu1tb, 32768), (avt, avtb, 32768)):
                for c0 in range(0, cols, 1024):
                    st2, sb2, nm2 = ((CW, CWB, "cw0") if (c0 // 1024) % 2 == 0
                                     else (CW2, CWB2, "cw1"))
                    p.dma("sync", st2[:, :], srcw[:, c0:c0 + 1024],
                          writes=[B(nm2)])
                    p.emit("vector", lambda v, _s=st2, _b=sb2: v.tensor_copy(
                        _b[:, :], _s[:, :]), reads=[B(nm2)],
                        writes=[B(nm2 + "b")])
                    p.dma("sync", dstw[:, c0:c0 + 1024], sb2[:, :],
                          reads=[B(nm2 + "b")], writes=[B(dstw.tensor.name)])

            p.dma("sync",
                  bass.AP(WD0, 0,
                          [[8192, 128], [256, 32], [1, 256]]),
                  bass.AP(wvdtb.tensor, 0, [[256, 128], [128 * 256, 32], [1, 256]]),
                  reads=[B("wvdtb")], writes=[B("wd0")])
            for nch in range(8):
                p.dma("sync", WA0[:, :],
                      avtb[:, nch * 4096:(nch + 1) * 4096],
                      reads=[B("avtb")], writes=[B("wa0")])

                def mmA(t):
                    ins = None
                    for kc in range(32):
                        ins = t.matmul(P0S[:, 0:256],
                                       WA0[:, kc * 128:(kc + 1) * 128],
                                       WD0[:, kc * 256:(kc + 1) * 256],
                                       start=(kc == 0), stop=(kc == 31))
                    return ins

                p.emit("tensor", mmA, reads=[B("wa0"), B("wd0")], writes=[B("P0S")])
                p.emit("scalar", lambda s: s.copy(EV0[:, 0:256], P0S[:, 0:256]),
                       reads=[B("P0S")], writes=[B("ev0")], waits=p.tick())
                p.dma("sync", apd[nch * 128:(nch + 1) * 128, 0:256], EV0[:, 0:256],
                      reads=[B("ev0")], writes=[B("apd")])

            p.dma("sync", FWR[:, :], fwrtb, reads=[B("fwrtb")], writes=[B("FWR")])
            p.dma("sync", AKT[:, :], aktb, reads=[B("aktb")], writes=[B("AKT")])
            for c2 in range(2):
                p.emit("tensor", lambda t, _c=c2: t.matmul(
                    P0S[0:48, :], FWR[:, :], AKT[:, _c * 512:(_c + 1) * 512],
                    start=True, stop=True),
                    reads=[B("FWR"), B("AKT")], writes=[B("P0S")])
                p.emit("vector", lambda v, _c=c2: v.tensor_copy(
                    RKBF[:, _c * 512:(_c + 1) * 512], P0S[0:48, :]),
                    reads=[B("P0S")], writes=[B("RKBF")], waits=p.tick())

            p.dma("sync",
                  bass.AP(FWQ, 0,
                          [[2048, 128], [64, 32], [1, 64]]),
                  bass.AP(fwqtb.tensor, 0, [[64, 128], [128 * 64, 32], [1, 64]]),
                  reads=[B("fwqtb")], writes=[B("FWQ")])
            p.dma("sync",
                  bass.AP(APR, 0,
                          [[2048, 128], [256, 8], [1, 256]]),
                  bass.AP(apd.tensor, 0, [[256, 128], [128 * 256, 8], [1, 256]]),
                  reads=[B("apd")], writes=[B("APR")])
            p.dma("sync",
                  bass.AP(WVU, 0,
                          [[8192, 128], [4096, 2], [1, 4096]]),
                  bass.AP(wvutb.tensor, 0,
                          [[4096, 128], [128 * 4096, 2], [1, 4096]]),
                  reads=[B("wvutb")], writes=[B("WVU")])
            p.dma("sync", W2S[:, 0:8],
                  bass.AP(wu2tb.tensor, 0, [[1, 128], [128, 8]]),
                  reads=[B("wu2tb")], writes=[B("W2S")], slow=True)

        p.fence()

        deferred_units = []
        for srcw, dstw, cols in ((awq, wqb, 131072), (awk, wkb, 32768),
                                 (awv, wvb, 32768), (awo, wob, 131072)):
            for c0 in range(0, cols, 1024):
                deferred_units.append((srcw, dstw, c0))

        def emit_cast_unit(i, unit):
            srcw, dstw, c0 = unit
            stage, stageb, sb = ((DCA, DCB, "dc0") if i % 2 == 0
                                 else (DCA2, DCB2, "dc1"))
            p.dma("scalar", stage[:, :], srcw[:, c0:c0 + 1024], writes=[B(sb)])
            p.emit("scalar", lambda s, _s=stage, _b=stageb: s.copy(
                _b[:, :], _s[:, :]), reads=[B(sb)], writes=[B(sb + "b")])
            p.dma("scalar", dstw[:, c0:c0 + 1024], stageb[:, :],
                  reads=[B(sb + "b")],
                  writes=[B("%s_c%d" % (dstw.tensor.name, c0 // 4096))])

        dc_state = [0]

        def cast_filler(n):
            i0 = dc_state[0]
            for j in range(i0, min(i0 + n, len(deferred_units))):
                emit_cast_unit(j, deferred_units[j])
            dc_state[0] = min(i0 + n, len(deferred_units))

        def cast_drain():
            cast_filler(len(deferred_units))

        LVACC = ROWD[0:1, 0:1]
        LVTMP = ROWD[0:1, 1:2]
        LVST = ROWD[0:1, 8:16]
        MIV = ROWD[0:1, 16:17]
        LVACC2 = ROWD[0:1, 24:25]
        LVTMP2 = ROWD[0:1, 25:26]
        LVST2 = ROWD[0:1, 32:40]
        BQ = ROWD[0:1, 40:41]
        BK = ROWD[0:1, 41:42]
        BB_ = ROWD[0:1, 42:43]

        def emit_ffn(xsrc, dst, rowbank, lvg, lvc, prepass, tag, filler=None):
            xsrc_b = B(xsrc.tensor.name)

            def fill(n):
                if filler is not None:
                    filler(n)

            with ExitStack() as fs:
                fe = fs.enter_context
                FA = fe(nc.psum_tensor(tag + "FA", [128, 512], F32))
                FB = fe(nc.psum_tensor(tag + "FB", [128, 512], F32))
                FC = fe(nc.psum_tensor(tag + "FC", [128, 1024], F32))
                FD = fe(nc.psum_tensor(tag + "FD", [128, 1024], F32))
                FE = fe(nc.psum_tensor(tag + "FE", [128, 512], F32))
                FT = fe(nc.psum_tensor(tag + "FT", [128, 1024], BF16))
                XT4 = fe(nc.sbuf_tensor(tag + "XT4", [128, 2048], F32))
                XT4B = fe(nc.sbuf_tensor(tag + "XT4B", [128, 2048], F32))
                XB = fe(nc.sbuf_tensor(tag + "XB", [128, 16384], BF16))
                SQ4 = fe(nc.sbuf_tensor(tag + "SQ4", [128, 2048], BF16))
                W1 = fe(nc.sbuf_tensor(tag + "W1", [128, 8192], BF16))
                PTS = fe(nc.sbuf_tensor(tag + "PTS", [128, 4096], BF16))
                PB = fe(nc.sbuf_tensor(tag + "PB", [128, 1024], BF16))
                QB = fe(nc.sbuf_tensor(tag + "QB", [64, 512], BF16))
                RQ = fe(nc.sbuf_tensor(tag + "RQ", [48, 512], BF16))
                RTS = fe(nc.sbuf_tensor(tag + "RTS", [128, 1024], F32))
                FSS = fe(nc.sbuf_tensor(tag + "FSS", [128, 1024], F32))
                SELS = fe(nc.sbuf_tensor(tag + "SELS", [128, 1024], F32))
                MSK8 = fe(nc.sbuf_tensor(tag + "MSK8", [128, 1024],
                                         mybir.dt.uint8))
                NEGB = fe(nc.sbuf_tensor(tag + "NEGB", [128, 2], F32))
                T8 = fe(nc.sbuf_tensor(tag + "T8", [128, 8], F32))
                NB2 = fe(nc.sbuf_tensor(tag + "NB2", [128, 2], F32))
                SRC_ = fe(nc.sbuf_tensor(tag + "SRC", [128, 2], F32))
                RB512 = fe(nc.sbuf_tensor(tag + "RB512", [128, 512], F32))
                GAB = fe(nc.sbuf_tensor(tag + "GAB", [128, 1536], F32))
                ASB = fe(nc.sbuf_tensor(tag + "ASB", [128, 1024], BF16))
                AXE = fe(nc.sbuf_tensor(tag + "AXE", [128, 512], BF16))
                UF = fe(nc.sbuf_tensor(tag + "UF", [128, 512], F32))
                UB = fe(nc.sbuf_tensor(tag + "UB", [128, 512], BF16))
                AX4 = fe(nc.sbuf_tensor(tag + "AX4", [128, 2048], BF16))
                CF1 = fe(nc.sbuf_tensor(tag + "CF1", [128, 2048], F32))
                CF2 = fe(nc.sbuf_tensor(tag + "CF2", [128, 2048], F32))

                p.emit("vector", lambda v: v.memset(NEGB[:, :], NEGBIG),
                       writes=[B(tag + "negb")])
                NEGBB = NEGB[:, 0:1].to_broadcast([128, 1024])

                def oct_ap(dram, tile, o):
                    return bass.AP(dram.tensor, (o * 4 * 128) * TOK + tile * 512,
                                   [[TOK, 128], [128 * TOK, 4], [1, 512]])

                def xbfd_tile_ap(tile):
                    return bass.AP(xbfd.tensor, tile * 512,
                                   [[TOK, 128], [128 * TOK, 32], [1, 512]])

                s_negmu = ROWA[0:1, 0:512]
                s_r = ROWA[0:1, 512:1024]
                s_t1 = ROWA[0:1, 1024:1536]
                s_t2 = ROWA[0:1, 1536:2048]
                s_v = ROWA[0:1, 2048:2560]
                s_sr = ROWA[0:1, 2560:3072]
                s_le = ROWA[0:1, 3072:3584]
                s_g = ROWA[0:1, 3584:4096]
                s_a = ROWB[0:1, 0:512]
                s_b2 = ROWB[0:1, 512:1024]
                s_lv = ROWB[0:1, 1024:1536]
                s_g2 = ROWB[0:1, 1536:2048]

                def ln_rows(tile, store_bank, lvacc, lvtmp):
                    tk = p.tick()
                    p.emit("vector", lambda v: v.tensor_scalar(
                        s_negmu, FA[0:1, :], -1.0 / 4096.0, None, ALU.mult),
                        reads=[B(tag + "FA")], writes=[B(tag + "negmu")], waits=tk)
                    p.emit("vector", lambda v: v.tensor_scalar(
                        s_t1, FB[0:1, :], 1.0 / 4096.0, None, ALU.mult),
                        reads=[B(tag + "FB")], writes=[B(tag + "t1")], waits=tk)
                    p.emit("vector", lambda v: v.tensor_mul(
                        s_t2, s_negmu, s_negmu),
                        reads=[B(tag + "negmu")], writes=[B(tag + "t2")])
                    p.emit("vector", lambda v: v.tensor_sub(s_v, s_t1, s_t2),
                           reads=[B(tag + "t1"), B(tag + "t2")],
                           writes=[B(tag + "v")])
                    p.emit("scalar", lambda s: s.activation(
                        s_t1, s_v, AF.Ln, bias=EPS),
                        reads=[B(tag + "v")], writes=[B(tag + "t1")])
                    p.emit("scalar", lambda s: s.activation(
                        s_r, s_t1, AF.Exp, scale=-0.5),
                        reads=[B(tag + "t1")], writes=[B(tag + "r")])
                    tsl = slice(tile * 512, (tile + 1) * 512)
                    p.dma("sync", rowsd[store_bank:store_bank + 1, tsl], s_negmu,
                          reads=[B(tag + "negmu")],
                          writes=[B("rw%d" % store_bank)])
                    p.dma("sync", rowsd[store_bank + 1:store_bank + 2, tsl], s_r,
                          reads=[B(tag + "r")],
                          writes=[B("rw%d" % (store_bank + 1))])
                    if lvacc is not None:
                        p.emit("vector", lambda v: v.tensor_mul(s_t2, s_r, s_r),
                               reads=[B(tag + "r")], writes=[B(tag + "t2")])
                        p.emit("vector", lambda v: v.tensor_mul(s_t1, s_v, s_t2),
                               reads=[B(tag + "v"), B(tag + "t2")],
                               writes=[B(tag + "t1")])
                        p.emit("scalar", lambda s: s.activation(
                            s_t2, s_t1, AF.Ln, bias=1.0),
                            reads=[B(tag + "t1")], writes=[B(tag + "t2")])
                        p.dma("sync", rowsd[store_bank + 7:store_bank + 8, tsl],
                              s_t2, reads=[B(tag + "t2")],
                              writes=[B("rw%d" % (store_bank + 7))])
                        p.emit("vector", lambda v: v.reduce_sum(
                            lvtmp, s_t2, axis=AX),
                            reads=[B(tag + "t2")], writes=[B(tag + "lvt")])
                        p.emit("vector", lambda v: v.tensor_add(
                            lvacc, lvacc, lvtmp),
                            reads=[B(tag + "lvt"), B(tag + "lvacc")],
                            writes=[B(tag + "lvacc")])

                def stats_chain2(first, src_sb, sq_sb, src_buf, sq_buf):
                    def mm(t, _f=first, _src=src_sb, _sq=sq_sb):
                        ins = None
                        for c8 in range(2):
                            st = (_f and c8 == 0)
                            sl = slice(c8 * 512, (c8 + 1) * 512)
                            ins = t.matmul(FA[0:1, :], ONESB[:, :], _src[:, sl],
                                           start=st, stop=False)
                            ins = t.matmul(FB[0:1, :], ONESB[:, :], _sq[:, sl],
                                           start=st, stop=False)
                        return ins
                    p.emit("tensor", mm,
                           reads=[src_buf, sq_buf, B("ONESB")],
                           writes=([B(tag + "FA"), B(tag + "FB")]
                                   if first else []))

                def stats_chain(first, src_sb, sq_sb, src_buf, nchunks=4):
                    def mm(t, _f=first, _src=src_sb, _sq=sq_sb, _n=nchunks):
                        ins = None
                        for c8 in range(_n):
                            st = (_f and c8 == 0)
                            sl = slice(c8 * 512, (c8 + 1) * 512)
                            ins = t.matmul(FA[0:1, :], ONESB[:, :], _src[:, sl],
                                           start=st, stop=False)
                            ins = t.matmul(FB[0:1, :], ONESB[:, :], _sq[:, sl],
                                           start=st, stop=False)
                        return ins
                    p.emit("tensor", mm,
                           reads=[src_buf, B(tag + "sq"), B("ONESB")],
                           writes=([B(tag + "FA"), B(tag + "FB")]
                                   if first else []))

                if prepass:
                    p.emit("vector", lambda v: v.memset(LVACC, 0.0),
                           writes=[B(tag + "lvacc")])
                    for tile in range(NT):
                        for o in range(8):
                            xt = XT4 if o % 2 == 0 else XT4B
                            xtb = B(tag + "xt4%d" % (o % 2))
                            xbsl = XB[:, o * 2048:(o + 1) * 2048]
                            p.dma("sync", xt[:, :], oct_ap(xsrc, tile, o),
                                  reads=[xsrc_b], writes=[xtb])
                            p.emit("vector", lambda v, _x=xbsl, _xt=xt:
                                   v.tensor_copy(_x, _xt[:, :]),
                                   reads=[xtb],
                                   writes=[B(tag + "xb")])
                            p.dma("sync", oct_ap(xbfd, tile, o), xbsl,
                                  reads=[B(tag + "xb")], writes=[B("xbfd")])
                            p.emit("gpsimd", lambda g, _x=xbsl: g.tensor_mul(
                                SQ4[:, :], _x, _x),
                                reads=[B(tag + "xb")], writes=[B(tag + "sq")])
                            stats_chain(o == 0, xbsl, SQ4, B(tag + "xb"))
                        fill(48)
                        ln_rows(tile, rowbank, LVACC, LVTMP)
                    p.emit("vector", lambda v: v.memset(LVST, 0.0),
                           writes=[B(tag + "lvst")])
                    p.emit("vector", lambda v: v.tensor_copy(
                        ROWD[0:1, 8:9], LVACC),
                        reads=[B(tag + "lvacc")], writes=[B(tag + "lvst")])
                    p.dma("sync", lvc, LVST, reads=[B(tag + "lvst")],
                          writes=[B("lvc" + tag)])

                    def coll_lv(g):
                        return g.collective_compute(
                            "AllReduce", ALU.add, replica_groups=groups_all,
                            ins=[lvc.opt()], outs=[lvg.opt()])
                    p.emit("gpsimd", coll_lv, reads=[B("lvc" + tag)],
                           writes=[B("lvg" + tag)], semkey="gpsimdD")

                donext = dst is not y_out
                for tile in range(NT):
                    tsl = slice(tile * 512, (tile + 1) * 512)
                    p.dma("sync", bass.AP(XB, 0,
                                          [[16384, 128], [512, 32], [1, 512]]),
                          xbfd_tile_ap(tile), reads=[B("xbfd")],
                          writes=[B(tag + "xbt")])
                    p.dma("sync", s_negmu, rowsd[rowbank:rowbank + 1, tsl],
                          reads=[B("rw%d" % rowbank)], writes=[B(tag + "negmu")])
                    p.dma("sync", s_r, rowsd[rowbank + 1:rowbank + 2, tsl],
                          reads=[B("rw%d" % (rowbank + 1))],
                          writes=[B(tag + "r")])
                    p.emit("tensor", lambda t: t.matmul(
                        FE[:, :], ONESF[:, :], s_r, start=True, stop=True),
                        reads=[B(tag + "r"), B("ONESF")], writes=[B(tag + "FE")])
                    p.emit("scalar", lambda s: s.copy(RB512[:, :], FE[:, :]),
                           reads=[B(tag + "FE")], writes=[B(tag + "rb")],
                           waits=p.tick())

                    def mmq(t):
                        ins = None
                        for kc in range(32):
                            ins = t.matmul(FA[0:64, :],
                                           FWQ[:, kc * 64:(kc + 1) * 64],
                                           XB[:, kc * 512:(kc + 1) * 512],
                                           start=(kc == 0), stop=False)
                        ins = t.matmul(FA[0:64, :], CSF[0:1, 0:64], s_negmu,
                                       start=False, stop=True)
                        return ins
                    p.emit("tensor", mmq,
                           reads=[B(tag + "xbt"), B("FWQ"), B("csf"),
                                  B(tag + "negmu")],
                           writes=[B(tag + "FA")])
                    p.emit("vector", lambda v: v.tensor_mul(
                        QB[:, :], FA[0:64, :], RB512[0:64, :]),
                        reads=[B(tag + "FA"), B(tag + "rb")],
                        writes=[B(tag + "qb")], waits=p.tick())

                    p.emit("tensor", lambda t: t.matmul(
                        FB[0:48, :], FWR[:, :], QB[:, :], start=True, stop=True),
                        reads=[B(tag + "qb"), B("FWR")], writes=[B(tag + "FB")])
                    p.emit("scalar", lambda s: s.copy(RQ[:, :], FB[0:48, :]),
                           reads=[B(tag + "FB")], writes=[B(tag + "rq")],
                           waits=p.tick())
                    fill(48)

                    for sc in range(4):
                        csl = slice(sc * 128, (sc + 1) * 128)

                        def mmrout(t, _c=csl):
                            i = t.matmul(FC[:, 0:512], RQ[:, _c], RKBF[:, 0:512],
                                         start=True, stop=False)
                            i = t.matmul(FC[:, 0:512], ONESF[:, :],
                                         LR[0:1, 0:512], start=False, stop=True)
                            i = t.matmul(FC[:, 512:1024], RQ[:, _c],
                                         RKBF[:, 512:1024], start=True,
                                         stop=False)
                            i = t.matmul(FC[:, 512:1024], ONESF[:, :],
                                         LR[0:1, 512:1024], start=False,
                                         stop=True)
                            return i
                        p.emit("tensor", mmrout,
                               reads=[B(tag + "rq"), B("RKBF"), B("LRR"),
                                      B("ONESF")],
                               writes=[B(tag + "FC")])

                        def mmf(t, _c=csl):
                            i = t.matmul(FD[:, 0:512], QB[:, _c], AKT[:, 0:512],
                                         start=True, stop=False)
                            i = t.matmul(FD[:, 0:512], ONESF[:, :],
                                         LR[0:1, 0:512], start=False, stop=True)
                            i = t.matmul(FD[:, 512:1024], QB[:, _c],
                                         AKT[:, 512:1024], start=True,
                                         stop=False)
                            i = t.matmul(FD[:, 512:1024], ONESF[:, :],
                                         LR[0:1, 512:1024], start=False,
                                         stop=True)
                            return i
                        p.emit("tensor", mmf,
                               reads=[B(tag + "qb"), B("AKT"), B("LRR"),
                                      B("ONESF")],
                               writes=[B(tag + "FD")])

                        p.emit("scalar", lambda s: s.copy(RTS[:, :], FC[:, :]),
                               reads=[B(tag + "FC")], writes=[B(tag + "rts")],
                               waits=p.tick())
                        p.emit("scalar", lambda s: s.copy(FSS[:, :], FD[:, :]),
                               reads=[B(tag + "FD")], writes=[B(tag + "fss")],
                               waits=p.tick())
                        p.emit("vector", lambda v: v.max(out=T8[:, :],
                                                         in_=RTS[:, :]),
                               reads=[B(tag + "rts")], writes=[B(tag + "t8")])
                        p.emit("vector", lambda v: v.reduce_max(
                            NB2[:, 0:1], FSS[:, :], axis=AX, negate=True),
                            reads=[B(tag + "fss")], writes=[B(tag + "nb2")])
                        p.emit("vector", lambda v: v.tensor_scalar(
                            MSK8[:, :], RTS[:, :], T8[:, 7:8], None, ALU.is_ge),
                            reads=[B(tag + "t8"), B(tag + "rts")],
                            writes=[B(tag + "msk8")])
                        p.emit("vector", lambda v: v.select(
                            SELS[:, :], MSK8[:, :], FSS[:, :], NEGBB),
                            reads=[B(tag + "msk8"), B(tag + "fss"),
                                   B(tag + "negb")],
                            writes=[B(tag + "sels")])
                        p.emit("scalar", lambda s: s.activation(
                            PB[:, :], SELS[:, :], AF.Exp, bias=NB2[:, 0:1],
                            accum_out=SRC_[:, 0:1]),
                            reads=[B(tag + "sels"), B(tag + "nb2")],
                            writes=[B(tag + "pb"), B(tag + "srcol")])
                        p.emit("tensor", lambda t: t.transpose(
                            FC[0:1, 0:128], SRC_[:, 0:1], IDF),
                            reads=[B(tag + "srcol"), B("IDF")],
                            writes=[B(tag + "FC")])
                        p.emit("scalar", lambda s, _sc=sc: s.copy(
                            s_sr[0:1, _sc * 128:(_sc + 1) * 128],
                            FC[0:1, 0:128]),
                            reads=[B(tag + "FC")], writes=[B(tag + "srrow")],
                            waits=p.tick())

                        def mmtr(t):
                            ins = None
                            for j in range(8):
                                ins = t.transpose(FT[:, j * 128:(j + 1) * 128],
                                                  PB[:, j * 128:(j + 1) * 128],
                                                  IDB)
                            return ins
                        p.emit("tensor", mmtr, reads=[B(tag + "pb"), B("IDB")],
                               writes=[B(tag + "FT")])
                        p.emit("scalar", lambda s, _sc=sc: s.copy(
                            bass.AP(PTS, _sc * 128,
                                    [[4096, 128], [512, 8], [1, 128]]),
                            FT[:, :]),
                            reads=[B(tag + "FT")],
                            writes=[B(tag + "pts%d" % sc)],
                            waits=p.tick())

                    pts_reads = [B(tag + "pts%d" % s4) for s4 in range(4)]
                    for vc in range(2):
                        tgt, tgn = ((FA, tag + "FA") if vc == 0
                                    else (FB, tag + "FB"))

                        def mmas(t, _vc=vc, _tg=tgt):
                            ins = None
                            for n8 in range(8):
                                ins = t.matmul(
                                    _tg[:, :],
                                    APR[:, n8 * 256 + _vc * 128:
                                        n8 * 256 + _vc * 128 + 128],
                                    PTS[:, n8 * 512:(n8 + 1) * 512],
                                    start=(n8 == 0), stop=(n8 == 7))
                            return ins
                        p.emit("tensor", mmas, reads=pts_reads + [B("APR")],
                               writes=[B(tgn)])
                        p.emit("scalar", lambda s, _vc=vc, _tg=tgt: s.copy(
                            ASB[:, _vc * 512:(_vc + 1) * 512], _tg[:, :]),
                            reads=[B(tgn)], writes=[B(tag + "asb")],
                            waits=p.tick())

                    for cc in range(32):
                        tgt, tgn = ((FC, tag + "FC") if cc % 2 == 0
                                    else (FD, tag + "FD"))

                        def mmau(t, _cc=cc, _tg=tgt):
                            i = t.matmul(_tg[:, 0:512],
                                         WVU[:, _cc * 128:_cc * 128 + 128],
                                         ASB[:, 0:512], start=True, stop=False)
                            i = t.matmul(_tg[:, 0:512],
                                         WVU[:, 4096 + _cc * 128:
                                             4096 + _cc * 128 + 128],
                                         ASB[:, 512:1024], start=False,
                                         stop=True)
                            return i
                        p.emit("tensor", mmau, reads=[B(tag + "asb"), B("WVU")],
                               writes=[B(tgn)])
                        p.emit("scalar", lambda s, _tg=tgt: s.copy(
                            AXE[:, :], _tg[:, 0:512]),
                            reads=[B(tgn)], writes=[B(tag + "axe")],
                            waits=p.tick())
                        p.dma("sync",
                              auxd[cc * 128:(cc + 1) * 128, tsl],
                              AXE[:, :], reads=[B(tag + "axe")],
                              writes=[B("auxd")])
                    fill(48)

                    for uc in range(8):
                        ws = uc % 2
                        wb_ = B(tag + "w1%d" % ws)
                        p.dma("scalar", W1[:, ws * 4096:(ws + 1) * 4096],
                              wu1tb[:, uc * 4096:(uc + 1) * 4096],
                              reads=[B("wu1tb")], writes=[wb_])
                        if tile == 0 and prepass:
                            def mmw1cs(t, _ws=ws):
                                ins = None
                                for kc in range(32):
                                    ins = t.matmul(
                                        FE[0:1, 0:128], ONESB[:, :],
                                        W1[:, _ws * 4096 + kc * 128:
                                           _ws * 4096 + kc * 128 + 128],
                                        start=(kc == 0), stop=(kc == 31))
                                return ins
                            p.emit("tensor", mmw1cs, reads=[wb_, B("ONESB")],
                                   writes=[B(tag + "FE")])
                            p.emit("vector", lambda v, _uc=uc: v.tensor_copy(
                                CSF[0:1, 64 + _uc * 128:64 + (_uc + 1) * 128],
                                FE[0:1, 0:128]),
                                reads=[B(tag + "FE")], writes=[B("csf")],
                                waits=p.tick())
                        tgt, tgn = ((FA, tag + "FA") if uc % 2 == 0
                                    else (FB, tag + "FB"))

                        def mmu(t, _uc=uc, _tg=tgt, _ws=ws):
                            ins = None
                            for kc in range(32):
                                ins = t.matmul(
                                    _tg[:, :],
                                    W1[:, _ws * 4096 + kc * 128:
                                       _ws * 4096 + kc * 128 + 128],
                                    XB[:, kc * 512:(kc + 1) * 512],
                                    start=(kc == 0), stop=False)
                            ins = t.matmul(_tg[:, :],
                                           CSF[0:1, 64 + _uc * 128:
                                               64 + (_uc + 1) * 128],
                                           s_negmu, start=False, stop=True)
                            return ins
                        p.emit("tensor", mmu,
                               reads=[wb_, B(tag + "xbt"), B("csf"),
                                      B(tag + "negmu")],
                               writes=[B(tgn)])
                        p.emit("vector", lambda v, _tg=tgt: v.tensor_mul(
                            UF[:, :], _tg[:, :], RB512[:, :]),
                            reads=[B(tgn), B(tag + "rb")],
                            writes=[B(tag + "uf")], waits=p.tick())
                        p.emit("scalar", lambda s, _uc=uc: s.activation(
                            UB[:, :], UF[:, :], AF.Gelu_apprx_tanh,
                            bias=BO2[:, 32 + _uc:33 + _uc]),
                            reads=[B(tag + "uf"), B("BU1")],
                            writes=[B(tag + "ub")])
                        p.emit("tensor", lambda t, _uc=uc: t.matmul(
                            FE[0:1, :], W2S[:, _uc:_uc + 1], UB[:, :],
                            start=(_uc == 0), stop=(_uc == 7)),
                            reads=[B(tag + "ub"), B("W2S")],
                            writes=([B(tag + "FE")] if uc == 0 else []))
                    p.emit("vector", lambda v: v.tensor_scalar(
                        s_le, FE[0:1, :], SCAL[0:1, 0:1], None, ALU.add),
                        reads=[B(tag + "FE"), B("SCAL")],
                        writes=[B(tag + "le")], waits=p.tick())
                    fill(48)

                    if tile == 0:
                        p.dma("sync", LVST, lvg, reads=[B("lvg" + tag)],
                              writes=[B(tag + "lvst")])
                        p.emit("vector", lambda v: v.tensor_scalar(
                            MIV, ROWD[0:1, 8:9], 1.0 / NTOK_ALL, 1e-6,
                            ALU.mult, ALU.add),
                            reads=[B(tag + "lvst")], writes=[B(tag + "miv")])
                        p.emit("scalar", lambda s: s.activation(
                            MIV, MIV, AF.Ln),
                            reads=[B(tag + "miv")], writes=[B(tag + "miv")])
                        p.emit("scalar", lambda s: s.activation(
                            MIV, MIV, AF.Exp, scale=-1.0),
                            reads=[B(tag + "miv")], writes=[B(tag + "miv")])
                    p.emit("scalar", lambda s: s.activation(
                        s_g, s_le, AF.Tanh, scale=0.5),
                        reads=[B(tag + "le")], writes=[B(tag + "g")])
                    p.emit("vector", lambda v: v.tensor_scalar(
                        s_g, s_g, 0.5, 0.5, ALU.mult, ALU.add),
                        reads=[B(tag + "g")], writes=[B(tag + "g")])
                    p.dma("sync", s_lv, rowsd[rowbank + 7:rowbank + 8, tsl],
                          reads=[B("rw%d" % (rowbank + 7))],
                          writes=[B(tag + "lvr")])
                    p.emit("vector", lambda v: v.tensor_scalar(
                        s_lv, s_lv, MIV[0:1, 0:1], 0.5, ALU.mult, ALU.mult),
                        reads=[B(tag + "lvr"), B(tag + "miv")],
                        writes=[B(tag + "lvr")])
                    p.emit("vector", lambda v: v.scalar_tensor_tensor(
                        s_g, s_g, 2.5, s_lv, ALU.mult, ALU.add),
                        reads=[B(tag + "g"), B(tag + "lvr")],
                        writes=[B(tag + "g")])
                    p.emit("vector", lambda v: v.tensor_scalar(
                        s_g, s_g, 0.0, 5.0, ALU.max, ALU.min),
                        reads=[B(tag + "g")], writes=[B(tag + "g")])
                    p.emit("vector", lambda v: v.tensor_scalar(
                        s_g, s_g, -0.5, 1.0 / 1.5, ALU.add, ALU.mult),
                        reads=[B(tag + "g")], writes=[B(tag + "g")])
                    p.emit("vector", lambda v: v.tensor_scalar(
                        s_g, s_g, 0.0, 1.0, ALU.max, ALU.min),
                        reads=[B(tag + "g")], writes=[B(tag + "g")])
                    p.emit("vector", lambda v: v.tensor_scalar(
                        s_g, s_g, SCAL[0:1, 1:2], SCAL[0:1, 2:3],
                        ALU.mult, ALU.add),
                        reads=[B(tag + "g"), B("SCAL")], writes=[B(tag + "g")])
                    p.emit("scalar", lambda s: s.activation(
                        s_g, s_g, AF.Tanh, scale=0.5),
                        reads=[B(tag + "g")], writes=[B(tag + "g")])
                    p.emit("vector", lambda v: v.tensor_scalar(
                        s_g, s_g, 0.5, 0.5, ALU.mult, ALU.add),
                        reads=[B(tag + "g")], writes=[B(tag + "g")])
                    p.emit("vector", lambda v: v.tensor_scalar(
                        s_g2, s_g, 0.05, None, ALU.is_ge),
                        reads=[B(tag + "g")], writes=[B(tag + "g2")])
                    p.emit("vector", lambda v: v.tensor_mul(s_g, s_g, s_g2),
                           reads=[B(tag + "g2")], writes=[B(tag + "g")])
                    p.emit("scalar", lambda s: s.activation(s_g2, s_sr, AF.Ln),
                           reads=[B(tag + "srrow")], writes=[B(tag + "g2")])
                    p.emit("scalar", lambda s: s.activation(
                        s_g2, s_g2, AF.Exp, scale=-1.0),
                        reads=[B(tag + "g2")], writes=[B(tag + "g2")])
                    p.emit("vector", lambda v: v.scalar_tensor_tensor(
                        s_g, s_g, 0.5, s_g2, ALU.mult, ALU.mult),
                        reads=[B(tag + "g2"), B(tag + "g")],
                        writes=[B(tag + "g")])
                    p.emit("vector", lambda v: v.scalar_tensor_tensor(
                        s_b2, s_negmu, 0.5, s_r, ALU.mult, ALU.mult),
                        reads=[B(tag + "negmu"), B(tag + "r")],
                        writes=[B(tag + "b2")])
                    p.emit("vector", lambda v: v.tensor_scalar(
                        s_a, s_r, 0.5, 1.0, ALU.mult, ALU.add),
                        reads=[B(tag + "r")], writes=[B(tag + "a")])

                    for i, (nm, row) in enumerate((("gb", s_g), ("ab", s_a),
                                                   ("bb", s_b2))):
                        rb_ = B(tag + ("g" if i == 0 else ("a" if i == 1
                                                           else "b2")))
                        p.emit("tensor", lambda t, _r=row: t.matmul(
                            FE[:, :], ONESF[:, :], _r, start=True, stop=True),
                            reads=[rb_, B("ONESF")], writes=[B(tag + "FE")])
                        p.emit("scalar", lambda s, _i=i: s.copy(
                            GAB[:, _i * 512:(_i + 1) * 512], FE[:, :]),
                            reads=[B(tag + "FE")], writes=[B(tag + nm)],
                            waits=p.tick())

                    def b3(i, w):
                        return bass.AP(GAB, i * 512,
                                       [[1536, 128], [0, w // 512], [1, 512]])

                    def hex_ap(dram, tile, h):
                        return bass.AP(dram.tensor,
                                       (h * 2 * 128) * TOK + tile * 512,
                                       [[TOK, 128], [128 * TOK, 2], [1, 512]])

                    for h in range(16):
                        sl2 = h % 2
                        xt = (XT4 if sl2 == 0 else XT4B)[:, 0:1024]
                        ax = AX4[:, sl2 * 1024:(sl2 + 1) * 1024]
                        cf1 = CF1[:, sl2 * 1024:(sl2 + 1) * 1024]
                        cf2 = CF2[:, sl2 * 1024:(sl2 + 1) * 1024]
                        xtb = B(tag + "xt4%d" % sl2)
                        axb = B(tag + "ax4%d" % sl2)
                        c1b = B(tag + "cf1%d" % sl2)
                        c2b = B(tag + "cf2%d" % sl2)
                        p.dma("sync", xt, hex_ap(xsrc, tile, h),
                              reads=[xsrc_b], writes=[xtb])
                        p.dma("sync", ax, hex_ap(auxd, tile, h),
                              reads=[B("auxd")], writes=[axb])
                        p.emit("vector", lambda v, _a=ax, _c=cf1: v.tensor_tensor(
                            _c, _a, b3(0, 1024), ALU.mult),
                            reads=[axb, B(tag + "gb")], writes=[c1b])
                        p.emit("vector", lambda v, _x=xt, _c=cf2: v.tensor_tensor(
                            _c, _x, b3(1, 1024), ALU.mult),
                            reads=[xtb, B(tag + "ab")], writes=[c2b])
                        p.emit("vector", lambda v, _c1=cf1, _c2=cf2: v.tensor_add(
                            _c1, _c1, _c2),
                            reads=[c2b, c1b], writes=[c1b])
                        p.emit("vector", lambda v, _c1=cf1, _c2=cf2: v.tensor_tensor(
                            _c2, _c1, b3(2, 1024), ALU.add),
                            reads=[c1b, B(tag + "bb")], writes=[c2b])
                        p.dma("sync", hex_ap(dst, tile, h), cf2,
                              reads=[c2b], writes=[B(dst.tensor.name)])
                        if donext:
                            p.emit("gpsimd", lambda g, _a=ax, _c=cf2:
                                   g.tensor_copy(_a, _c),
                                   reads=[c2b], writes=[axb])
                            p.dma("sync", hex_ap(xbfd, tile, h), ax,
                                  reads=[axb], writes=[B("xbfd")])
                            p.emit("gpsimd", lambda g, _a=ax, _s=sl2:
                                   g.tensor_mul(
                                       SQ4[:, _s * 1024:(_s + 1) * 1024],
                                       _a, _a),
                                   reads=[axb], writes=[B(tag + "sq%d" % sl2)])
                            stats_chain2(h == 0, ax,
                                         SQ4[:, sl2 * 1024:(sl2 + 1) * 1024],
                                         axb, B(tag + "sq%d" % sl2))
                    if donext:
                        ln_rows(tile, 16, None, None)
                    fill(48)

        emit_ffn(x_in, x1d, rowbank=0, lvg=lv_g, lvc=lv_c, prepass=True,
                 tag="f1", filler=cast_filler)
        cast_drain()
        p.fence()

        with ExitStack() as s2:
            e2 = s2.enter_context
            QA = e2(nc.psum_tensor("QA", [128, 512], F32))
            QB_ = e2(nc.psum_tensor("QBp", [128, 512], F32))
            QE = e2(nc.psum_tensor("QE", [128, 512], F32))
            QS0 = e2(nc.psum_tensor("QS0", [128, 512], F32))
            QS1 = e2(nc.psum_tensor("QS1", [128, 512], F32))
            XBH = e2(nc.sbuf_tensor("XBH", [128, 32768], BF16))
            WQB = e2(nc.sbuf_tensor("WQBs", [128, 8192], BF16))
            WVH = e2(nc.sbuf_tensor("WVH", [128, 8192], BF16))
            QF = e2(nc.sbuf_tensor("QF", [128, 1024], F32))
            QSW = e2(nc.sbuf_tensor("QSW", [128, 1024], F32))
            QRO = e2(nc.sbuf_tensor("QRO", [128, 1024], BF16))
            QSQ = e2(nc.sbuf_tensor("QSQ", [128, 1024], BF16))
            TBC = e2(nc.sbuf_tensor("TBC", [128, 1024], F32))
            TBS = e2(nc.sbuf_tensor("TBS", [128, 1024], F32))
            R2B = e2(nc.sbuf_tensor("R2B", [128, 1024], F32))
            R2C = e2(nc.sbuf_tensor("R2C", [128, 8], F32))
            CSL = e2(nc.sbuf_tensor("CSL", [1, 128], F32))
            CSLV = e2(nc.sbuf_tensor("CSLV", [1, 256], F32))
            VBE = e2(nc.sbuf_tensor("VBE", [128, 256], BF16))
            QMX = e2(nc.sbuf_tensor("QMX", [128, 16], F32))
            STR = e2(nc.sbuf_tensor("STR", [1, 128], F32))
            ONEQK = e2(nc.sbuf_tensor("ONEQK", [128, 1280], BF16))
            ONEQ = ONEQK[:, 0:1024]
            ONEK = ONEQK[:, 1024:1280]
            p.dma("sync", ONEQK[:, 0:1024], oneq, writes=[B("ONEQ")])
            p.dma("sync", ONEQK[:, 1024:1280], onek, writes=[B("ONEK")])

            s2_negmu = ROWA[0:1, 0:1024]
            s2_r = ROWA[0:1, 1024:2048]

            p.emit("vector", lambda v: v.memset(QMX[:, 0:4], 0.0),
                   writes=[B("qmx")])

            for half in range(NH):
                hsl = slice(half * 1024, (half + 1) * 1024)
                p.dma("sync",
                      bass.AP(XBH, 0,
                              [[32768, 128], [1024, 32], [1, 1024]]),
                      bass.AP(xbfd.tensor, half * 1024,
                              [[TOK, 128], [128 * TOK, 32], [1, 1024]]),
                      reads=[B("xbfd")], writes=[B("xbh")])
                p.dma("sync", s2_negmu, rowsd[16:17, hsl],
                      reads=[B("rw16")], writes=[B("s2negmu")])
                p.dma("sync", s2_r, rowsd[17:18, hsl],
                      reads=[B("rw17")], writes=[B("s2r")])
                for c2 in range(2):
                    p.emit("tensor", lambda t, _c=c2: t.matmul(
                        QE[:, :], ONESF[:, :],
                        s2_r[0:1, _c * 512:(_c + 1) * 512],
                        start=True, stop=True),
                        reads=[B("s2r"), B("ONESF")], writes=[B("QE")])
                    p.emit("scalar", lambda s, _c=c2: s.copy(
                        R2B[:, _c * 512:(_c + 1) * 512], QE[:, :]),
                        reads=[B("QE")], writes=[B("r2b")], waits=p.tick())
                for tc in range(8):
                    p.emit("tensor", lambda t, _tc=tc: t.transpose(
                        QE[:, 0:1], s2_r[0:1, _tc * 128:(_tc + 1) * 128],
                        IDF[0:1, 0:1]),
                        reads=[B("s2r"), B("IDF")], writes=[B("QE")])
                    p.emit("scalar", lambda s, _tc=tc: s.copy(
                        R2C[:, _tc:_tc + 1], QE[:, 0:1]),
                        reads=[B("QE")], writes=[B("r2c")], waits=p.tick())
                p.dma("sync", TBC[:, :], tabc[:, hsl], writes=[B("tbc")])
                p.dma("sync", TBS[:, :], tabs[:, hsl], writes=[B("tbs")])
                p.emit("vector", lambda v: v.tensor_mul(
                    TBC[:, :], TBC[:, :], R2B[:, :]),
                    reads=[B("r2b"), B("tbc")], writes=[B("tbc")])
                p.emit("vector", lambda v: v.tensor_mul(
                    TBS[:, :], TBS[:, :], R2B[:, :]),
                    reads=[B("r2b"), B("tbs")], writes=[B("tbs")])

                for kind, nocs, wsrc, csrow, dst_c, onh, onb, qsb in (
                        ("q", 32, wqb, 0, qc_d, ONEQ, "ONEQ", QS0),
                        ("k", 8, wkb, 1, kc_d, ONEK, "ONEK", QS1)):
                    qsn = qsb.name
                    for oc in range(nocs):
                        ws = oc % 2
                        wb_ = B("wqb%d" % ws)
                        p.dma("scalar", WQB[:, ws * 4096:(ws + 1) * 4096],
                              wsrc[:, oc * 4096:(oc + 1) * 4096],
                              reads=[B("%s_c%d" % (wsrc.tensor.name, oc))],
                              writes=[wb_])

                        def mmcs(t, _ws=ws):
                            ins = None
                            for kc in range(32):
                                ins = t.matmul(
                                    QE[0:1, 0:128], ONESB[:, :],
                                    WQB[:, _ws * 4096 + kc * 128:
                                        _ws * 4096 + kc * 128 + 128],
                                    start=(kc == 0), stop=(kc == 31))
                            return ins
                        p.emit("tensor", mmcs, reads=[wb_, B("ONESB")],
                               writes=[B("QE")])
                        p.emit("scalar", lambda s: s.copy(CSL[0:1, :],
                                                          QE[0:1, 0:128]),
                               reads=[B("QE")], writes=[B("csl")],
                               waits=p.tick())
                        for tcol in range(2):
                            tgt, tgn = (QA, "QA") if tcol == 0 else (QB_, "QBp")

                            def mmp(t, _tg=tgt, _ws=ws, _tc=tcol):
                                ins = None
                                for kc in range(32):
                                    ins = t.matmul(
                                        _tg[:, :],
                                        WQB[:, _ws * 4096 + kc * 128:
                                            _ws * 4096 + kc * 128 + 128],
                                        XBH[:, kc * 1024 + _tc * 512:
                                            kc * 1024 + _tc * 512 + 512],
                                        start=(kc == 0), stop=False)
                                ins = t.matmul(
                                    _tg[:, :], CSL[0:1, :],
                                    s2_negmu[0:1, _tc * 512:(_tc + 1) * 512],
                                    start=False, stop=True)
                                return ins
                            p.emit("tensor", mmp,
                                   reads=[wb_, B("xbh"), B("csl"), B("s2negmu")],
                                   writes=[B(tgn)])
                            p.emit("scalar", lambda s, _tg=tgt, _tc=tcol: s.copy(
                                QF[:, _tc * 512:(_tc + 1) * 512], _tg[:, :]),
                                reads=[B(tgn)], writes=[B("qf")],
                                waits=p.tick())
                        p.emit("gpsimd", lambda g: g.tensor_copy(
                            QSW[0:64, :], QF[64:128, :]),
                            reads=[B("qf")], writes=[B("qsw")])
                        p.emit("gpsimd", lambda g: g.tensor_copy(
                            QSW[64:128, :], QF[0:64, :]),
                            reads=[B("qf")], writes=[B("qsw")])
                        p.emit("vector", lambda v: v.tensor_mul(
                            QF[:, :], QF[:, :], TBC[:, :]),
                            reads=[B("tbc"), B("qsw"), B("qf")],
                            writes=[B("qf")])
                        p.emit("vector", lambda v: v.tensor_mul(
                            QSW[:, :], QSW[:, :], TBS[:, :]),
                            reads=[B("tbs"), B("qsw")], writes=[B("qsw")])
                        p.emit("vector", lambda v: v.tensor_add(
                            QRO[:, :], QF[:, :], QSW[:, :]),
                            reads=[B("qf"), B("qsw")], writes=[B("qro")])
                        p.emit("gpsimd", lambda g: g.tensor_mul(
                            QSQ[:, :], QRO[:, :], QRO[:, :]),
                            reads=[B("qro")], writes=[B("qsq")])
                        for tcol in range(2):
                            p.emit("tensor", lambda t, _oc=oc, _tc=tcol,
                                   _on=onh, _qs=qsb: t.matmul(
                                       _qs[0:32, :] if _tc == 0 else _qs[32:64, :],
                                       _on[:, _oc * 32:(_oc + 1) * 32],
                                       QSQ[:, _tc * 512:(_tc + 1) * 512],
                                       start=(_oc == 0), stop=False,
                                       tile_position=((0, 0) if _tc == 0
                                                      else (0, 32))),
                                   reads=[B("qsq"), B(onb)],
                                   writes=([B(qsn)] if (oc == 0 and tcol == 0)
                                           else []))
                        p.dma("sync", dst_c[oc * 128:(oc + 1) * 128, hsl],
                              QRO[:, :], reads=[B("qro")],
                              writes=[B(dst_c.tensor.name)])

                    col0 = 0 if kind == "q" else 1
                    tk = p.tick()
                    p.emit("vector", lambda v, _qs=qsb, _c=col0: v.reduce_max(
                        QMX[0:32, 8 + _c:9 + _c], _qs[0:32, :], axis=AX),
                        reads=[B(qsn)], writes=[B("qmxr")], waits=tk)
                    p.emit("vector", lambda v, _qs=qsb, _c=col0: v.reduce_max(
                        QMX[32:64, 8 + _c:9 + _c], _qs[32:64, :], axis=AX),
                        reads=[B(qsn)], writes=[B("qmxr")], waits=tk)
                    p.emit("vector", lambda v, _c=col0: v.tensor_copy(
                        QMX[0:32, 10 + _c:11 + _c], QMX[32:64, 8 + _c:9 + _c]),
                        reads=[B("qmxr")], writes=[B("qmxr")])
                    p.emit("vector", lambda v, _c=col0: v.tensor_max(
                        QMX[0:32, 12 + _c:13 + _c], QMX[0:32, 8 + _c:9 + _c],
                        QMX[0:32, 10 + _c:11 + _c]),
                        reads=[B("qmxr")], writes=[B("qmxr")])
                    p.emit("vector", lambda v, _c=col0: v.tensor_max(
                        QMX[0:32, _c:_c + 1], QMX[0:32, _c:_c + 1],
                        QMX[0:32, 12 + _c:13 + _c]),
                        reads=[B("qmxr"), B("qmx")], writes=[B("qmx")])

                for dh in range(4):
                    p.dma("scalar", WVH[:, :],
                          wvb[:, dh * 8192:(dh + 1) * 8192],
                          reads=[B("wvb_c%d" % (dh * 2)),
                                 B("wvb_c%d" % (dh * 2 + 1))],
                          writes=[B("wvh")])

                    def mmcsv(t):
                        ins = None
                        for kc in range(32):
                            ins = t.matmul(QE[0:1, 0:256], ONESB[:, :],
                                           WVH[:, kc * 256:(kc + 1) * 256],
                                           start=(kc == 0), stop=(kc == 31))
                        return ins
                    p.emit("tensor", mmcsv, reads=[B("wvh"), B("ONESB")],
                           writes=[B("QE")])
                    p.emit("scalar", lambda s: s.copy(CSLV[0:1, :],
                                                      QE[0:1, 0:256]),
                           reads=[B("QE")], writes=[B("cslv")],
                           waits=p.tick())
                    for tc in range(8):
                        tgt, tgn = (QA, "QA") if tc % 2 == 0 else (QB_, "QBp")

                        def mmv(t, _tg=tgt, _tc=tc):
                            ins = None
                            for kc in range(32):
                                ins = t.matmul(
                                    _tg[:, 0:256],
                                    XBH[:, kc * 1024 + _tc * 128:
                                        kc * 1024 + _tc * 128 + 128],
                                    WVH[:, kc * 256:(kc + 1) * 256],
                                    start=(kc == 0), stop=False)
                            ins = t.matmul(
                                _tg[:, 0:256],
                                s2_negmu[0:1, _tc * 128:(_tc + 1) * 128],
                                CSLV[0:1, :], start=False, stop=True)
                            return ins
                        p.emit("tensor", mmv,
                               reads=[B("wvh"), B("xbh"), B("cslv"),
                                      B("s2negmu")],
                               writes=[B(tgn)])
                        p.emit("vector", lambda v, _tg=tgt, _tc=tc:
                               v.tensor_scalar(
                                   VBE[:, :], _tg[:, 0:256],
                                   R2C[:, _tc:_tc + 1], None, ALU.mult),
                               reads=[B(tgn), B("r2c")], writes=[B("vbe")],
                               waits=p.tick())
                        p.dma("sync",
                              vc_d[half * 1024 + tc * 128:
                                   half * 1024 + (tc + 1) * 128,
                                   dh * 256:(dh + 1) * 256],
                              VBE[:, :], reads=[B("vbe")], writes=[B("vc_d")])

            p.emit("tensor", lambda t: t.transpose(
                QE[0:1, 0:32], QMX[0:32, 0:1], IDF[0:32, 0:32]),
                reads=[B("qmx"), B("IDF")], writes=[B("QE")])
            p.emit("scalar", lambda s: s.copy(STR[0:1, 0:32], QE[0:1, 0:32]),
                   reads=[B("QE")], writes=[B("str")], waits=p.tick())
            p.emit("tensor", lambda t: t.transpose(
                QE[0:1, 0:32], QMX[0:32, 1:2], IDF[0:32, 0:32]),
                reads=[B("qmx"), B("IDF"), B("str")], writes=[B("QE")])
            p.emit("scalar", lambda s: s.copy(STR[0:1, 32:64], QE[0:1, 0:32]),
                   reads=[B("QE")], writes=[B("str")], waits=p.tick())
            p.dma("sync", st_c, STR[0:1, 0:64], reads=[B("str")],
                  writes=[B("st_c")])

            def coll_st(g):
                return g.collective_compute(
                    "AllReduce", ALU.max, replica_groups=groups_all,
                    ins=[st_c.opt()], outs=[st_g.opt()])
            p.emit("gpsimd", coll_st, reads=[B("st_c")], writes=[B("st_g")], semkey="gpsimdD")
            p.dma("sync", STR[0:1, 0:64], st_g, reads=[B("st_g")],
                  writes=[B("str")])
            p.emit("vector", lambda v: v.reduce_max(
                BQ, STR[0:1, 0:32], axis=AX),
                reads=[B("str")], writes=[B("bq")])
            p.emit("vector", lambda v: v.reduce_max(
                BK, STR[0:1, 32:64], axis=AX),
                reads=[B("str")], writes=[B("bk")])
            p.emit("vector", lambda v: v.tensor_mul(BB_, BQ, BK),
                   reads=[B("bq"), B("bk")], writes=[B("bbv")])
            p.emit("scalar", lambda s: s.activation(BB_, BB_, AF.Ln),
                   reads=[B("bbv")], writes=[B("bbv")])
            p.emit("scalar", lambda s: s.activation(
                BB_, BB_, AF.Exp, scale=0.5, bias=LN_CSC_MARGIN),
                reads=[B("bbv")], writes=[B("bbv")])
            p.emit("vector", lambda v: v.tensor_scalar(
                BB_, BB_, -1.0, None, ALU.mult),
                reads=[B("bbv")], writes=[B("bbv")])
            p.emit("tensor", lambda t: t.matmul(
                QE[:, 0:1], ONESF[:, :], BB_, start=True, stop=True),
                reads=[B("bbv"), B("ONESF")], writes=[B("QE")])
            p.emit("scalar", lambda s: s.copy(BIAS[:, 0:1], QE[:, 0:1]),
                   reads=[B("QE")], writes=[B("bias")], waits=p.tick())

            for nm, src, dst_ in (("q", qc_d, qg_d), ("k", kc_d, kg_d),
                                  ("v", vc_d, vg_d)):
                def coll(g, _s=src, _d=dst_):
                    return g.collective_compute(
                        "AllGather", ALU.bypass, replica_groups=groups_pair,
                        ins=[_s.opt()], outs=[_d.opt()])
                p.emit("gpsimd", coll, reads=[B(src.tensor.name)],
                       writes=[B(dst_.tensor.name)], semkey="gpsimdD")

        p.fence()

        with ExitStack() as s3:
            e3 = s3.enter_context
            ST0 = e3(nc.psum_tensor("ST0", [128, 1024], F32))
            ST1 = e3(nc.psum_tensor("ST1", [128, 1024], F32))
            OA = e3(nc.psum_tensor("OAp", [128, 512], F32))
            OB_ = e3(nc.psum_tensor("OBp", [128, 512], F32))
            SS = e3(nc.psum_tensor("SSp", [128, 512], F32))
            QH = e3(nc.sbuf_tensor("QH", [128, 2 * T], BF16))
            KH = e3(nc.sbuf_tensor("KH", [128, 2 * T], BF16))
            VH = e3(nc.sbuf_tensor("VH", [128, 2 * T], BF16))
            PTR = e3(nc.sbuf_tensor("PTR", [128, 4096], BF16))
            SACC = e3(nc.sbuf_tensor("SACC", [128, 1024], F32))
            SACB = e3(nc.sbuf_tensor("SACB", [128, 1024], BF16))
            RSB = e3(nc.sbuf_tensor("RSB", [128, 1024], F32))
            OEV = e3(nc.sbuf_tensor("OEV", [128, 1024], BF16))
            SRW = e3(nc.sbuf_tensor("SRW", [1, 1024], F32))

            rs = p.regs

            def dyn_dma(out_ap, tensor, base_reg, const_off, dims, reads, writes):
                def fn(engine):
                    engine.reg_alu(rs["rt"], rs[base_reg], const_off, ALU.add)
                    return engine.dma_start(
                        out=out_ap, in_=bass.AP(tensor, rs["rt"], dims))
                return p.emit("sync", fn, reads=reads, writes=writes,
                              inc=16, semkey="syncD")

            for r in range(2):
                dyn_dma(qm_d[:, r * TOK:(r + 1) * TOK], qg_d.tensor, "rq",
                        r * 4096 * TOK, [[TOK, 2048], [1, TOK]],
                        reads=[B("qg_d")], writes=[B("qm_d")])
                dyn_dma(km_d[:, r * TOK:(r + 1) * TOK], kg_d.tensor, "rk",
                        r * 1024 * TOK, [[TOK, 512], [1, TOK]],
                        reads=[B("kg_d")], writes=[B("km_d")])
                dyn_dma(vm_d[r * TOK:(r + 1) * TOK, :], vg_d.tensor, "rv",
                        r * TOK * 1024, [[1024, TOK], [1, 512]],
                        reads=[B("vg_d")], writes=[B("vm_d")])

            for gl in range(4):
                gs = gl % 2
                p.dma("sync", KH[:, gs * T:(gs + 1) * T],
                      km_d[gl * 128:(gl + 1) * 128, :],
                      reads=[B("km_d")], writes=[B("kh%d" % gs)])
                p.dma("sync",
                      bass.AP(VH, gs * T, [[2 * T, 128], [128, NKC], [1, 128]]),
                      bass.AP(vm_d.tensor, gl * 128,
                              [[512, 128], [128 * 512, NKC], [1, 128]]),
                      reads=[B("vm_d")], writes=[B("vh%d" % gs)])
                for h4 in range(4):
                    hl = gl * 4 + h4
                    qs_ = hl % 2
                    p.dma("sync", QH[:, qs_ * T:(qs_ + 1) * T],
                          qm_d[hl * 128:(hl + 1) * 128, :],
                          reads=[B("qm_d")], writes=[B("qh%d" % qs_)])
                    for qsb in range(NQSB):
                        C = 8 * qsb + 8
                        for c in range(C):
                            SP, spn = (ST0, "ST0") if c % 2 == 0 else (ST1, "ST1")
                            slot = c % 4
                            diag_i = c - 8 * qsb
                            lo = 128 * diag_i if diag_i >= 0 else 0

                            def mmst_(t, _SP=SP, _c=c, _lo=lo, _g=gs, _q=qs_,
                                      _qsb=qsb):
                                ins = None
                                for hf in range(2):
                                    a = max(_lo, hf * 512)
                                    bnd = (hf + 1) * 512
                                    if a >= bnd:
                                        continue
                                    ins = t.matmul(
                                        _SP[:, a:bnd],
                                        KH[:, _g * T + _c * 128:
                                           _g * T + (_c + 1) * 128],
                                        QH[:, _q * T + _qsb * 1024 + a:
                                           _q * T + _qsb * 1024 + bnd],
                                        start=True, stop=True)
                                return ins
                            p.emit("tensor", mmst_,
                                   reads=[B("kh%d" % gs), B("qh%d" % qs_)],
                                   writes=[B(spn)])
                            p.emit("scalar", lambda s, _SP=SP, _lo=lo, _sl=slot:
                                   s.activation(
                                       PTR[:, _sl * 1024 + _lo:_sl * 1024 + 1024],
                                       _SP[:, _lo:1024], AF.Exp,
                                       bias=BIAS[:, 0:1], scale=CSC),
                                   reads=[B(spn), B("bias")],
                                   writes=[B("ptr%d" % slot)],
                                   waits=p.tick())
                            if diag_i >= 0:
                                p.emit("vector", lambda v, _sl=slot, _lo=lo:
                                       v.tensor_mul(
                                           PTR[:, _sl * 1024 + _lo:
                                               _sl * 1024 + _lo + 128],
                                           PTR[:, _sl * 1024 + _lo:
                                               _sl * 1024 + _lo + 128],
                                           TRIB),
                                       reads=[B("ptr%d" % slot), B("TRI")],
                                       writes=[B("ptr%d" % slot)])

                            if c == 0:
                                p.emit("vector", lambda v, _sl=slot:
                                       v.tensor_copy(
                                           SACC[:, :],
                                           PTR[:, _sl * 1024:
                                               _sl * 1024 + 1024]),
                                       reads=[B("ptr%d" % slot)],
                                       writes=[B("sacc")])
                            else:
                                p.emit("vector", lambda v, _sl=slot, _lo=lo:
                                       v.tensor_tensor(
                                           SACC[:, _lo:1024],
                                           SACC[:, _lo:1024],
                                           PTR[:, _sl * 1024 + _lo:
                                               _sl * 1024 + 1024],
                                           ALU.add),
                                       reads=[B("ptr%d" % slot), B("sacc")],
                                       writes=[B("sacc")])

                            def mmpv(t, _c=c, _lo=lo, _sl=slot, _g=gs,
                                     _st=(c == 0)):
                                ins = None
                                for hf in range(2):
                                    a = max(_lo, hf * 512)
                                    bnd = (hf + 1) * 512
                                    if a >= bnd:
                                        continue
                                    OP = OA if hf == 0 else OB_
                                    ins = t.matmul(
                                        OP[:, a - hf * 512:bnd - hf * 512],
                                        VH[:, _g * T + _c * 128:
                                           _g * T + (_c + 1) * 128],
                                        PTR[:, _sl * 1024 + a:_sl * 1024 + bnd],
                                        start=_st, stop=False)
                                return ins
                            p.emit("tensor", mmpv,
                                   reads=[B("ptr%d" % slot), B("vh%d" % gs)],
                                   writes=([B("OAp"), B("OBp")]
                                           if c == 0 else []))

                        p.emit("vector", lambda v: v.tensor_copy(
                            SACB[:, :], SACC[:, :]),
                            reads=[B("sacc")], writes=[B("sacb")])

                        def mmss(t):
                            i = t.matmul(SS[0:1, :], ONESB[:, :],
                                         SACB[:, 0:512], start=True, stop=True)
                            i = t.matmul(SS[32:33, :], ONESB[:, :],
                                         SACB[:, 512:1024], start=True,
                                         stop=True, tile_position=(0, 32))
                            return i
                        p.emit("tensor", mmss, reads=[B("sacb"), B("ONESB")],
                               writes=[B("SSp")])
                        tk = p.tick()
                        p.emit("scalar", lambda s: s.copy(
                            SRW[0:1, 0:512], SS[0:1, :]),
                            reads=[B("SSp")], writes=[B("srw")], waits=tk)
                        p.emit("scalar", lambda s: s.copy(
                            SRW[0:1, 512:1024], SS[32:33, :]),
                            reads=[B("SSp")], writes=[B("srw")], waits=tk)
                        p.emit("scalar", lambda s: s.activation(
                            SRW[0:1, :], SRW[0:1, :], AF.Ln),
                            reads=[B("srw")], writes=[B("srw")])
                        p.emit("scalar", lambda s: s.activation(
                            SRW[0:1, :], SRW[0:1, :], AF.Exp, scale=-1.0),
                            reads=[B("srw")], writes=[B("srw")])
                        for c2 in range(2):
                            p.emit("tensor", lambda t, _c=c2: t.matmul(
                                ST0[:, _c * 512:(_c + 1) * 512], ONESF[:, :],
                                SRW[0:1, _c * 512:(_c + 1) * 512],
                                start=True, stop=True),
                                reads=[B("srw"), B("ONESF")], writes=[B("ST0")])
                        p.emit("scalar", lambda s: s.copy(RSB[:, :], ST0[:, :]),
                               reads=[B("ST0")], writes=[B("rsb")],
                               waits=p.tick())
                        p.emit("vector", lambda v: v.tensor_mul(
                            OEV[:, 0:512], OA[:, :], RSB[:, 0:512]),
                            reads=[B("OAp"), B("rsb")], writes=[B("oev")])
                        p.emit("vector", lambda v: v.tensor_mul(
                            OEV[:, 512:1024], OB_[:, :], RSB[:, 512:1024]),
                            reads=[B("OBp"), B("rsb")], writes=[B("oev")])
                        p.dma("sync",
                              oc_d[hl * 128:(hl + 1) * 128,
                                   qsb * 1024:(qsb + 1) * 1024],
                              OEV[:, :], reads=[B("oev")], writes=[B("oc_d")])

            def coll_o(g):
                return g.collective_compute(
                    "AllGather", ALU.bypass, replica_groups=groups_pair,
                    ins=[oc_d.opt()], outs=[og_d.opt()])
            p.emit("gpsimd", coll_o, reads=[B("oc_d")], writes=[B("og_d")], semkey="gpsimdD")

        p.fence()

        with ExitStack() as s4:
            e4 = s4.enter_context
            WA_ = e4(nc.psum_tensor("WAp", [128, 512], F32))
            WB2 = e4(nc.psum_tensor("WBp", [128, 512], F32))
            SX = e4(nc.psum_tensor("SXp", [128, 512], F32))
            SX2 = e4(nc.psum_tensor("SX2p", [128, 512], F32))
            OGH = e4(nc.sbuf_tensor("OGH", [128, 32768], BF16))
            WOB = e4(nc.sbuf_tensor("WOBs", [128, 8192], BF16))
            X1C = e4(nc.sbuf_tensor("X1C", [128, 1024], F32))
            X2C = e4(nc.sbuf_tensor("X2C", [128, 1024], F32))
            X2B = e4(nc.sbuf_tensor("X2B", [128, 1024], BF16))
            SQ2 = e4(nc.sbuf_tensor("SQ2", [128, 1024], BF16))

            s4_negmu = ROWA[0:1, 0:512]
            s4_r = ROWA[0:1, 512:1024]
            s4_t1 = ROWA[0:1, 1024:1536]
            s4_t2 = ROWA[0:1, 1536:2048]
            s4_v = ROWA[0:1, 2048:2560]
            p.emit("vector", lambda v: v.memset(LVACC2, 0.0), writes=[B("lvacc2")])

            rs = p.regs
            for r4 in range(4):
                def omdma(engine, _r=r4):
                    engine.reg_alu(rs["rt"], rs["ro"], _r * 1024 * T, ALU.add)
                    return engine.dma_start(
                        out=om_d[_r * 1024:(_r + 1) * 1024, :],
                        in_=bass.AP(og_d.tensor, rs["rt"],
                                    [[T, 1024], [1, TOK]]))
                p.emit("sync", omdma, reads=[B("og_d")], writes=[B("om_d")],
                       inc=16, semkey="syncD")
            for half in range(NH):
                p.dma("sync",
                      bass.AP(OGH, 0, [[32768, 128], [1024, 32], [1, 1024]]),
                      bass.AP(om_d.tensor, half * 1024,
                              [[TOK, 128], [128 * TOK, 32], [1, 1024]]),
                      reads=[B("om_d")], writes=[B("ogh")])
                for cc in range(32):
                    ws = cc % 2
                    wb_ = B("wob%d" % ws)
                    p.dma("scalar", WOB[:, ws * 4096:(ws + 1) * 4096],
                          wob[:, cc * 4096:(cc + 1) * 4096],
                          reads=[B("wob_c%d" % cc)], writes=[wb_])
                    p.dma("scalar", X1C[:, :],
                          bass.AP(x1d.tensor, cc * 128 * TOK + half * 1024,
                                  [[TOK, 128], [1, 1024]]),
                          reads=[B("x1d")], writes=[B("x1c")])
                    for tcol in range(2):
                        tgt, tgn = (WA_, "WAp") if tcol == 0 else (WB2, "WBp")

                        def mmo(t, _tg=tgt, _ws=ws, _tc=tcol):
                            ins = None
                            for hd in range(32):
                                ins = t.matmul(
                                    _tg[:, :],
                                    WOB[:, _ws * 4096 + hd * 128:
                                        _ws * 4096 + hd * 128 + 128],
                                    OGH[:, hd * 1024 + _tc * 512:
                                        hd * 1024 + _tc * 512 + 512],
                                    start=(hd == 0), stop=(hd == 31))
                            return ins
                        p.emit("tensor", mmo, reads=[wb_, B("ogh")],
                               writes=[B(tgn)])
                        p.emit("vector", lambda v, _tg=tgt, _tc=tcol, _cc=cc:
                               v.scalar_tensor_tensor(
                                   X2C[:, _tc * 512:(_tc + 1) * 512], _tg[:, :],
                                   BO2[:, _cc:_cc + 1],
                                   X1C[:, _tc * 512:(_tc + 1) * 512],
                                   ALU.add, ALU.add),
                               reads=[B(tgn), B("x1c"), B("BO2")],
                               writes=[B("x2c")], waits=p.tick())
                    p.dma("sync",
                          bass.AP(x2d.tensor, cc * 128 * TOK + half * 1024,
                                  [[TOK, 128], [1, 1024]]),
                          X2C[:, :], reads=[B("x2c")], writes=[B("x2d")])
                    p.emit("gpsimd", lambda g: g.tensor_copy(X2B[:, :],
                                                             X2C[:, :]),
                           reads=[B("x2c")], writes=[B("x2b")])
                    p.dma("sync",
                          bass.AP(xbfd.tensor, cc * 128 * TOK + half * 1024,
                                  [[TOK, 128], [1, 1024]]),
                          X2B[:, :], reads=[B("x2b")], writes=[B("xbfd")])
                    p.emit("gpsimd", lambda g: g.tensor_mul(
                        SQ2[:, :], X2B[:, :], X2B[:, :]),
                        reads=[B("x2b")], writes=[B("sq2")])

                    def mmsx(t, _cc=cc):
                        ins = None
                        for tc in range(2):
                            st = (_cc == 0)
                            tp = (0, 0) if tc == 0 else (0, 32)
                            tsl = slice(tc * 512, (tc + 1) * 512)
                            ins = t.matmul(
                                SX[0:1, :] if tc == 0 else SX[32:33, :],
                                ONESB[:, :], X2B[:, tsl],
                                start=st, stop=False, tile_position=tp)
                            ins = t.matmul(
                                SX2[0:1, :] if tc == 0 else SX2[32:33, :],
                                ONESB[:, :], SQ2[:, tsl],
                                start=st, stop=False, tile_position=tp)
                        return ins
                    p.emit("tensor", mmsx,
                           reads=[B("x2b"), B("sq2"), B("ONESB")],
                           writes=([B("SXp"), B("SX2p")] if cc == 0 else []))

                for tc in range(2):
                    tile = half * 2 + tc
                    tsl = slice(tile * 512, (tile + 1) * 512)
                    srcS = SX[0:1, :] if tc == 0 else SX[32:33, :]
                    srcS2 = SX2[0:1, :] if tc == 0 else SX2[32:33, :]
                    tk = p.tick()
                    p.emit("vector", lambda v, _s=srcS: v.tensor_scalar(
                        s4_negmu, _s, -1.0 / 4096.0, None, ALU.mult),
                        reads=[B("SXp")], writes=[B("s4negmu")], waits=tk)
                    p.emit("vector", lambda v, _s=srcS2: v.tensor_scalar(
                        s4_t1, _s, 1.0 / 4096.0, None, ALU.mult),
                        reads=[B("SX2p")], writes=[B("s4t1")], waits=tk)
                    p.emit("vector", lambda v: v.tensor_mul(
                        s4_t2, s4_negmu, s4_negmu),
                        reads=[B("s4negmu")], writes=[B("s4t2")])
                    p.emit("vector", lambda v: v.tensor_sub(s4_v, s4_t1, s4_t2),
                           reads=[B("s4t1"), B("s4t2")], writes=[B("s4v")])
                    p.emit("scalar", lambda s: s.activation(
                        s4_t1, s4_v, AF.Ln, bias=EPS),
                        reads=[B("s4v")], writes=[B("s4t1")])
                    p.emit("scalar", lambda s: s.activation(
                        s4_r, s4_t1, AF.Exp, scale=-0.5),
                        reads=[B("s4t1")], writes=[B("s4r")])
                    p.dma("sync", rowsd[8:9, tsl], s4_negmu,
                          reads=[B("s4negmu")], writes=[B("rw8")])
                    p.dma("sync", rowsd[9:10, tsl], s4_r,
                          reads=[B("s4r")], writes=[B("rw9")])
                    p.emit("vector", lambda v: v.tensor_mul(s4_t2, s4_r, s4_r),
                           reads=[B("s4r")], writes=[B("s4t2")])
                    p.emit("vector", lambda v: v.tensor_mul(s4_t1, s4_v, s4_t2),
                           reads=[B("s4v"), B("s4t2")], writes=[B("s4t1")])
                    p.emit("scalar", lambda s: s.activation(
                        s4_t2, s4_t1, AF.Ln, bias=1.0),
                        reads=[B("s4t1")], writes=[B("s4t2")])
                    p.dma("sync", rowsd[15:16, tsl], s4_t2,
                          reads=[B("s4t2")], writes=[B("rw15")])
                    p.emit("vector", lambda v: v.reduce_sum(
                        LVTMP2, s4_t2, axis=AX),
                        reads=[B("s4t2")], writes=[B("lvt2")])
                    p.emit("vector", lambda v: v.tensor_add(
                        LVACC2, LVACC2, LVTMP2),
                        reads=[B("lvt2"), B("lvacc2")], writes=[B("lvacc2")])

            p.emit("vector", lambda v: v.memset(LVST2, 0.0), writes=[B("lvst2")])
            p.emit("vector", lambda v: v.tensor_copy(
                ROWD[0:1, 32:33], LVACC2),
                reads=[B("lvacc2")], writes=[B("lvst2")])
            p.dma("sync", lv_c2, LVST2, reads=[B("lvst2")], writes=[B("lv_c2")])

            def coll_lv2(g):
                return g.collective_compute(
                    "AllReduce", ALU.add, replica_groups=groups_all,
                    ins=[lv_c2.opt()], outs=[lv_g2.opt()])
            p.emit("gpsimd", coll_lv2, reads=[B("lv_c2")], writes=[B("lvgf2")], semkey="gpsimdD")

        p.fence()

        emit_ffn(x2d, y_out, rowbank=8, lvg=lv_g2, lvc=lv_c2, prepass=False,
                 tag="f2")

        for name in dbg:
            src = io[name]
            dst = io["dbg_" + name]
            p.dma("sync", dst, src, reads=[B(name)], writes=[B("dbg_" + name)])

        snap = [(k, p.cnt[k]) for k in SEMKEYS if p.cnt[k] > 0]
        p.emit("sync", lambda s: s.nop(), waits=snap, inc=1)

        with nc.Block() as block:
            p.replay(block)

    return nc



def _rope_tables(half, TOK):
    pos = np.arange(half * TOK, (half + 1) * TOK, dtype=np.float64)
    freqs = 1.0 / (10000.0 ** (np.arange(0, 128, 2, dtype=np.float64) / 128.0))
    ang = freqs[:, None] * pos[None, :]
    c = np.cos(ang).astype(np.float32)
    s = np.sin(ang).astype(np.float32)
    tabc = np.concatenate([c, c], axis=0)
    tabs = np.concatenate([-s, s], axis=0)
    return np.ascontiguousarray(tabc), np.ascontiguousarray(tabs)


def _pmajor(W_T, nblk_out, dblk):
    f, n = W_T.shape
    assert f == 4096 and n == nblk_out * dblk
    a = W_T.reshape(32, 128, nblk_out, dblk)
    return np.ascontiguousarray(a.transpose(1, 2, 0, 3).reshape(128, -1))


def _perm_deinterleave(ncols):
    perm = np.zeros(ncols, dtype=np.int64)
    for h in range(ncols // 128):
        b = h * 128
        perm[b:b + 64] = b + np.arange(64) * 2
        perm[b + 64:b + 128] = b + np.arange(64) * 2 + 1
    return perm


def prepare_inputs(inputs, npairs, T):
    TOK = T // 2
    bf = ml_dtypes.bfloat16
    x = np.asarray(inputs["L0_attn_x"], dtype=np.float32)

    shared = dict(
        awq=_pmajor(
            np.asarray(inputs["L0_attn_Wq"], np.float32)[:,
                                                         _perm_deinterleave(4096)],
            32, 128),
        awk=_pmajor(
            np.asarray(inputs["L0_attn_Wk"], np.float32)[:,
                                                         _perm_deinterleave(1024)],
            8, 128),
        awv=_pmajor(np.asarray(inputs["L0_attn_Wv"], np.float32), 4, 256),
        awo=_pmajor(np.asarray(inputs["L0_attn_Wo"], np.float32), 32, 128),
        fwqt=np.ascontiguousarray(np.asarray(inputs["L0_ffn_Wq"], np.float32).T),
        fwrt=np.ascontiguousarray(np.asarray(inputs["L0_ffn_Wr"], np.float32).T),
        akt=np.ascontiguousarray(
            np.asarray(inputs["L0_ffn_aux_keys"], np.float32).T),
        avt=_pmajor(np.asarray(inputs["L0_ffn_aux_values"], np.float32).T,
                    8, 128),
        wvdt=np.ascontiguousarray(np.asarray(inputs["L0_ffn_Wvd"], np.float32).T),
        wvut=np.ascontiguousarray(np.asarray(inputs["L0_ffn_Wvu"], np.float32).T),
        wu1t=_pmajor(np.asarray(inputs["L0_ffn_Wu1"], np.float32).T, 8, 128),
        wu2t=np.ascontiguousarray(np.asarray(inputs["L0_ffn_Wu2"], np.float32).T),
        bo2=np.ascontiguousarray(
            np.asarray(inputs["L0_attn_bo"], np.float32).reshape(32, 128).T),
        bu12=np.ascontiguousarray(
            np.asarray(inputs["L0_ffn_bu1"], np.float32).reshape(8, 128).T),
        scal=np.array([[float(np.asarray(inputs["L0_ffn_bu2"]).ravel()[0]),
                        float(np.asarray(inputs["L0_ffn_gate_w1"])),
                        float(np.asarray(inputs["L0_ffn_gate_bias"])), 0.0]],
                      np.float32),
        logrel=np.ascontiguousarray(
            np.asarray(inputs["L0_ffn_log_rel"], np.float32).reshape(1, 1024)),
        identf=np.eye(128, dtype=np.float32),
        identb=np.eye(128, dtype=np.float32).astype(bf),
        tri=np.ascontiguousarray(
            (np.arange(128)[:, None] <= np.arange(128)[None, :])
            .astype(np.float32)).astype(bf),
    )
    onq = np.zeros((128, 1024), np.float32)
    for h in range(32):
        onq[:, h * 32 + h] = 1.0
    onk = np.zeros((128, 256), np.float32)
    for g in range(8):
        onk[:, g * 32 + 4 * g:g * 32 + 4 * g + 4] = 1.0
    shared["oneq"] = onq.astype(bf)
    shared["onek"] = onk.astype(bf)

    in_maps = []
    for core in range(2 * npairs):
        b, half = core // 2, core % 2
        m = dict(shared)
        m["x"] = np.ascontiguousarray(
            x[b, half * TOK:(half + 1) * TOK, :].T.astype(np.float32))
        tc, ts = _rope_tables(half, TOK)
        m["tabc"] = tc
        m["tabs"] = ts
        m["pari"] = np.full((1, 1), half, np.int32)
        in_maps.append(m)
    return in_maps


def kernel(**inputs):
    npairs, T = 4, 4096
    TOK = T // 2
    in_maps = prepare_inputs(inputs, npairs, T)
    nc = build_program(npairs, T)
    res = run_bass_kernel_spmd(nc, in_maps, core_ids=list(range(2 * npairs)))
    out = np.empty((npairs, T, 4096), dtype=np.float32)
    for core in range(2 * npairs):
        b, half = core // 2, core % 2
        out[b, half * TOK:(half + 1) * TOK, :] = res.results[core]["y"].T
    return out


# version 05
# speedup vs baseline: 1.1029x; 1.0142x over previous
import numpy as np
import ml_dtypes
from contextlib import ExitStack

import concourse.bass as bass
import concourse.mybir as mybir
from concourse.bass_utils import run_bass_kernel_spmd

F32 = mybir.dt.float32
BF16 = mybir.dt.bfloat16
I32 = mybir.dt.int32
AF = mybir.ActivationFunctionType
ALU = mybir.AluOpType
AX = mybir.AxisListType.X

ENGINES = ("sync", "vector", "scalar", "tensor", "gpsimd")
NEGBIG = -1.0e30
EPS = 1e-5
CSC = float(128.0 ** -0.5)
LN_CSC_MARGIN = float(np.log(1.02 * CSC))


class Buf:
    __slots__ = ("ready", "users", "name")

    def __init__(self, name=""):
        self.ready = None
        self.users = []
        self.name = name


SEMKEYS = list(ENGINES) + [e + "D" for e in ENGINES]


class Prog:
    def __init__(self, nc, ctx):
        self.nc = nc
        self.q = {e: [] for e in ENGINES}
        self.sem = {k: ctx.enter_context(nc.semaphore("S" + k)) for k in SEMKEYS}
        self.cnt = {k: 0 for k in SEMKEYS}
        self.clock = {e: {k: 0 for k in SEMKEYS} for e in ENGINES}
        self.regs = {}
        self.bufs = {}

    def B(self, name):
        if name not in self.bufs:
            self.bufs[name] = Buf(name)
        return self.bufs[name]

    def emit(self, eng, fn, reads=(), writes=(), waits=(), inc=1, semkey=None):
        key = semkey or eng
        need = {}

        def add(t):
            if t is None:
                return
            f, v = t
            if v > need.get(f, 0):
                need[f] = v

        for b in reads:
            add(b.ready)
        for b in writes:
            add(b.ready)
            for t in b.users:
                add(t)
        for t in waits:
            add(t)

        cl = self.clock[eng]
        real = tuple((f, v) for f, v in need.items() if v > cl[f])
        for f, v in real:
            cl[f] = v

        self.cnt[key] += inc
        ticket = (key, self.cnt[key])
        sems = self.sem
        my_sem = sems[key]

        def g(engine, _fn=fn, _real=real, _inc=inc, _sem=my_sem):
            for f, v in _real:
                engine.wait_ge(sems[f], v)
            ins = _fn(engine)
            if _inc:
                ins.then_inc(_sem, _inc)

        self.q[eng].append(g)
        if inc:
            for b in reads:
                b.users.append(ticket)
            for b in writes:
                b.ready = ticket
                b.users = []
        return ticket

    def dma(self, eng, out_ap, in_ap, reads=(), writes=(), waits=(), slow=False):
        def fn(engine, _o=out_ap, _i=in_ap, _s=slow):
            return engine.dma_start(out=_o, in_=_i, allow_slow_non_contiguous=_s)
        return self.emit(eng, fn, reads=reads, writes=writes, waits=waits,
                         inc=16, semkey=eng + "D")

    def tick(self, eng="tensor"):
        return [(eng, self.cnt[eng])]

    def fence(self):
        snap = [(k, self.cnt[k]) for k in SEMKEYS if self.cnt[k] > 0]
        for e in ENGINES:
            self.emit(e, lambda engine: engine.nop(), waits=snap, inc=1)

    def replay(self, block):
        for e in ENGINES:
            qe = self.q[e]
            if not qe:
                continue

            def body(engine, _qe=qe):
                for f in _qe:
                    f(engine)

            getattr(block, e)(body)


def build_program(npairs, T, dbg=()):
    TOK = T // 2
    NT = TOK // 512
    NQSB = T // 1024
    NKC = T // 128
    NH = TOK // 1024
    NCORE = 2 * npairs
    NTOK_ALL = npairs * T

    nc = bass.Bass("TRN2", target_bir_lowering=False)
    ctx = ExitStack()
    io = {}

    def din(name, shape, dt=F32):
        io[name] = nc.dram_tensor(name, list(shape), dt, kind="ExternalInput").ap()
        return io[name]

    def dout(name, shape, dt=F32):
        io[name] = nc.dram_tensor(name, list(shape), dt, kind="ExternalOutput").ap()
        return io[name]

    def dint(name, shape, dt=F32):
        io[name] = nc.dram_tensor(name, list(shape), dt, kind="Internal").ap()
        return io[name]

    x_in = din("x", (4096, TOK))
    y_out = dout("y", (4096, TOK))
    awq = din("awq", (128, 131072))
    awk = din("awk", (128, 32768))
    awv = din("awv", (128, 32768))
    awo = din("awo", (128, 131072))
    fwqt = din("fwqt", (4096, 64))
    fwrt = din("fwrt", (64, 48))
    akt = din("akt", (64, 1024))
    avt = din("avt", (128, 32768))
    wvdt = din("wvdt", (4096, 256))
    wvut = din("wvut", (256, 4096))
    wu1t = din("wu1t", (128, 32768))
    wu2t = din("wu2t", (1024, 1))
    bo2 = din("bo2", (128, 32))
    bu12 = din("bu12", (128, 8))
    scal = din("scal", (1, 4))
    logrel = din("logrel", (1, 1024))
    tabc = din("tabc", (128, TOK))
    tabs = din("tabs", (128, TOK))
    identf = din("identf", (128, 128))
    identb = din("identb", (128, 128), BF16)
    tri = din("tri", (128, 128), BF16)
    oneq = din("oneq", (128, 1024), BF16)
    onek = din("onek", (128, 256), BF16)
    pari = din("pari", (1, 1), I32)

    x1d = dint("x1d", (4096, TOK))
    x2d = dint("x2d", (4096, TOK))
    xbfd = dint("xbfd", (4096, TOK), BF16)
    auxd = dint("auxd", (4096, TOK), BF16)
    rowsd = dint("rowsd", (32, TOK))
    wqb = dint("wqb", (128, 131072), BF16)
    wkb = dint("wkb", (128, 32768), BF16)
    wvb = dint("wvb", (128, 32768), BF16)
    wob = dint("wob", (128, 131072), BF16)
    fwqtb = dint("fwqtb", (4096, 64), BF16)
    fwrtb = dint("fwrtb", (64, 48), BF16)
    aktb = dint("aktb", (64, 1024), BF16)
    avtb = dint("avtb", (128, 32768), BF16)
    wvdtb = dint("wvdtb", (4096, 256), BF16)
    wvutb = dint("wvutb", (256, 4096), BF16)
    wu1tb = dint("wu1tb", (128, 32768), BF16)
    wu2tb = dint("wu2tb", (1024, 1), BF16)
    apd = dint("apd", (1024, 256), BF16)
    colsd = dint("colsd", (4, 4096))
    qc_d = dint("qc_d", (4096, TOK), BF16)
    qg_d = dint("qg_d", (2 * 4096, TOK), BF16)
    kc_d = dint("kc_d", (1024, TOK), BF16)
    kg_d = dint("kg_d", (2 * 1024, TOK), BF16)
    vc_d = dint("vc_d", (TOK, 1024), BF16)
    vg_d = dint("vg_d", (2 * TOK, 1024), BF16)
    oc_d = dint("oc_d", (2048, T), BF16)
    og_d = dint("og_d", (4096, T), BF16)
    qm_d = dint("qm_d", (2048, T), BF16)
    km_d = dint("km_d", (512, T), BF16)
    vm_d = dint("vm_d", (T, 512), BF16)
    om_d = dint("om_d", (4096, TOK), BF16)
    st_c = dint("st_c", (1, 64))
    st_g = dint("st_g", (1, 64))
    lv_c = dint("lv_c", (1, 8))
    lv_g = dint("lv_g", (1, 8))
    lv_c2 = dint("lv_c2", (1, 8))
    lv_g2 = dint("lv_g2", (1, 8))

    for name in dbg:
        src = io[name]
        dout("dbg_" + name, list(src.tensor.shape), src.tensor.dtype)

    groups_pair = [[2 * i, 2 * i + 1] for i in range(npairs)]
    groups_all = [list(range(NCORE))]

    with ctx:
        p = Prog(nc, ctx)
        B = p.B
        e = ctx.enter_context

        CONST = e(nc.sbuf_tensor("CONST", [128, 256], F32))
        IDF = CONST[:, 0:128]
        ONESF = CONST[0:1, 128:256]
        CONSTB = e(nc.sbuf_tensor("CONSTB", [128, 384], BF16))
        IDB = CONSTB[:, 0:128]
        TRIB = CONSTB[:, 128:256]
        ONESB = CONSTB[:, 256:257]
        BO2 = e(nc.sbuf_tensor("BO2", [128, 48], F32))
        SCAL = e(nc.sbuf_tensor("SCAL", [1, 16], F32))
        LR = e(nc.sbuf_tensor("LRR", [1, 1024], F32))
        PTI = e(nc.sbuf_tensor("PTI", [1, 8], I32))
        BIAS = e(nc.sbuf_tensor("BIAS", [128, 8], F32))
        ROWA = e(nc.sbuf_tensor("ROWA", [1, 4096], F32))
        ROWB = e(nc.sbuf_tensor("ROWB", [1, 4096], F32))
        ROWD = e(nc.sbuf_tensor("ROWD", [1, 128], F32))
        RKBF = e(nc.sbuf_tensor("RKBF", [48, 1024], BF16))
        FWQ = e(nc.sbuf_tensor("FWQ", [128, 2048], BF16))
        FWR = e(nc.sbuf_tensor("FWR", [64, 48], BF16))
        AKT = e(nc.sbuf_tensor("AKT", [64, 1024], BF16))
        APR = e(nc.sbuf_tensor("APR", [128, 2048], BF16))
        WVU = e(nc.sbuf_tensor("WVU", [128, 8192], BF16))
        W2S = e(nc.sbuf_tensor("W2S", [128, 8], BF16))
        CSF = e(nc.sbuf_tensor("CSF", [1, 1152], F32))
        DCA = e(nc.sbuf_tensor("DCA", [128, 1024], F32))
        DCB = e(nc.sbuf_tensor("DCB", [128, 1024], BF16))
        DCA2, DCB2 = DCA, DCB

        p.dma("sync", CONST[:, 0:128], identf, writes=[B("IDF")])
        p.dma("sync", CONSTB[:, 0:128], identb, writes=[B("IDB")])
        p.dma("sync", CONSTB[:, 128:256], tri, writes=[B("TRI")])
        p.dma("sync", BO2[:, 0:32], bo2, writes=[B("BO2")])
        p.dma("sync", BO2[:, 32:40], bu12, writes=[B("BU1")])
        p.dma("sync", SCAL[0:1, 0:4], scal, writes=[B("SCAL")])
        p.dma("sync", LR[0:1, :], logrel, writes=[B("LRR")])
        p.dma("sync", PTI[0:1, 0:1], pari, writes=[B("PTI")])
        p.emit("vector", lambda v: v.memset(CONST[0:1, 128:256], 1.0),
               writes=[B("ONESF")])
        p.emit("vector", lambda v: v.memset(CONSTB[:, 256:257], 1.0),
               writes=[B("ONESB")])
        CONSTV = e(nc.sbuf_tensor("CONSTV", [128, 4], F32))
        for ci, cv in ((0, EPS), (1, 1.0), (2, LN_CSC_MARGIN)):
            p.emit("vector", lambda v, _i=ci, _v=cv: v.memset(
                CONSTV[:, _i:_i + 1], _v), writes=[B("constv")])
            nc.const_aps.register(F32, cv, CONSTV[:, ci:ci + 1])

        def mkregs(engine):
            rp = engine.alloc_register("rp")
            engine.reg_load(rp, PTI[0:1, 0:1])
            r1 = engine.alloc_register("rqr")
            engine.reg_mul(r1, rp, 2048 * TOK)
            r2 = engine.alloc_register("rkr")
            engine.reg_mul(r2, rp, 512 * TOK)
            r3 = engine.alloc_register("rvr")
            engine.reg_mul(r3, rp, 512)
            r4 = engine.alloc_register("ror")
            engine.reg_mul(r4, rp, TOK)
            rt = engine.alloc_register("rtt")
            p.regs.update(rq=r1, rk=r2, rv=r3, ro=r4, rt=rt)
            return engine.nop()

        p.emit("sync", mkregs, reads=[B("PTI")])

        with ExitStack() as s0:
            e0 = s0.enter_context
            P0S = e0(nc.psum_tensor("P0S", [128, 512], F32))
            CW = e0(nc.sbuf_tensor("CW", [128, 1024], F32))
            CW2 = e0(nc.sbuf_tensor("CW2", [128, 1024], F32))
            CWB = e0(nc.sbuf_tensor("CWB", [128, 1024], BF16))
            CWB2 = e0(nc.sbuf_tensor("CWB2", [128, 1024], BF16))
            WA0 = e0(nc.sbuf_tensor("WA0", [128, 4096], BF16))
            WD0 = e0(nc.sbuf_tensor("WD0", [128, 8192], BF16))
            EV0 = e0(nc.sbuf_tensor("EV0", [128, 256], BF16))

            def cast_weight(src, dst, rows, cols, colsum=None):
                nrc = (rows + 127) // 128
                for c0 in range(0, cols, 512):
                    cc = min(512, cols - c0)
                    for ri in range(nrc):
                        r0 = ri * 128
                        rr = min(128, rows - r0)
                        stage, stageb, sb = (
                            (CW, CWB, "cw0") if ri % 2 == 0 else (CW2, CWB2, "cw1"))
                        p.dma("sync", stage[0:rr, 0:cc],
                              src[r0:r0 + rr, c0:c0 + cc], writes=[B(sb)])
                        p.emit("vector",
                               lambda v, _s=stage, _b=stageb, _rr=rr, _cc=cc:
                               v.tensor_copy(_b[0:_rr, 0:_cc], _s[0:_rr, 0:_cc]),
                               reads=[B(sb)], writes=[B(sb + "b")])
                        p.dma("sync", dst[r0:r0 + rr, c0:c0 + cc],
                              stageb[0:rr, 0:cc], reads=[B(sb + "b")],
                              writes=[B(dst.tensor.name)])
                        if colsum is not None:
                            p.emit("tensor",
                                   lambda t, _b=stageb, _rr=rr, _cc=cc,
                                   _st=(ri == 0), _sp=(ri == nrc - 1):
                                   t.matmul(P0S[0:1, 0:_cc], ONESB[0:_rr, :],
                                            _b[0:_rr, 0:_cc], start=_st, stop=_sp),
                                   reads=[B(sb + "b"), B("ONESB")],
                                   writes=([B("P0S")] if ri == 0 else []))
                    if colsum is not None:
                        kindcs, off = colsum
                        p.emit("scalar",
                               lambda s, _cc=cc: s.copy(ROWA[0:1, 0:_cc],
                                                        P0S[0:1, 0:_cc]),
                               reads=[B("P0S")], writes=[B("csrow")],
                               waits=p.tick())
                        if kindcs == "sbuf":
                            p.emit("vector",
                                   lambda v, _c0=c0, _cc=cc, _o=off:
                                   v.tensor_copy(CSF[0:1, _o + _c0:_o + _c0 + _cc],
                                                 ROWA[0:1, 0:_cc]),
                                   reads=[B("csrow")], writes=[B("csf")])
                        else:
                            p.dma("sync", colsd[off:off + 1, c0:c0 + cc],
                                  ROWA[0:1, 0:cc], reads=[B("csrow")],
                                  writes=[B("colsd")])

            cast_weight(fwqt, fwqtb, 4096, 64, colsum=("sbuf", 0))
            cast_weight(fwrt, fwrtb, 64, 48)
            cast_weight(akt, aktb, 64, 1024)
            cast_weight(wvdt, wvdtb, 4096, 256)
            cast_weight(wvut, wvutb, 256, 4096)
            cast_weight(wu2t, wu2tb, 1024, 1)
            for srcw, dstw, cols in ((wu1t, wu1tb, 32768), (avt, avtb, 32768)):
                for c0 in range(0, cols, 1024):
                    st2, sb2, nm2 = ((CW, CWB, "cw0") if (c0 // 1024) % 2 == 0
                                     else (CW2, CWB2, "cw1"))
                    p.dma("sync", st2[:, :], srcw[:, c0:c0 + 1024],
                          writes=[B(nm2)])
                    p.emit("vector", lambda v, _s=st2, _b=sb2: v.tensor_copy(
                        _b[:, :], _s[:, :]), reads=[B(nm2)],
                        writes=[B(nm2 + "b")])
                    p.dma("sync", dstw[:, c0:c0 + 1024], sb2[:, :],
                          reads=[B(nm2 + "b")], writes=[B(dstw.tensor.name)])

            p.dma("sync",
                  bass.AP(WD0, 0,
                          [[8192, 128], [256, 32], [1, 256]]),
                  bass.AP(wvdtb.tensor, 0, [[256, 128], [128 * 256, 32], [1, 256]]),
                  reads=[B("wvdtb")], writes=[B("wd0")])
            for nch in range(8):
                p.dma("sync", WA0[:, :],
                      avtb[:, nch * 4096:(nch + 1) * 4096],
                      reads=[B("avtb")], writes=[B("wa0")])

                def mmA(t):
                    ins = None
                    for kc in range(32):
                        ins = t.matmul(P0S[:, 0:256],
                                       WA0[:, kc * 128:(kc + 1) * 128],
                                       WD0[:, kc * 256:(kc + 1) * 256],
                                       start=(kc == 0), stop=(kc == 31))
                    return ins

                p.emit("tensor", mmA, reads=[B("wa0"), B("wd0")], writes=[B("P0S")])
                p.emit("scalar", lambda s: s.copy(EV0[:, 0:256], P0S[:, 0:256]),
                       reads=[B("P0S")], writes=[B("ev0")], waits=p.tick())
                p.dma("sync", apd[nch * 128:(nch + 1) * 128, 0:256], EV0[:, 0:256],
                      reads=[B("ev0")], writes=[B("apd")])

            p.dma("sync", FWR[:, :], fwrtb, reads=[B("fwrtb")], writes=[B("FWR")])
            p.dma("sync", AKT[:, :], aktb, reads=[B("aktb")], writes=[B("AKT")])
            for c2 in range(2):
                p.emit("tensor", lambda t, _c=c2: t.matmul(
                    P0S[0:48, :], FWR[:, :], AKT[:, _c * 512:(_c + 1) * 512],
                    start=True, stop=True),
                    reads=[B("FWR"), B("AKT")], writes=[B("P0S")])
                p.emit("vector", lambda v, _c=c2: v.tensor_copy(
                    RKBF[:, _c * 512:(_c + 1) * 512], P0S[0:48, :]),
                    reads=[B("P0S")], writes=[B("RKBF")], waits=p.tick())

            p.dma("sync",
                  bass.AP(FWQ, 0,
                          [[2048, 128], [64, 32], [1, 64]]),
                  bass.AP(fwqtb.tensor, 0, [[64, 128], [128 * 64, 32], [1, 64]]),
                  reads=[B("fwqtb")], writes=[B("FWQ")])
            p.dma("sync",
                  bass.AP(APR, 0,
                          [[2048, 128], [256, 8], [1, 256]]),
                  bass.AP(apd.tensor, 0, [[256, 128], [128 * 256, 8], [1, 256]]),
                  reads=[B("apd")], writes=[B("APR")])
            p.dma("sync",
                  bass.AP(WVU, 0,
                          [[8192, 128], [4096, 2], [1, 4096]]),
                  bass.AP(wvutb.tensor, 0,
                          [[4096, 128], [128 * 4096, 2], [1, 4096]]),
                  reads=[B("wvutb")], writes=[B("WVU")])
            p.dma("sync", W2S[:, 0:8],
                  bass.AP(wu2tb.tensor, 0, [[1, 128], [128, 8]]),
                  reads=[B("wu2tb")], writes=[B("W2S")], slow=True)

        p.fence()

        deferred_units = []
        for srcw, dstw, cols in ((awq, wqb, 131072), (awk, wkb, 32768),
                                 (awv, wvb, 32768), (awo, wob, 131072)):
            for c0 in range(0, cols, 1024):
                deferred_units.append((srcw, dstw, c0))

        def emit_cast_unit(i, unit):
            srcw, dstw, c0 = unit
            stage, stageb, sb = ((DCA, DCB, "dc0") if i % 2 == 0
                                 else (DCA2, DCB2, "dc1"))
            p.dma("scalar", stage[:, :], srcw[:, c0:c0 + 1024], writes=[B(sb)])
            p.emit("scalar", lambda s, _s=stage, _b=stageb: s.copy(
                _b[:, :], _s[:, :]), reads=[B(sb)], writes=[B(sb + "b")])
            p.dma("scalar", dstw[:, c0:c0 + 1024], stageb[:, :],
                  reads=[B(sb + "b")],
                  writes=[B("%s_c%d" % (dstw.tensor.name, c0 // 4096))])

        dc_state = [0]

        def cast_filler(n):
            i0 = dc_state[0]
            for j in range(i0, min(i0 + n, len(deferred_units))):
                emit_cast_unit(j, deferred_units[j])
            dc_state[0] = min(i0 + n, len(deferred_units))

        def cast_drain():
            cast_filler(len(deferred_units))

        LVACC = ROWD[0:1, 0:1]
        LVTMP = ROWD[0:1, 1:2]
        LVST = ROWD[0:1, 8:16]
        MIV = ROWD[0:1, 16:17]
        LVACC2 = ROWD[0:1, 24:25]
        LVTMP2 = ROWD[0:1, 25:26]
        LVST2 = ROWD[0:1, 32:40]
        BQ = ROWD[0:1, 40:41]
        BK = ROWD[0:1, 41:42]
        BB_ = ROWD[0:1, 42:43]

        def emit_ffn(xsrc, dst, rowbank, lvg, lvc, prepass, tag, filler=None):
            xsrc_b = B(xsrc.tensor.name)

            def fill(n):
                if filler is not None:
                    filler(n)

            with ExitStack() as fs:
                fe = fs.enter_context
                FA = fe(nc.psum_tensor(tag + "FA", [128, 512], F32))
                FB = fe(nc.psum_tensor(tag + "FB", [128, 512], F32))
                FC = fe(nc.psum_tensor(tag + "FC", [128, 1024], F32))
                FD = fe(nc.psum_tensor(tag + "FD", [128, 1024], F32))
                FE = fe(nc.psum_tensor(tag + "FE", [128, 512], F32))
                FT = fe(nc.psum_tensor(tag + "FT", [128, 1024], BF16))
                XT4 = fe(nc.sbuf_tensor(tag + "XT4", [128, 2048], F32))
                XT4B = fe(nc.sbuf_tensor(tag + "XT4B", [128, 2048], F32))
                XB = fe(nc.sbuf_tensor(tag + "XB", [128, 16384], BF16))
                SQ4 = fe(nc.sbuf_tensor(tag + "SQ4", [128, 2048], BF16))
                W1 = fe(nc.sbuf_tensor(tag + "W1", [128, 8192], BF16))
                PTS = fe(nc.sbuf_tensor(tag + "PTS", [128, 4096], BF16))
                PB = fe(nc.sbuf_tensor(tag + "PB", [128, 1024], BF16))
                QB = fe(nc.sbuf_tensor(tag + "QB", [64, 512], BF16))
                RQ = fe(nc.sbuf_tensor(tag + "RQ", [48, 512], BF16))
                RTS = fe(nc.sbuf_tensor(tag + "RTS", [128, 1024], F32))
                FSS = fe(nc.sbuf_tensor(tag + "FSS", [128, 1024], F32))
                SELS = fe(nc.sbuf_tensor(tag + "SELS", [128, 1024], F32))
                MSK8 = fe(nc.sbuf_tensor(tag + "MSK8", [128, 1024],
                                         mybir.dt.uint8))
                NEGB = fe(nc.sbuf_tensor(tag + "NEGB", [128, 2], F32))
                T8 = fe(nc.sbuf_tensor(tag + "T8", [128, 8], F32))
                NB2 = fe(nc.sbuf_tensor(tag + "NB2", [128, 2], F32))
                SRC_ = fe(nc.sbuf_tensor(tag + "SRC", [128, 2], F32))
                RB512 = fe(nc.sbuf_tensor(tag + "RB512", [128, 512], F32))
                GAB = fe(nc.sbuf_tensor(tag + "GAB", [128, 1536], F32))
                ASB = fe(nc.sbuf_tensor(tag + "ASB", [128, 1024], BF16))
                AXE = fe(nc.sbuf_tensor(tag + "AXE", [128, 512], BF16))
                UF = fe(nc.sbuf_tensor(tag + "UF", [128, 512], F32))
                UB = fe(nc.sbuf_tensor(tag + "UB", [128, 512], BF16))
                AX4 = fe(nc.sbuf_tensor(tag + "AX4", [128, 2048], BF16))
                CF1 = fe(nc.sbuf_tensor(tag + "CF1", [128, 2048], F32))
                CF2 = fe(nc.sbuf_tensor(tag + "CF2", [128, 2048], F32))

                p.emit("vector", lambda v: v.memset(NEGB[:, :], NEGBIG),
                       writes=[B(tag + "negb")])
                NEGBB = NEGB[:, 0:1].to_broadcast([128, 1024])

                def oct_ap(dram, tile, o):
                    return bass.AP(dram.tensor, (o * 4 * 128) * TOK + tile * 512,
                                   [[TOK, 128], [128 * TOK, 4], [1, 512]])

                def xbfd_tile_ap(tile):
                    return bass.AP(xbfd.tensor, tile * 512,
                                   [[TOK, 128], [128 * TOK, 32], [1, 512]])

                s_negmu = ROWA[0:1, 0:512]
                s_r = ROWA[0:1, 512:1024]
                s_t1 = ROWA[0:1, 1024:1536]
                s_t2 = ROWA[0:1, 1536:2048]
                s_v = ROWA[0:1, 2048:2560]
                s_sr = ROWA[0:1, 2560:3072]
                s_le = ROWA[0:1, 3072:3584]
                s_g = ROWA[0:1, 3584:4096]
                s_a = ROWB[0:1, 0:512]
                s_b2 = ROWB[0:1, 512:1024]
                s_lv = ROWB[0:1, 1024:1536]
                s_g2 = ROWB[0:1, 1536:2048]

                def ln_rows(tile, store_bank, lvacc, lvtmp):
                    tk = p.tick()
                    p.emit("vector", lambda v: v.tensor_scalar(
                        s_negmu, FA[0:1, :], -1.0 / 4096.0, None, ALU.mult),
                        reads=[B(tag + "FA")], writes=[B(tag + "negmu")], waits=tk)
                    p.emit("vector", lambda v: v.tensor_scalar(
                        s_t1, FB[0:1, :], 1.0 / 4096.0, None, ALU.mult),
                        reads=[B(tag + "FB")], writes=[B(tag + "t1")], waits=tk)
                    p.emit("vector", lambda v: v.tensor_mul(
                        s_t2, s_negmu, s_negmu),
                        reads=[B(tag + "negmu")], writes=[B(tag + "t2")])
                    p.emit("vector", lambda v: v.tensor_sub(s_v, s_t1, s_t2),
                           reads=[B(tag + "t1"), B(tag + "t2")],
                           writes=[B(tag + "v")])
                    p.emit("scalar", lambda s: s.activation(
                        s_t1, s_v, AF.Ln, bias=EPS),
                        reads=[B(tag + "v")], writes=[B(tag + "t1")])
                    p.emit("scalar", lambda s: s.activation(
                        s_r, s_t1, AF.Exp, scale=-0.5),
                        reads=[B(tag + "t1")], writes=[B(tag + "r")])
                    tsl = slice(tile * 512, (tile + 1) * 512)
                    p.dma("sync", rowsd[store_bank:store_bank + 1, tsl], s_negmu,
                          reads=[B(tag + "negmu")],
                          writes=[B("rw%d" % store_bank)])
                    p.dma("sync", rowsd[store_bank + 1:store_bank + 2, tsl], s_r,
                          reads=[B(tag + "r")],
                          writes=[B("rw%d" % (store_bank + 1))])
                    if lvacc is not None:
                        p.emit("vector", lambda v: v.tensor_mul(s_t2, s_r, s_r),
                               reads=[B(tag + "r")], writes=[B(tag + "t2")])
                        p.emit("vector", lambda v: v.tensor_mul(s_t1, s_v, s_t2),
                               reads=[B(tag + "v"), B(tag + "t2")],
                               writes=[B(tag + "t1")])
                        p.emit("scalar", lambda s: s.activation(
                            s_t2, s_t1, AF.Ln, bias=1.0),
                            reads=[B(tag + "t1")], writes=[B(tag + "t2")])
                        p.dma("sync", rowsd[store_bank + 7:store_bank + 8, tsl],
                              s_t2, reads=[B(tag + "t2")],
                              writes=[B("rw%d" % (store_bank + 7))])
                        p.emit("vector", lambda v: v.reduce_sum(
                            lvtmp, s_t2, axis=AX),
                            reads=[B(tag + "t2")], writes=[B(tag + "lvt")])
                        p.emit("vector", lambda v: v.tensor_add(
                            lvacc, lvacc, lvtmp),
                            reads=[B(tag + "lvt"), B(tag + "lvacc")],
                            writes=[B(tag + "lvacc")])

                def stats_chain2(first, src_sb, sq_sb, src_buf, sq_buf):
                    def mm(t, _f=first, _src=src_sb, _sq=sq_sb):
                        ins = None
                        for c8 in range(2):
                            st = (_f and c8 == 0)
                            sl = slice(c8 * 512, (c8 + 1) * 512)
                            ins = t.matmul(FA[0:1, :], ONESB[:, :], _src[:, sl],
                                           start=st, stop=False)
                            ins = t.matmul(FB[0:1, :], ONESB[:, :], _sq[:, sl],
                                           start=st, stop=False)
                        return ins
                    p.emit("tensor", mm,
                           reads=[src_buf, sq_buf, B("ONESB")],
                           writes=([B(tag + "FA"), B(tag + "FB")]
                                   if first else []))

                def stats_chain(first, src_sb, sq_sb, src_buf, nchunks=4):
                    def mm(t, _f=first, _src=src_sb, _sq=sq_sb, _n=nchunks):
                        ins = None
                        for c8 in range(_n):
                            st = (_f and c8 == 0)
                            sl = slice(c8 * 512, (c8 + 1) * 512)
                            ins = t.matmul(FA[0:1, :], ONESB[:, :], _src[:, sl],
                                           start=st, stop=False)
                            ins = t.matmul(FB[0:1, :], ONESB[:, :], _sq[:, sl],
                                           start=st, stop=False)
                        return ins
                    p.emit("tensor", mm,
                           reads=[src_buf, B(tag + "sq"), B("ONESB")],
                           writes=([B(tag + "FA"), B(tag + "FB")]
                                   if first else []))

                if prepass:
                    p.emit("vector", lambda v: v.memset(LVACC, 0.0),
                           writes=[B(tag + "lvacc")])
                    for tile in range(NT):
                        for o in range(8):
                            xt = XT4 if o % 2 == 0 else XT4B
                            xtb = B(tag + "xt4%d" % (o % 2))
                            xbsl = XB[:, o * 2048:(o + 1) * 2048]
                            p.dma("sync", xt[:, :], oct_ap(xsrc, tile, o),
                                  reads=[xsrc_b], writes=[xtb])
                            p.emit("vector", lambda v, _x=xbsl, _xt=xt:
                                   v.tensor_copy(_x, _xt[:, :]),
                                   reads=[xtb],
                                   writes=[B(tag + "xb")])
                            p.dma("sync", oct_ap(xbfd, tile, o), xbsl,
                                  reads=[B(tag + "xb")], writes=[B("xbfd")])
                            p.emit("gpsimd", lambda g, _x=xbsl: g.tensor_mul(
                                SQ4[:, :], _x, _x),
                                reads=[B(tag + "xb")], writes=[B(tag + "sq")])
                            stats_chain(o == 0, xbsl, SQ4, B(tag + "xb"))
                        fill(48)
                        ln_rows(tile, rowbank, LVACC, LVTMP)
                    p.emit("vector", lambda v: v.memset(LVST, 0.0),
                           writes=[B(tag + "lvst")])
                    p.emit("vector", lambda v: v.tensor_copy(
                        ROWD[0:1, 8:9], LVACC),
                        reads=[B(tag + "lvacc")], writes=[B(tag + "lvst")])
                    p.dma("sync", lvc, LVST, reads=[B(tag + "lvst")],
                          writes=[B("lvc" + tag)])

                    def coll_lv(g):
                        return g.collective_compute(
                            "AllReduce", ALU.add, replica_groups=groups_all,
                            ins=[lvc.opt()], outs=[lvg.opt()])
                    p.emit("gpsimd", coll_lv, reads=[B("lvc" + tag)],
                           writes=[B("lvg" + tag)], semkey="gpsimdD")

                donext = dst is not y_out
                for tile in range(NT):
                    tsl = slice(tile * 512, (tile + 1) * 512)
                    p.dma("sync", bass.AP(XB, 0,
                                          [[16384, 128], [512, 32], [1, 512]]),
                          xbfd_tile_ap(tile), reads=[B("xbfd")],
                          writes=[B(tag + "xbt")])
                    p.dma("sync", s_negmu, rowsd[rowbank:rowbank + 1, tsl],
                          reads=[B("rw%d" % rowbank)], writes=[B(tag + "negmu")])
                    p.dma("sync", s_r, rowsd[rowbank + 1:rowbank + 2, tsl],
                          reads=[B("rw%d" % (rowbank + 1))],
                          writes=[B(tag + "r")])
                    p.emit("tensor", lambda t: t.matmul(
                        FE[:, :], ONESF[:, :], s_r, start=True, stop=True),
                        reads=[B(tag + "r"), B("ONESF")], writes=[B(tag + "FE")])
                    p.emit("scalar", lambda s: s.copy(RB512[:, :], FE[:, :]),
                           reads=[B(tag + "FE")], writes=[B(tag + "rb")],
                           waits=p.tick())

                    def mmq(t):
                        ins = None
                        for kc in range(32):
                            ins = t.matmul(FA[0:64, :],
                                           FWQ[:, kc * 64:(kc + 1) * 64],
                                           XB[:, kc * 512:(kc + 1) * 512],
                                           start=(kc == 0), stop=False)
                        ins = t.matmul(FA[0:64, :], CSF[0:1, 0:64], s_negmu,
                                       start=False, stop=True)
                        return ins
                    p.emit("tensor", mmq,
                           reads=[B(tag + "xbt"), B("FWQ"), B("csf"),
                                  B(tag + "negmu")],
                           writes=[B(tag + "FA")])
                    p.emit("vector", lambda v: v.tensor_mul(
                        QB[:, :], FA[0:64, :], RB512[0:64, :]),
                        reads=[B(tag + "FA"), B(tag + "rb")],
                        writes=[B(tag + "qb")], waits=p.tick())

                    p.emit("tensor", lambda t: t.matmul(
                        FB[0:48, :], FWR[:, :], QB[:, :], start=True, stop=True),
                        reads=[B(tag + "qb"), B("FWR")], writes=[B(tag + "FB")])
                    p.emit("scalar", lambda s: s.copy(RQ[:, :], FB[0:48, :]),
                           reads=[B(tag + "FB")], writes=[B(tag + "rq")],
                           waits=p.tick())
                    fill(48)

                    for sc in range(4):
                        csl = slice(sc * 128, (sc + 1) * 128)

                        def mmrout(t, _c=csl):
                            i = t.matmul(FC[:, 0:512], RQ[:, _c], RKBF[:, 0:512],
                                         start=True, stop=False)
                            i = t.matmul(FC[:, 0:512], ONESF[:, :],
                                         LR[0:1, 0:512], start=False, stop=True)
                            i = t.matmul(FC[:, 512:1024], RQ[:, _c],
                                         RKBF[:, 512:1024], start=True,
                                         stop=False)
                            i = t.matmul(FC[:, 512:1024], ONESF[:, :],
                                         LR[0:1, 512:1024], start=False,
                                         stop=True)
                            return i
                        p.emit("tensor", mmrout,
                               reads=[B(tag + "rq"), B("RKBF"), B("LRR"),
                                      B("ONESF")],
                               writes=[B(tag + "FC")])

                        def mmf(t, _c=csl):
                            i = t.matmul(FD[:, 0:512], QB[:, _c], AKT[:, 0:512],
                                         start=True, stop=False)
                            i = t.matmul(FD[:, 0:512], ONESF[:, :],
                                         LR[0:1, 0:512], start=False, stop=True)
                            i = t.matmul(FD[:, 512:1024], QB[:, _c],
                                         AKT[:, 512:1024], start=True,
                                         stop=False)
                            i = t.matmul(FD[:, 512:1024], ONESF[:, :],
                                         LR[0:1, 512:1024], start=False,
                                         stop=True)
                            return i
                        p.emit("tensor", mmf,
                               reads=[B(tag + "qb"), B("AKT"), B("LRR"),
                                      B("ONESF")],
                               writes=[B(tag + "FD")])

                        p.emit("scalar", lambda s: s.copy(RTS[:, :], FC[:, :]),
                               reads=[B(tag + "FC")], writes=[B(tag + "rts")],
                               waits=p.tick())
                        p.emit("vector", lambda v: v.tensor_copy(
                            FSS[:, :], FD[:, :]),
                            reads=[B(tag + "FD")], writes=[B(tag + "fss")],
                            waits=p.tick())
                        p.emit("vector", lambda v: v.max(out=T8[:, :],
                                                         in_=RTS[:, :]),
                               reads=[B(tag + "rts")], writes=[B(tag + "t8")])
                        p.emit("vector", lambda v: v.reduce_max(
                            NB2[:, 0:1], FSS[:, :], axis=AX, negate=True),
                            reads=[B(tag + "fss")], writes=[B(tag + "nb2")])
                        p.emit("vector", lambda v: v.tensor_scalar(
                            MSK8[:, :], RTS[:, :], T8[:, 7:8], None, ALU.is_ge),
                            reads=[B(tag + "t8"), B(tag + "rts")],
                            writes=[B(tag + "msk8")])
                        p.emit("vector", lambda v: v.select(
                            SELS[:, :], MSK8[:, :], FSS[:, :], NEGBB),
                            reads=[B(tag + "msk8"), B(tag + "fss"),
                                   B(tag + "negb")],
                            writes=[B(tag + "sels")])
                        p.emit("scalar", lambda s: s.activation(
                            PB[:, :], SELS[:, :], AF.Exp, bias=NB2[:, 0:1],
                            accum_out=SRC_[:, 0:1]),
                            reads=[B(tag + "sels"), B(tag + "nb2")],
                            writes=[B(tag + "pb"), B(tag + "srcol")])
                        p.emit("tensor", lambda t: t.transpose(
                            FC[0:1, 0:128], SRC_[:, 0:1], IDF),
                            reads=[B(tag + "srcol"), B("IDF")],
                            writes=[B(tag + "FC")])
                        p.emit("scalar", lambda s, _sc=sc: s.copy(
                            s_sr[0:1, _sc * 128:(_sc + 1) * 128],
                            FC[0:1, 0:128]),
                            reads=[B(tag + "FC")], writes=[B(tag + "srrow")],
                            waits=p.tick())

                        def mmtr(t):
                            ins = None
                            for j in range(8):
                                ins = t.transpose(FT[:, j * 128:(j + 1) * 128],
                                                  PB[:, j * 128:(j + 1) * 128],
                                                  IDB)
                            return ins
                        p.emit("tensor", mmtr, reads=[B(tag + "pb"), B("IDB")],
                               writes=[B(tag + "FT")])
                        p.emit("scalar", lambda s, _sc=sc: s.copy(
                            bass.AP(PTS, _sc * 128,
                                    [[4096, 128], [512, 8], [1, 128]]),
                            FT[:, :]),
                            reads=[B(tag + "FT")],
                            writes=[B(tag + "pts%d" % sc)],
                            waits=p.tick())

                    pts_reads = [B(tag + "pts%d" % s4) for s4 in range(4)]
                    for vc in range(2):
                        tgt, tgn = ((FA, tag + "FA") if vc == 0
                                    else (FB, tag + "FB"))

                        def mmas(t, _vc=vc, _tg=tgt):
                            ins = None
                            for n8 in range(8):
                                ins = t.matmul(
                                    _tg[:, :],
                                    APR[:, n8 * 256 + _vc * 128:
                                        n8 * 256 + _vc * 128 + 128],
                                    PTS[:, n8 * 512:(n8 + 1) * 512],
                                    start=(n8 == 0), stop=(n8 == 7))
                            return ins
                        p.emit("tensor", mmas, reads=pts_reads + [B("APR")],
                               writes=[B(tgn)])
                        p.emit("scalar", lambda s, _vc=vc, _tg=tgt: s.copy(
                            ASB[:, _vc * 512:(_vc + 1) * 512], _tg[:, :]),
                            reads=[B(tgn)], writes=[B(tag + "asb")],
                            waits=p.tick())

                    for cc in range(32):
                        tgt, tgn = ((FC, tag + "FC") if cc % 2 == 0
                                    else (FD, tag + "FD"))

                        def mmau(t, _cc=cc, _tg=tgt):
                            i = t.matmul(_tg[:, 0:512],
                                         WVU[:, _cc * 128:_cc * 128 + 128],
                                         ASB[:, 0:512], start=True, stop=False)
                            i = t.matmul(_tg[:, 0:512],
                                         WVU[:, 4096 + _cc * 128:
                                             4096 + _cc * 128 + 128],
                                         ASB[:, 512:1024], start=False,
                                         stop=True)
                            return i
                        p.emit("tensor", mmau, reads=[B(tag + "asb"), B("WVU")],
                               writes=[B(tgn)])
                        if cc % 2 == 0:
                            p.emit("scalar", lambda s, _tg=tgt: s.copy(
                                AXE[:, :], _tg[:, 0:512]),
                                reads=[B(tgn)], writes=[B(tag + "axe")],
                                waits=p.tick())
                        else:
                            p.emit("vector", lambda v, _tg=tgt: v.tensor_copy(
                                AXE[:, :], _tg[:, 0:512]),
                                reads=[B(tgn)], writes=[B(tag + "axe")],
                                waits=p.tick())
                        p.dma("sync",
                              auxd[cc * 128:(cc + 1) * 128, tsl],
                              AXE[:, :], reads=[B(tag + "axe")],
                              writes=[B("auxd")])
                    fill(48)

                    for uc in range(8):
                        ws = uc % 2
                        wb_ = B(tag + "w1%d" % ws)
                        p.dma("scalar", W1[:, ws * 4096:(ws + 1) * 4096],
                              wu1tb[:, uc * 4096:(uc + 1) * 4096],
                              reads=[B("wu1tb")], writes=[wb_])
                        if tile == 0 and prepass:
                            def mmw1cs(t, _ws=ws):
                                ins = None
                                for kc in range(32):
                                    ins = t.matmul(
                                        FE[0:1, 0:128], ONESB[:, :],
                                        W1[:, _ws * 4096 + kc * 128:
                                           _ws * 4096 + kc * 128 + 128],
                                        start=(kc == 0), stop=(kc == 31))
                                return ins
                            p.emit("tensor", mmw1cs, reads=[wb_, B("ONESB")],
                                   writes=[B(tag + "FE")])
                            p.emit("vector", lambda v, _uc=uc: v.tensor_copy(
                                CSF[0:1, 64 + _uc * 128:64 + (_uc + 1) * 128],
                                FE[0:1, 0:128]),
                                reads=[B(tag + "FE")], writes=[B("csf")],
                                waits=p.tick())
                        tgt, tgn = ((FA, tag + "FA") if uc % 2 == 0
                                    else (FB, tag + "FB"))

                        def mmu(t, _uc=uc, _tg=tgt, _ws=ws):
                            ins = None
                            for kc in range(32):
                                ins = t.matmul(
                                    _tg[:, :],
                                    W1[:, _ws * 4096 + kc * 128:
                                       _ws * 4096 + kc * 128 + 128],
                                    XB[:, kc * 512:(kc + 1) * 512],
                                    start=(kc == 0), stop=False)
                            ins = t.matmul(_tg[:, :],
                                           CSF[0:1, 64 + _uc * 128:
                                               64 + (_uc + 1) * 128],
                                           s_negmu, start=False, stop=True)
                            return ins
                        p.emit("tensor", mmu,
                               reads=[wb_, B(tag + "xbt"), B("csf"),
                                      B(tag + "negmu")],
                               writes=[B(tgn)])
                        p.emit("vector", lambda v, _tg=tgt: v.tensor_mul(
                            UF[:, :], _tg[:, :], RB512[:, :]),
                            reads=[B(tgn), B(tag + "rb")],
                            writes=[B(tag + "uf")], waits=p.tick())
                        p.emit("scalar", lambda s, _uc=uc: s.activation(
                            UB[:, :], UF[:, :], AF.Gelu_apprx_tanh,
                            bias=BO2[:, 32 + _uc:33 + _uc]),
                            reads=[B(tag + "uf"), B("BU1")],
                            writes=[B(tag + "ub")])
                        p.emit("tensor", lambda t, _uc=uc: t.matmul(
                            FE[0:1, :], W2S[:, _uc:_uc + 1], UB[:, :],
                            start=(_uc == 0), stop=(_uc == 7)),
                            reads=[B(tag + "ub"), B("W2S")],
                            writes=([B(tag + "FE")] if uc == 0 else []))
                    p.emit("vector", lambda v: v.tensor_scalar(
                        s_le, FE[0:1, :], SCAL[0:1, 0:1], None, ALU.add),
                        reads=[B(tag + "FE"), B("SCAL")],
                        writes=[B(tag + "le")], waits=p.tick())
                    fill(48)

                    if tile == 0:
                        p.dma("sync", LVST, lvg, reads=[B("lvg" + tag)],
                              writes=[B(tag + "lvst")])
                        p.emit("vector", lambda v: v.tensor_scalar(
                            MIV, ROWD[0:1, 8:9], 1.0 / NTOK_ALL, 1e-6,
                            ALU.mult, ALU.add),
                            reads=[B(tag + "lvst")], writes=[B(tag + "miv")])
                        p.emit("scalar", lambda s: s.activation(
                            MIV, MIV, AF.Ln),
                            reads=[B(tag + "miv")], writes=[B(tag + "miv")])
                        p.emit("scalar", lambda s: s.activation(
                            MIV, MIV, AF.Exp, scale=-1.0),
                            reads=[B(tag + "miv")], writes=[B(tag + "miv")])
                    p.emit("scalar", lambda s: s.activation(
                        s_g, s_le, AF.Tanh, scale=0.5),
                        reads=[B(tag + "le")], writes=[B(tag + "g")])
                    p.emit("vector", lambda v: v.tensor_scalar(
                        s_g, s_g, 0.5, 0.5, ALU.mult, ALU.add),
                        reads=[B(tag + "g")], writes=[B(tag + "g")])
                    p.dma("sync", s_lv, rowsd[rowbank + 7:rowbank + 8, tsl],
                          reads=[B("rw%d" % (rowbank + 7))],
                          writes=[B(tag + "lvr")])
                    p.emit("vector", lambda v: v.tensor_scalar(
                        s_lv, s_lv, MIV[0:1, 0:1], 0.5, ALU.mult, ALU.mult),
                        reads=[B(tag + "lvr"), B(tag + "miv")],
                        writes=[B(tag + "lvr")])
                    p.emit("vector", lambda v: v.scalar_tensor_tensor(
                        s_g, s_g, 2.5, s_lv, ALU.mult, ALU.add),
                        reads=[B(tag + "g"), B(tag + "lvr")],
                        writes=[B(tag + "g")])
                    p.emit("vector", lambda v: v.tensor_scalar(
                        s_g, s_g, 0.0, 5.0, ALU.max, ALU.min),
                        reads=[B(tag + "g")], writes=[B(tag + "g")])
                    p.emit("vector", lambda v: v.tensor_scalar(
                        s_g, s_g, -0.5, 1.0 / 1.5, ALU.add, ALU.mult),
                        reads=[B(tag + "g")], writes=[B(tag + "g")])
                    p.emit("vector", lambda v: v.tensor_scalar(
                        s_g, s_g, 0.0, 1.0, ALU.max, ALU.min),
                        reads=[B(tag + "g")], writes=[B(tag + "g")])
                    p.emit("vector", lambda v: v.tensor_scalar(
                        s_g, s_g, SCAL[0:1, 1:2], SCAL[0:1, 2:3],
                        ALU.mult, ALU.add),
                        reads=[B(tag + "g"), B("SCAL")], writes=[B(tag + "g")])
                    p.emit("scalar", lambda s: s.activation(
                        s_g, s_g, AF.Tanh, scale=0.5),
                        reads=[B(tag + "g")], writes=[B(tag + "g")])
                    p.emit("vector", lambda v: v.tensor_scalar(
                        s_g, s_g, 0.5, 0.5, ALU.mult, ALU.add),
                        reads=[B(tag + "g")], writes=[B(tag + "g")])
                    p.emit("vector", lambda v: v.tensor_scalar(
                        s_g2, s_g, 0.05, None, ALU.is_ge),
                        reads=[B(tag + "g")], writes=[B(tag + "g2")])
                    p.emit("vector", lambda v: v.tensor_mul(s_g, s_g, s_g2),
                           reads=[B(tag + "g2")], writes=[B(tag + "g")])
                    p.emit("scalar", lambda s: s.activation(s_g2, s_sr, AF.Ln),
                           reads=[B(tag + "srrow")], writes=[B(tag + "g2")])
                    p.emit("scalar", lambda s: s.activation(
                        s_g2, s_g2, AF.Exp, scale=-1.0),
                        reads=[B(tag + "g2")], writes=[B(tag + "g2")])
                    p.emit("vector", lambda v: v.scalar_tensor_tensor(
                        s_g, s_g, 0.5, s_g2, ALU.mult, ALU.mult),
                        reads=[B(tag + "g2"), B(tag + "g")],
                        writes=[B(tag + "g")])
                    p.emit("vector", lambda v: v.scalar_tensor_tensor(
                        s_b2, s_negmu, 0.5, s_r, ALU.mult, ALU.mult),
                        reads=[B(tag + "negmu"), B(tag + "r")],
                        writes=[B(tag + "b2")])
                    p.emit("vector", lambda v: v.tensor_scalar(
                        s_a, s_r, 0.5, 1.0, ALU.mult, ALU.add),
                        reads=[B(tag + "r")], writes=[B(tag + "a")])

                    for i, (nm, row) in enumerate((("gb", s_g), ("ab", s_a),
                                                   ("bb", s_b2))):
                        rb_ = B(tag + ("g" if i == 0 else ("a" if i == 1
                                                           else "b2")))
                        p.emit("tensor", lambda t, _r=row: t.matmul(
                            FE[:, :], ONESF[:, :], _r, start=True, stop=True),
                            reads=[rb_, B("ONESF")], writes=[B(tag + "FE")])
                        p.emit("scalar", lambda s, _i=i: s.copy(
                            GAB[:, _i * 512:(_i + 1) * 512], FE[:, :]),
                            reads=[B(tag + "FE")], writes=[B(tag + nm)],
                            waits=p.tick())

                    def b3(i, w):
                        return bass.AP(GAB, i * 512,
                                       [[1536, 128], [0, w // 512], [1, 512]])

                    def hex_ap(dram, tile, h):
                        return bass.AP(dram.tensor,
                                       (h * 2 * 128) * TOK + tile * 512,
                                       [[TOK, 128], [128 * TOK, 2], [1, 512]])

                    for h in range(16):
                        sl2 = h % 2
                        xt = (XT4 if sl2 == 0 else XT4B)[:, 0:1024]
                        ax = AX4[:, sl2 * 1024:(sl2 + 1) * 1024]
                        cf1 = CF1[:, sl2 * 1024:(sl2 + 1) * 1024]
                        cf2 = CF2[:, sl2 * 1024:(sl2 + 1) * 1024]
                        xtb = B(tag + "xt4%d" % sl2)
                        axb = B(tag + "ax4%d" % sl2)
                        c1b = B(tag + "cf1%d" % sl2)
                        c2b = B(tag + "cf2%d" % sl2)
                        p.dma("sync", xt, hex_ap(xsrc, tile, h),
                              reads=[xsrc_b], writes=[xtb])
                        p.dma("sync", ax, hex_ap(auxd, tile, h),
                              reads=[B("auxd")], writes=[axb])
                        p.emit("vector", lambda v, _a=ax, _c=cf1: v.tensor_tensor(
                            _c, _a, b3(0, 1024), ALU.mult),
                            reads=[axb, B(tag + "gb")], writes=[c1b])
                        p.emit("vector", lambda v, _x=xt, _c=cf2: v.tensor_tensor(
                            _c, _x, b3(1, 1024), ALU.mult),
                            reads=[xtb, B(tag + "ab")], writes=[c2b])
                        p.emit("vector", lambda v, _c1=cf1, _c2=cf2: v.tensor_add(
                            _c1, _c1, _c2),
                            reads=[c2b, c1b], writes=[c1b])
                        p.emit("vector", lambda v, _c1=cf1, _c2=cf2: v.tensor_tensor(
                            _c2, _c1, b3(2, 1024), ALU.add),
                            reads=[c1b, B(tag + "bb")], writes=[c2b])
                        p.dma("sync", hex_ap(dst, tile, h), cf2,
                              reads=[c2b], writes=[B(dst.tensor.name)])
                        if donext:
                            p.emit("gpsimd", lambda g, _a=ax, _c=cf2:
                                   g.tensor_copy(_a, _c),
                                   reads=[c2b], writes=[axb])
                            p.dma("sync", hex_ap(xbfd, tile, h), ax,
                                  reads=[axb], writes=[B("xbfd")])
                            p.emit("gpsimd", lambda g, _a=ax, _s=sl2:
                                   g.tensor_mul(
                                       SQ4[:, _s * 1024:(_s + 1) * 1024],
                                       _a, _a),
                                   reads=[axb], writes=[B(tag + "sq%d" % sl2)])
                            stats_chain2(h == 0, ax,
                                         SQ4[:, sl2 * 1024:(sl2 + 1) * 1024],
                                         axb, B(tag + "sq%d" % sl2))
                    if donext:
                        ln_rows(tile, 16, None, None)
                    fill(48)

        emit_ffn(x_in, x1d, rowbank=0, lvg=lv_g, lvc=lv_c, prepass=True,
                 tag="f1", filler=cast_filler)
        cast_drain()
        p.fence()

        with ExitStack() as s2:
            e2 = s2.enter_context
            QA = e2(nc.psum_tensor("QA", [128, 512], F32))
            QB_ = e2(nc.psum_tensor("QBp", [128, 512], F32))
            QE = e2(nc.psum_tensor("QE", [128, 512], F32))
            QS0 = e2(nc.psum_tensor("QS0", [128, 512], F32))
            QS1 = e2(nc.psum_tensor("QS1", [128, 512], F32))
            XBH = e2(nc.sbuf_tensor("XBH", [128, 32768], BF16))
            WQB = e2(nc.sbuf_tensor("WQBs", [128, 8192], BF16))
            WVH = e2(nc.sbuf_tensor("WVH", [128, 8192], BF16))
            QF = e2(nc.sbuf_tensor("QF", [128, 1024], F32))
            QSW = e2(nc.sbuf_tensor("QSW", [128, 1024], F32))
            QRO = e2(nc.sbuf_tensor("QRO", [128, 1024], BF16))
            QSQ = e2(nc.sbuf_tensor("QSQ", [128, 1024], BF16))
            TBC = e2(nc.sbuf_tensor("TBC", [128, 1024], F32))
            TBS = e2(nc.sbuf_tensor("TBS", [128, 1024], F32))
            R2B = e2(nc.sbuf_tensor("R2B", [128, 1024], F32))
            R2C = e2(nc.sbuf_tensor("R2C", [128, 8], F32))
            CSL = e2(nc.sbuf_tensor("CSL", [1, 128], F32))
            CSLV = e2(nc.sbuf_tensor("CSLV", [1, 256], F32))
            VBE = e2(nc.sbuf_tensor("VBE", [128, 256], BF16))
            QMX = e2(nc.sbuf_tensor("QMX", [128, 16], F32))
            STR = e2(nc.sbuf_tensor("STR", [1, 128], F32))
            ONEQK = e2(nc.sbuf_tensor("ONEQK", [128, 1280], BF16))
            ONEQ = ONEQK[:, 0:1024]
            ONEK = ONEQK[:, 1024:1280]
            p.dma("sync", ONEQK[:, 0:1024], oneq, writes=[B("ONEQ")])
            p.dma("sync", ONEQK[:, 1024:1280], onek, writes=[B("ONEK")])

            s2_negmu = ROWA[0:1, 0:1024]
            s2_r = ROWA[0:1, 1024:2048]

            p.emit("vector", lambda v: v.memset(QMX[:, 0:4], 0.0),
                   writes=[B("qmx")])

            for half in range(NH):
                hsl = slice(half * 1024, (half + 1) * 1024)
                p.dma("sync",
                      bass.AP(XBH, 0,
                              [[32768, 128], [1024, 32], [1, 1024]]),
                      bass.AP(xbfd.tensor, half * 1024,
                              [[TOK, 128], [128 * TOK, 32], [1, 1024]]),
                      reads=[B("xbfd")], writes=[B("xbh")])
                p.dma("sync", s2_negmu, rowsd[16:17, hsl],
                      reads=[B("rw16")], writes=[B("s2negmu")])
                p.dma("sync", s2_r, rowsd[17:18, hsl],
                      reads=[B("rw17")], writes=[B("s2r")])
                for c2 in range(2):
                    p.emit("tensor", lambda t, _c=c2: t.matmul(
                        QE[:, :], ONESF[:, :],
                        s2_r[0:1, _c * 512:(_c + 1) * 512],
                        start=True, stop=True),
                        reads=[B("s2r"), B("ONESF")], writes=[B("QE")])
                    p.emit("scalar", lambda s, _c=c2: s.copy(
                        R2B[:, _c * 512:(_c + 1) * 512], QE[:, :]),
                        reads=[B("QE")], writes=[B("r2b")], waits=p.tick())
                for tc in range(8):
                    p.emit("tensor", lambda t, _tc=tc: t.transpose(
                        QE[:, 0:1], s2_r[0:1, _tc * 128:(_tc + 1) * 128],
                        IDF[0:1, 0:1]),
                        reads=[B("s2r"), B("IDF")], writes=[B("QE")])
                    p.emit("scalar", lambda s, _tc=tc: s.copy(
                        R2C[:, _tc:_tc + 1], QE[:, 0:1]),
                        reads=[B("QE")], writes=[B("r2c")], waits=p.tick())
                p.dma("sync", TBC[:, :], tabc[:, hsl], writes=[B("tbc")])
                p.dma("sync", TBS[:, :], tabs[:, hsl], writes=[B("tbs")])
                p.emit("vector", lambda v: v.tensor_mul(
                    TBC[:, :], TBC[:, :], R2B[:, :]),
                    reads=[B("r2b"), B("tbc")], writes=[B("tbc")])
                p.emit("vector", lambda v: v.tensor_mul(
                    TBS[:, :], TBS[:, :], R2B[:, :]),
                    reads=[B("r2b"), B("tbs")], writes=[B("tbs")])

                for kind, nocs, wsrc, csrow, dst_c, onh, onb, qsb in (
                        ("q", 32, wqb, 0, qc_d, ONEQ, "ONEQ", QS0),
                        ("k", 8, wkb, 1, kc_d, ONEK, "ONEK", QS1)):
                    qsn = qsb.name
                    for oc in range(nocs):
                        ws = oc % 2
                        wb_ = B("wqb%d" % ws)
                        p.dma("scalar", WQB[:, ws * 4096:(ws + 1) * 4096],
                              wsrc[:, oc * 4096:(oc + 1) * 4096],
                              reads=[B("%s_c%d" % (wsrc.tensor.name, oc))],
                              writes=[wb_])

                        def mmcs(t, _ws=ws):
                            ins = None
                            for kc in range(32):
                                ins = t.matmul(
                                    QE[0:1, 0:128], ONESB[:, :],
                                    WQB[:, _ws * 4096 + kc * 128:
                                        _ws * 4096 + kc * 128 + 128],
                                    start=(kc == 0), stop=(kc == 31))
                            return ins
                        p.emit("tensor", mmcs, reads=[wb_, B("ONESB")],
                               writes=[B("QE")])
                        p.emit("scalar", lambda s: s.copy(CSL[0:1, :],
                                                          QE[0:1, 0:128]),
                               reads=[B("QE")], writes=[B("csl")],
                               waits=p.tick())
                        for tcol in range(2):
                            tgt, tgn = (QA, "QA") if tcol == 0 else (QB_, "QBp")

                            def mmp(t, _tg=tgt, _ws=ws, _tc=tcol):
                                ins = None
                                for kc in range(32):
                                    ins = t.matmul(
                                        _tg[:, :],
                                        WQB[:, _ws * 4096 + kc * 128:
                                            _ws * 4096 + kc * 128 + 128],
                                        XBH[:, kc * 1024 + _tc * 512:
                                            kc * 1024 + _tc * 512 + 512],
                                        start=(kc == 0), stop=False)
                                ins = t.matmul(
                                    _tg[:, :], CSL[0:1, :],
                                    s2_negmu[0:1, _tc * 512:(_tc + 1) * 512],
                                    start=False, stop=True)
                                return ins
                            p.emit("tensor", mmp,
                                   reads=[wb_, B("xbh"), B("csl"), B("s2negmu")],
                                   writes=[B(tgn)])
                            p.emit("scalar", lambda s, _tg=tgt, _tc=tcol: s.copy(
                                QF[:, _tc * 512:(_tc + 1) * 512], _tg[:, :]),
                                reads=[B(tgn)], writes=[B("qf")],
                                waits=p.tick())
                        p.emit("gpsimd", lambda g: g.tensor_copy(
                            QSW[0:64, :], QF[64:128, :]),
                            reads=[B("qf")], writes=[B("qsw")])
                        p.emit("gpsimd", lambda g: g.tensor_copy(
                            QSW[64:128, :], QF[0:64, :]),
                            reads=[B("qf")], writes=[B("qsw")])
                        p.emit("vector", lambda v: v.tensor_mul(
                            QF[:, :], QF[:, :], TBC[:, :]),
                            reads=[B("tbc"), B("qsw"), B("qf")],
                            writes=[B("qf")])
                        p.emit("vector", lambda v: v.tensor_mul(
                            QSW[:, :], QSW[:, :], TBS[:, :]),
                            reads=[B("tbs"), B("qsw")], writes=[B("qsw")])
                        p.emit("vector", lambda v: v.tensor_add(
                            QRO[:, :], QF[:, :], QSW[:, :]),
                            reads=[B("qf"), B("qsw")], writes=[B("qro")])
                        p.emit("gpsimd", lambda g: g.tensor_mul(
                            QSQ[:, :], QRO[:, :], QRO[:, :]),
                            reads=[B("qro")], writes=[B("qsq")])
                        for tcol in range(2):
                            p.emit("tensor", lambda t, _oc=oc, _tc=tcol,
                                   _on=onh, _qs=qsb: t.matmul(
                                       _qs[0:32, :] if _tc == 0 else _qs[32:64, :],
                                       _on[:, _oc * 32:(_oc + 1) * 32],
                                       QSQ[:, _tc * 512:(_tc + 1) * 512],
                                       start=(_oc == 0), stop=False,
                                       tile_position=((0, 0) if _tc == 0
                                                      else (0, 32))),
                                   reads=[B("qsq"), B(onb)],
                                   writes=([B(qsn)] if (oc == 0 and tcol == 0)
                                           else []))
                        p.dma("sync", dst_c[oc * 128:(oc + 1) * 128, hsl],
                              QRO[:, :], reads=[B("qro")],
                              writes=[B(dst_c.tensor.name)])

                    col0 = 0 if kind == "q" else 1
                    tk = p.tick()
                    p.emit("vector", lambda v, _qs=qsb, _c=col0: v.reduce_max(
                        QMX[0:32, 8 + _c:9 + _c], _qs[0:32, :], axis=AX),
                        reads=[B(qsn)], writes=[B("qmxr")], waits=tk)
                    p.emit("vector", lambda v, _qs=qsb, _c=col0: v.reduce_max(
                        QMX[32:64, 8 + _c:9 + _c], _qs[32:64, :], axis=AX),
                        reads=[B(qsn)], writes=[B("qmxr")], waits=tk)
                    p.emit("vector", lambda v, _c=col0: v.tensor_copy(
                        QMX[0:32, 10 + _c:11 + _c], QMX[32:64, 8 + _c:9 + _c]),
                        reads=[B("qmxr")], writes=[B("qmxr")])
                    p.emit("vector", lambda v, _c=col0: v.tensor_max(
                        QMX[0:32, 12 + _c:13 + _c], QMX[0:32, 8 + _c:9 + _c],
                        QMX[0:32, 10 + _c:11 + _c]),
                        reads=[B("qmxr")], writes=[B("qmxr")])
                    p.emit("vector", lambda v, _c=col0: v.tensor_max(
                        QMX[0:32, _c:_c + 1], QMX[0:32, _c:_c + 1],
                        QMX[0:32, 12 + _c:13 + _c]),
                        reads=[B("qmxr"), B("qmx")], writes=[B("qmx")])

                for dh in range(4):
                    p.dma("scalar", WVH[:, :],
                          wvb[:, dh * 8192:(dh + 1) * 8192],
                          reads=[B("wvb_c%d" % (dh * 2)),
                                 B("wvb_c%d" % (dh * 2 + 1))],
                          writes=[B("wvh")])

                    def mmcsv(t):
                        ins = None
                        for kc in range(32):
                            ins = t.matmul(QE[0:1, 0:256], ONESB[:, :],
                                           WVH[:, kc * 256:(kc + 1) * 256],
                                           start=(kc == 0), stop=(kc == 31))
                        return ins
                    p.emit("tensor", mmcsv, reads=[B("wvh"), B("ONESB")],
                           writes=[B("QE")])
                    p.emit("scalar", lambda s: s.copy(CSLV[0:1, :],
                                                      QE[0:1, 0:256]),
                           reads=[B("QE")], writes=[B("cslv")],
                           waits=p.tick())
                    for tc in range(8):
                        tgt, tgn = (QA, "QA") if tc % 2 == 0 else (QB_, "QBp")

                        def mmv(t, _tg=tgt, _tc=tc):
                            ins = None
                            for kc in range(32):
                                ins = t.matmul(
                                    _tg[:, 0:256],
                                    XBH[:, kc * 1024 + _tc * 128:
                                        kc * 1024 + _tc * 128 + 128],
                                    WVH[:, kc * 256:(kc + 1) * 256],
                                    start=(kc == 0), stop=False)
                            ins = t.matmul(
                                _tg[:, 0:256],
                                s2_negmu[0:1, _tc * 128:(_tc + 1) * 128],
                                CSLV[0:1, :], start=False, stop=True)
                            return ins
                        p.emit("tensor", mmv,
                               reads=[B("wvh"), B("xbh"), B("cslv"),
                                      B("s2negmu")],
                               writes=[B(tgn)])
                        p.emit("vector", lambda v, _tg=tgt, _tc=tc:
                               v.tensor_scalar(
                                   VBE[:, :], _tg[:, 0:256],
                                   R2C[:, _tc:_tc + 1], None, ALU.mult),
                               reads=[B(tgn), B("r2c")], writes=[B("vbe")],
                               waits=p.tick())
                        p.dma("sync",
                              vc_d[half * 1024 + tc * 128:
                                   half * 1024 + (tc + 1) * 128,
                                   dh * 256:(dh + 1) * 256],
                              VBE[:, :], reads=[B("vbe")], writes=[B("vc_d")])

            p.emit("tensor", lambda t: t.transpose(
                QE[0:1, 0:32], QMX[0:32, 0:1], IDF[0:32, 0:32]),
                reads=[B("qmx"), B("IDF")], writes=[B("QE")])
            p.emit("scalar", lambda s: s.copy(STR[0:1, 0:32], QE[0:1, 0:32]),
                   reads=[B("QE")], writes=[B("str")], waits=p.tick())
            p.emit("tensor", lambda t: t.transpose(
                QE[0:1, 0:32], QMX[0:32, 1:2], IDF[0:32, 0:32]),
                reads=[B("qmx"), B("IDF"), B("str")], writes=[B("QE")])
            p.emit("scalar", lambda s: s.copy(STR[0:1, 32:64], QE[0:1, 0:32]),
                   reads=[B("QE")], writes=[B("str")], waits=p.tick())
            p.dma("sync", st_c, STR[0:1, 0:64], reads=[B("str")],
                  writes=[B("st_c")])

            def coll_st(g):
                return g.collective_compute(
                    "AllReduce", ALU.max, replica_groups=groups_all,
                    ins=[st_c.opt()], outs=[st_g.opt()])
            p.emit("gpsimd", coll_st, reads=[B("st_c")], writes=[B("st_g")], semkey="gpsimdD")
            p.dma("sync", STR[0:1, 0:64], st_g, reads=[B("st_g")],
                  writes=[B("str")])
            p.emit("vector", lambda v: v.reduce_max(
                BQ, STR[0:1, 0:32], axis=AX),
                reads=[B("str")], writes=[B("bq")])
            p.emit("vector", lambda v: v.reduce_max(
                BK, STR[0:1, 32:64], axis=AX),
                reads=[B("str")], writes=[B("bk")])
            p.emit("vector", lambda v: v.tensor_mul(BB_, BQ, BK),
                   reads=[B("bq"), B("bk")], writes=[B("bbv")])
            p.emit("scalar", lambda s: s.activation(BB_, BB_, AF.Ln),
                   reads=[B("bbv")], writes=[B("bbv")])
            p.emit("scalar", lambda s: s.activation(
                BB_, BB_, AF.Exp, scale=0.5, bias=LN_CSC_MARGIN),
                reads=[B("bbv")], writes=[B("bbv")])
            p.emit("vector", lambda v: v.tensor_scalar(
                BB_, BB_, -1.0, None, ALU.mult),
                reads=[B("bbv")], writes=[B("bbv")])
            p.emit("tensor", lambda t: t.matmul(
                QE[:, 0:1], ONESF[:, :], BB_, start=True, stop=True),
                reads=[B("bbv"), B("ONESF")], writes=[B("QE")])
            p.emit("scalar", lambda s: s.copy(BIAS[:, 0:1], QE[:, 0:1]),
                   reads=[B("QE")], writes=[B("bias")], waits=p.tick())

            for nm, src, dst_ in (("q", qc_d, qg_d), ("k", kc_d, kg_d),
                                  ("v", vc_d, vg_d)):
                def coll(g, _s=src, _d=dst_):
                    return g.collective_compute(
                        "AllGather", ALU.bypass, replica_groups=groups_pair,
                        ins=[_s.opt()], outs=[_d.opt()])
                p.emit("gpsimd", coll, reads=[B(src.tensor.name)],
                       writes=[B(dst_.tensor.name)], semkey="gpsimdD")

        p.fence()

        with ExitStack() as s3:
            e3 = s3.enter_context
            ST0 = e3(nc.psum_tensor("ST0", [128, 1024], F32))
            ST1 = e3(nc.psum_tensor("ST1", [128, 1024], F32))
            OA = e3(nc.psum_tensor("OAp", [128, 512], F32))
            OB_ = e3(nc.psum_tensor("OBp", [128, 512], F32))
            SS = e3(nc.psum_tensor("SSp", [128, 512], F32))
            QH = e3(nc.sbuf_tensor("QH", [128, 2 * T], BF16))
            KH = e3(nc.sbuf_tensor("KH", [128, 2 * T], BF16))
            VH = e3(nc.sbuf_tensor("VH", [128, 2 * T], BF16))
            PTR = e3(nc.sbuf_tensor("PTR", [128, 4096], BF16))
            SACC = e3(nc.sbuf_tensor("SACC", [128, 1024], F32))
            SACB = e3(nc.sbuf_tensor("SACB", [128, 1024], BF16))
            RSB = e3(nc.sbuf_tensor("RSB", [128, 1024], F32))
            OEV = e3(nc.sbuf_tensor("OEV", [128, 1024], BF16))
            OSB = e3(nc.sbuf_tensor("OSB", [128, 1024], F32))
            SRW = e3(nc.sbuf_tensor("SRW", [1, 1024], F32))

            rs = p.regs

            def dyn_dma(out_ap, tensor, base_reg, const_off, dims, reads, writes):
                def fn(engine):
                    engine.reg_alu(rs["rt"], rs[base_reg], const_off, ALU.add)
                    return engine.dma_start(
                        out=out_ap, in_=bass.AP(tensor, rs["rt"], dims))
                return p.emit("sync", fn, reads=reads, writes=writes,
                              inc=16, semkey="syncD")

            for r in range(2):
                dyn_dma(qm_d[:, r * TOK:(r + 1) * TOK], qg_d.tensor, "rq",
                        r * 4096 * TOK, [[TOK, 2048], [1, TOK]],
                        reads=[B("qg_d")], writes=[B("qm_d")])
                dyn_dma(km_d[:, r * TOK:(r + 1) * TOK], kg_d.tensor, "rk",
                        r * 1024 * TOK, [[TOK, 512], [1, TOK]],
                        reads=[B("kg_d")], writes=[B("km_d")])
                dyn_dma(vm_d[r * TOK:(r + 1) * TOK, :], vg_d.tensor, "rv",
                        r * TOK * 1024, [[1024, TOK], [1, 512]],
                        reads=[B("vg_d")], writes=[B("vm_d")])

            for gl in range(4):
                gs = gl % 2
                p.dma("sync", KH[:, gs * T:(gs + 1) * T],
                      km_d[gl * 128:(gl + 1) * 128, :],
                      reads=[B("km_d")], writes=[B("kh%d" % gs)])
                p.dma("sync",
                      bass.AP(VH, gs * T, [[2 * T, 128], [128, NKC], [1, 128]]),
                      bass.AP(vm_d.tensor, gl * 128,
                              [[512, 128], [128 * 512, NKC], [1, 128]]),
                      reads=[B("vm_d")], writes=[B("vh%d" % gs)])
                for h4 in range(4):
                    hl = gl * 4 + h4
                    qs_ = hl % 2
                    p.dma("sync", QH[:, qs_ * T:(qs_ + 1) * T],
                          qm_d[hl * 128:(hl + 1) * 128, :],
                          reads=[B("qm_d")], writes=[B("qh%d" % qs_)])
                    for qsb in range(NQSB):
                        C = 8 * qsb + 8
                        for c in range(C):
                            SP, spn = (ST0, "ST0") if c % 2 == 0 else (ST1, "ST1")
                            slot = c % 4
                            diag_i = c - 8 * qsb
                            lo = 128 * diag_i if diag_i >= 0 else 0

                            def mmst_(t, _SP=SP, _c=c, _lo=lo, _g=gs, _q=qs_,
                                      _qsb=qsb):
                                ins = None
                                for hf in range(2):
                                    a = max(_lo, hf * 512)
                                    bnd = (hf + 1) * 512
                                    if a >= bnd:
                                        continue
                                    ins = t.matmul(
                                        _SP[:, a:bnd],
                                        KH[:, _g * T + _c * 128:
                                           _g * T + (_c + 1) * 128],
                                        QH[:, _q * T + _qsb * 1024 + a:
                                           _q * T + _qsb * 1024 + bnd],
                                        start=True, stop=True)
                                return ins
                            p.emit("tensor", mmst_,
                                   reads=[B("kh%d" % gs), B("qh%d" % qs_)],
                                   writes=[B(spn)])
                            p.emit("scalar", lambda s, _SP=SP, _lo=lo, _sl=slot:
                                   s.activation(
                                       PTR[:, _sl * 1024 + _lo:_sl * 1024 + 1024],
                                       _SP[:, _lo:1024], AF.Exp,
                                       bias=BIAS[:, 0:1], scale=CSC),
                                   reads=[B(spn), B("bias")],
                                   writes=[B("ptr%d" % slot)],
                                   waits=p.tick())
                            if diag_i >= 0:
                                p.emit("vector", lambda v, _sl=slot, _lo=lo:
                                       v.tensor_mul(
                                           PTR[:, _sl * 1024 + _lo:
                                               _sl * 1024 + _lo + 128],
                                           PTR[:, _sl * 1024 + _lo:
                                               _sl * 1024 + _lo + 128],
                                           TRIB),
                                       reads=[B("ptr%d" % slot), B("TRI")],
                                       writes=[B("ptr%d" % slot)])

                            if c == 0:
                                p.emit("vector", lambda v, _sl=slot:
                                       v.tensor_copy(
                                           SACC[:, :],
                                           PTR[:, _sl * 1024:
                                               _sl * 1024 + 1024]),
                                       reads=[B("ptr%d" % slot)],
                                       writes=[B("sacc")])
                            else:
                                p.emit("vector", lambda v, _sl=slot, _lo=lo:
                                       v.tensor_tensor(
                                           SACC[:, _lo:1024],
                                           SACC[:, _lo:1024],
                                           PTR[:, _sl * 1024 + _lo:
                                               _sl * 1024 + 1024],
                                           ALU.add),
                                       reads=[B("ptr%d" % slot), B("sacc")],
                                       writes=[B("sacc")])

                            def mmpv(t, _c=c, _lo=lo, _sl=slot, _g=gs,
                                     _st=(c == 0)):
                                ins = None
                                for hf in range(2):
                                    a = max(_lo, hf * 512)
                                    bnd = (hf + 1) * 512
                                    if a >= bnd:
                                        continue
                                    OP = OA if hf == 0 else OB_
                                    ins = t.matmul(
                                        OP[:, a - hf * 512:bnd - hf * 512],
                                        VH[:, _g * T + _c * 128:
                                           _g * T + (_c + 1) * 128],
                                        PTR[:, _sl * 1024 + a:_sl * 1024 + bnd],
                                        start=_st, stop=False)
                                return ins
                            p.emit("tensor", mmpv,
                                   reads=[B("ptr%d" % slot), B("vh%d" % gs)],
                                   writes=([B("OAp"), B("OBp")]
                                           if c == 0 else []))

                        tk = p.tick()
                        p.emit("scalar", lambda s: s.copy(
                            OSB[:, 0:512], OA[:, :]),
                            reads=[B("OAp")], writes=[B("osb")], waits=tk)
                        p.emit("scalar", lambda s: s.copy(
                            OSB[:, 512:1024], OB_[:, :]),
                            reads=[B("OBp")], writes=[B("osb")], waits=tk)
                        p.emit("vector", lambda v: v.tensor_copy(
                            SACB[:, :], SACC[:, :]),
                            reads=[B("sacc")], writes=[B("sacb")])

                        def mmss(t):
                            i = t.matmul(SS[0:1, :], ONESB[:, :],
                                         SACB[:, 0:512], start=True, stop=True)
                            i = t.matmul(SS[32:33, :], ONESB[:, :],
                                         SACB[:, 512:1024], start=True,
                                         stop=True, tile_position=(0, 32))
                            return i
                        p.emit("tensor", mmss, reads=[B("sacb"), B("ONESB")],
                               writes=[B("SSp")])
                        tk = p.tick()
                        p.emit("scalar", lambda s: s.copy(
                            SRW[0:1, 0:512], SS[0:1, :]),
                            reads=[B("SSp")], writes=[B("srw")], waits=tk)
                        p.emit("scalar", lambda s: s.copy(
                            SRW[0:1, 512:1024], SS[32:33, :]),
                            reads=[B("SSp")], writes=[B("srw")], waits=tk)
                        p.emit("scalar", lambda s: s.activation(
                            SRW[0:1, :], SRW[0:1, :], AF.Ln),
                            reads=[B("srw")], writes=[B("srw")])
                        p.emit("scalar", lambda s: s.activation(
                            SRW[0:1, :], SRW[0:1, :], AF.Exp, scale=-1.0),
                            reads=[B("srw")], writes=[B("srw")])
                        for c2 in range(2):
                            p.emit("tensor", lambda t, _c=c2: t.matmul(
                                SS[:, :] if False else SS[:, 0:512],
                                ONESF[:, :],
                                SRW[0:1, _c * 512:(_c + 1) * 512],
                                start=True, stop=True),
                                reads=[B("srw"), B("ONESF")], writes=[B("SSp")])
                            p.emit("scalar", lambda s, _c=c2: s.copy(
                                RSB[:, _c * 512:(_c + 1) * 512], SS[:, 0:512]),
                                reads=[B("SSp")], writes=[B("rsb")],
                                waits=p.tick())
                        p.emit("vector", lambda v: v.tensor_mul(
                            OEV[:, 0:512], OSB[:, 0:512], RSB[:, 0:512]),
                            reads=[B("osb"), B("rsb")], writes=[B("oev")])
                        p.emit("vector", lambda v: v.tensor_mul(
                            OEV[:, 512:1024], OSB[:, 512:1024],
                            RSB[:, 512:1024]),
                            reads=[B("osb"), B("rsb")], writes=[B("oev")])
                        p.dma("sync",
                              oc_d[hl * 128:(hl + 1) * 128,
                                   qsb * 1024:(qsb + 1) * 1024],
                              OEV[:, :], reads=[B("oev")], writes=[B("oc_d")])

            def coll_o(g):
                return g.collective_compute(
                    "AllGather", ALU.bypass, replica_groups=groups_pair,
                    ins=[oc_d.opt()], outs=[og_d.opt()])
            p.emit("gpsimd", coll_o, reads=[B("oc_d")], writes=[B("og_d")], semkey="gpsimdD")

        p.fence()

        with ExitStack() as s4:
            e4 = s4.enter_context
            WA_ = e4(nc.psum_tensor("WAp", [128, 512], F32))
            WB2 = e4(nc.psum_tensor("WBp", [128, 512], F32))
            SX = e4(nc.psum_tensor("SXp", [128, 512], F32))
            SX2 = e4(nc.psum_tensor("SX2p", [128, 512], F32))
            OGH = e4(nc.sbuf_tensor("OGH", [128, 32768], BF16))
            WOB = e4(nc.sbuf_tensor("WOBs", [128, 8192], BF16))
            X1C = e4(nc.sbuf_tensor("X1C", [128, 1024], F32))
            X2C = e4(nc.sbuf_tensor("X2C", [128, 1024], F32))
            X2B = e4(nc.sbuf_tensor("X2B", [128, 1024], BF16))
            SQ2 = e4(nc.sbuf_tensor("SQ2", [128, 1024], BF16))

            s4_negmu = ROWA[0:1, 0:512]
            s4_r = ROWA[0:1, 512:1024]
            s4_t1 = ROWA[0:1, 1024:1536]
            s4_t2 = ROWA[0:1, 1536:2048]
            s4_v = ROWA[0:1, 2048:2560]
            p.emit("vector", lambda v: v.memset(LVACC2, 0.0), writes=[B("lvacc2")])

            rs = p.regs
            for r4 in range(4):
                def omdma(engine, _r=r4):
                    engine.reg_alu(rs["rt"], rs["ro"], _r * 1024 * T, ALU.add)
                    return engine.dma_start(
                        out=om_d[_r * 1024:(_r + 1) * 1024, :],
                        in_=bass.AP(og_d.tensor, rs["rt"],
                                    [[T, 1024], [1, TOK]]))
                p.emit("sync", omdma, reads=[B("og_d")], writes=[B("om_d")],
                       inc=16, semkey="syncD")
            for half in range(NH):
                p.dma("sync",
                      bass.AP(OGH, 0, [[32768, 128], [1024, 32], [1, 1024]]),
                      bass.AP(om_d.tensor, half * 1024,
                              [[TOK, 128], [128 * TOK, 32], [1, 1024]]),
                      reads=[B("om_d")], writes=[B("ogh")])
                for cc in range(32):
                    ws = cc % 2
                    wb_ = B("wob%d" % ws)
                    p.dma("scalar", WOB[:, ws * 4096:(ws + 1) * 4096],
                          wob[:, cc * 4096:(cc + 1) * 4096],
                          reads=[B("wob_c%d" % cc)], writes=[wb_])
                    p.dma("scalar", X1C[:, :],
                          bass.AP(x1d.tensor, cc * 128 * TOK + half * 1024,
                                  [[TOK, 128], [1, 1024]]),
                          reads=[B("x1d")], writes=[B("x1c")])
                    for tcol in range(2):
                        tgt, tgn = (WA_, "WAp") if tcol == 0 else (WB2, "WBp")

                        def mmo(t, _tg=tgt, _ws=ws, _tc=tcol):
                            ins = None
                            for hd in range(32):
                                ins = t.matmul(
                                    _tg[:, :],
                                    WOB[:, _ws * 4096 + hd * 128:
                                        _ws * 4096 + hd * 128 + 128],
                                    OGH[:, hd * 1024 + _tc * 512:
                                        hd * 1024 + _tc * 512 + 512],
                                    start=(hd == 0), stop=(hd == 31))
                            return ins
                        p.emit("tensor", mmo, reads=[wb_, B("ogh")],
                               writes=[B(tgn)])
                        p.emit("vector", lambda v, _tg=tgt, _tc=tcol, _cc=cc:
                               v.scalar_tensor_tensor(
                                   X2C[:, _tc * 512:(_tc + 1) * 512], _tg[:, :],
                                   BO2[:, _cc:_cc + 1],
                                   X1C[:, _tc * 512:(_tc + 1) * 512],
                                   ALU.add, ALU.add),
                               reads=[B(tgn), B("x1c"), B("BO2")],
                               writes=[B("x2c")], waits=p.tick())
                    p.dma("sync",
                          bass.AP(x2d.tensor, cc * 128 * TOK + half * 1024,
                                  [[TOK, 128], [1, 1024]]),
                          X2C[:, :], reads=[B("x2c")], writes=[B("x2d")])
                    p.emit("gpsimd", lambda g: g.tensor_copy(X2B[:, :],
                                                             X2C[:, :]),
                           reads=[B("x2c")], writes=[B("x2b")])
                    p.dma("sync",
                          bass.AP(xbfd.tensor, cc * 128 * TOK + half * 1024,
                                  [[TOK, 128], [1, 1024]]),
                          X2B[:, :], reads=[B("x2b")], writes=[B("xbfd")])
                    p.emit("gpsimd", lambda g: g.tensor_mul(
                        SQ2[:, :], X2B[:, :], X2B[:, :]),
                        reads=[B("x2b")], writes=[B("sq2")])

                    def mmsx(t, _cc=cc):
                        ins = None
                        for tc in range(2):
                            st = (_cc == 0)
                            tp = (0, 0) if tc == 0 else (0, 32)
                            tsl = slice(tc * 512, (tc + 1) * 512)
                            ins = t.matmul(
                                SX[0:1, :] if tc == 0 else SX[32:33, :],
                                ONESB[:, :], X2B[:, tsl],
                                start=st, stop=False, tile_position=tp)
                            ins = t.matmul(
                                SX2[0:1, :] if tc == 0 else SX2[32:33, :],
                                ONESB[:, :], SQ2[:, tsl],
                                start=st, stop=False, tile_position=tp)
                        return ins
                    p.emit("tensor", mmsx,
                           reads=[B("x2b"), B("sq2"), B("ONESB")],
                           writes=([B("SXp"), B("SX2p")] if cc == 0 else []))

                for tc in range(2):
                    tile = half * 2 + tc
                    tsl = slice(tile * 512, (tile + 1) * 512)
                    srcS = SX[0:1, :] if tc == 0 else SX[32:33, :]
                    srcS2 = SX2[0:1, :] if tc == 0 else SX2[32:33, :]
                    tk = p.tick()
                    p.emit("vector", lambda v, _s=srcS: v.tensor_scalar(
                        s4_negmu, _s, -1.0 / 4096.0, None, ALU.mult),
                        reads=[B("SXp")], writes=[B("s4negmu")], waits=tk)
                    p.emit("vector", lambda v, _s=srcS2: v.tensor_scalar(
                        s4_t1, _s, 1.0 / 4096.0, None, ALU.mult),
                        reads=[B("SX2p")], writes=[B("s4t1")], waits=tk)
                    p.emit("vector", lambda v: v.tensor_mul(
                        s4_t2, s4_negmu, s4_negmu),
                        reads=[B("s4negmu")], writes=[B("s4t2")])
                    p.emit("vector", lambda v: v.tensor_sub(s4_v, s4_t1, s4_t2),
                           reads=[B("s4t1"), B("s4t2")], writes=[B("s4v")])
                    p.emit("scalar", lambda s: s.activation(
                        s4_t1, s4_v, AF.Ln, bias=EPS),
                        reads=[B("s4v")], writes=[B("s4t1")])
                    p.emit("scalar", lambda s: s.activation(
                        s4_r, s4_t1, AF.Exp, scale=-0.5),
                        reads=[B("s4t1")], writes=[B("s4r")])
                    p.dma("sync", rowsd[8:9, tsl], s4_negmu,
                          reads=[B("s4negmu")], writes=[B("rw8")])
                    p.dma("sync", rowsd[9:10, tsl], s4_r,
                          reads=[B("s4r")], writes=[B("rw9")])
                    p.emit("vector", lambda v: v.tensor_mul(s4_t2, s4_r, s4_r),
                           reads=[B("s4r")], writes=[B("s4t2")])
                    p.emit("vector", lambda v: v.tensor_mul(s4_t1, s4_v, s4_t2),
                           reads=[B("s4v"), B("s4t2")], writes=[B("s4t1")])
                    p.emit("scalar", lambda s: s.activation(
                        s4_t2, s4_t1, AF.Ln, bias=1.0),
                        reads=[B("s4t1")], writes=[B("s4t2")])
                    p.dma("sync", rowsd[15:16, tsl], s4_t2,
                          reads=[B("s4t2")], writes=[B("rw15")])
                    p.emit("vector", lambda v: v.reduce_sum(
                        LVTMP2, s4_t2, axis=AX),
                        reads=[B("s4t2")], writes=[B("lvt2")])
                    p.emit("vector", lambda v: v.tensor_add(
                        LVACC2, LVACC2, LVTMP2),
                        reads=[B("lvt2"), B("lvacc2")], writes=[B("lvacc2")])

            p.emit("vector", lambda v: v.memset(LVST2, 0.0), writes=[B("lvst2")])
            p.emit("vector", lambda v: v.tensor_copy(
                ROWD[0:1, 32:33], LVACC2),
                reads=[B("lvacc2")], writes=[B("lvst2")])
            p.dma("sync", lv_c2, LVST2, reads=[B("lvst2")], writes=[B("lv_c2")])

            def coll_lv2(g):
                return g.collective_compute(
                    "AllReduce", ALU.add, replica_groups=groups_all,
                    ins=[lv_c2.opt()], outs=[lv_g2.opt()])
            p.emit("gpsimd", coll_lv2, reads=[B("lv_c2")], writes=[B("lvgf2")], semkey="gpsimdD")

        p.fence()

        emit_ffn(x2d, y_out, rowbank=8, lvg=lv_g2, lvc=lv_c2, prepass=False,
                 tag="f2")

        for name in dbg:
            src = io[name]
            dst = io["dbg_" + name]
            p.dma("sync", dst, src, reads=[B(name)], writes=[B("dbg_" + name)])

        snap = [(k, p.cnt[k]) for k in SEMKEYS if p.cnt[k] > 0]
        p.emit("sync", lambda s: s.nop(), waits=snap, inc=1)

        with nc.Block() as block:
            p.replay(block)

    return nc



def _rope_tables(half, TOK):
    pos = np.arange(half * TOK, (half + 1) * TOK, dtype=np.float64)
    freqs = 1.0 / (10000.0 ** (np.arange(0, 128, 2, dtype=np.float64) / 128.0))
    ang = freqs[:, None] * pos[None, :]
    c = np.cos(ang).astype(np.float32)
    s = np.sin(ang).astype(np.float32)
    tabc = np.concatenate([c, c], axis=0)
    tabs = np.concatenate([-s, s], axis=0)
    return np.ascontiguousarray(tabc), np.ascontiguousarray(tabs)


def _pmajor(W_T, nblk_out, dblk):
    f, n = W_T.shape
    assert f == 4096 and n == nblk_out * dblk
    a = W_T.reshape(32, 128, nblk_out, dblk)
    return np.ascontiguousarray(a.transpose(1, 2, 0, 3).reshape(128, -1))


def _perm_deinterleave(ncols):
    perm = np.zeros(ncols, dtype=np.int64)
    for h in range(ncols // 128):
        b = h * 128
        perm[b:b + 64] = b + np.arange(64) * 2
        perm[b + 64:b + 128] = b + np.arange(64) * 2 + 1
    return perm


def prepare_inputs(inputs, npairs, T):
    TOK = T // 2
    bf = ml_dtypes.bfloat16
    x = np.asarray(inputs["L0_attn_x"], dtype=np.float32)

    shared = dict(
        awq=_pmajor(
            np.asarray(inputs["L0_attn_Wq"], np.float32)[:,
                                                         _perm_deinterleave(4096)],
            32, 128),
        awk=_pmajor(
            np.asarray(inputs["L0_attn_Wk"], np.float32)[:,
                                                         _perm_deinterleave(1024)],
            8, 128),
        awv=_pmajor(np.asarray(inputs["L0_attn_Wv"], np.float32), 4, 256),
        awo=_pmajor(np.asarray(inputs["L0_attn_Wo"], np.float32), 32, 128),
        fwqt=np.ascontiguousarray(np.asarray(inputs["L0_ffn_Wq"], np.float32).T),
        fwrt=np.ascontiguousarray(np.asarray(inputs["L0_ffn_Wr"], np.float32).T),
        akt=np.ascontiguousarray(
            np.asarray(inputs["L0_ffn_aux_keys"], np.float32).T),
        avt=_pmajor(np.asarray(inputs["L0_ffn_aux_values"], np.float32).T,
                    8, 128),
        wvdt=np.ascontiguousarray(np.asarray(inputs["L0_ffn_Wvd"], np.float32).T),
        wvut=np.ascontiguousarray(np.asarray(inputs["L0_ffn_Wvu"], np.float32).T),
        wu1t=_pmajor(np.asarray(inputs["L0_ffn_Wu1"], np.float32).T, 8, 128),
        wu2t=np.ascontiguousarray(np.asarray(inputs["L0_ffn_Wu2"], np.float32).T),
        bo2=np.ascontiguousarray(
            np.asarray(inputs["L0_attn_bo"], np.float32).reshape(32, 128).T),
        bu12=np.ascontiguousarray(
            np.asarray(inputs["L0_ffn_bu1"], np.float32).reshape(8, 128).T),
        scal=np.array([[float(np.asarray(inputs["L0_ffn_bu2"]).ravel()[0]),
                        float(np.asarray(inputs["L0_ffn_gate_w1"])),
                        float(np.asarray(inputs["L0_ffn_gate_bias"])), 0.0]],
                      np.float32),
        logrel=np.ascontiguousarray(
            np.asarray(inputs["L0_ffn_log_rel"], np.float32).reshape(1, 1024)),
        identf=np.eye(128, dtype=np.float32),
        identb=np.eye(128, dtype=np.float32).astype(bf),
        tri=np.ascontiguousarray(
            (np.arange(128)[:, None] <= np.arange(128)[None, :])
            .astype(np.float32)).astype(bf),
    )
    onq = np.zeros((128, 1024), np.float32)
    for h in range(32):
        onq[:, h * 32 + h] = 1.0
    onk = np.zeros((128, 256), np.float32)
    for g in range(8):
        onk[:, g * 32 + 4 * g:g * 32 + 4 * g + 4] = 1.0
    shared["oneq"] = onq.astype(bf)
    shared["onek"] = onk.astype(bf)

    in_maps = []
    for core in range(2 * npairs):
        b, half = core // 2, core % 2
        m = dict(shared)
        m["x"] = np.ascontiguousarray(
            x[b, half * TOK:(half + 1) * TOK, :].T.astype(np.float32))
        tc, ts = _rope_tables(half, TOK)
        m["tabc"] = tc
        m["tabs"] = ts
        m["pari"] = np.full((1, 1), half, np.int32)
        in_maps.append(m)
    return in_maps


def kernel(**inputs):
    npairs, T = 4, 4096
    TOK = T // 2
    in_maps = prepare_inputs(inputs, npairs, T)
    nc = build_program(npairs, T)
    res = run_bass_kernel_spmd(nc, in_maps, core_ids=list(range(2 * npairs)))
    out = np.empty((npairs, T, 4096), dtype=np.float32)
    for core in range(2 * npairs):
        b, half = core // 2, core % 2
        out[b, half * TOK:(half + 1) * TOK, :] = res.results[core]["y"].T
    return out
